# version 15
# baseline (speedup 1.0000x reference)
import numpy as np
import concourse.bass as bass
import concourse.mybir as mybir
from concourse.bass_utils import run_bass_kernel_spmd

F32 = mybir.dt.float32
BF16 = mybir.dt.bfloat16
AF = mybir.ActivationFunctionType
ALU = mybir.AluOpType

H, D, BS, HID, NE = 32, 64, 256, 2048, 7
T = 1024
EPS = 1e-5

_nc_cache = [None]


def _host_consts():
    import ml_dtypes
    base = 1.0 / 2.0 ** (8.0 / H)
    rate = base ** (np.arange(H, dtype=np.float64) + 1.0)
    factor = 1.0 - 0.0 / (H - 1 + 1e-05) + 1e-05
    slope = rate * factor
    r1 = np.arange(1, BS + 1, dtype=np.float64)
    qd = np.exp(-slope[:, None] * r1[None, :])
    kd = np.exp(-slope[:, None] * (BS - r1[None, :]))
    bd = np.exp(-slope * BS)
    qdF = np.zeros((128, 16 * BS), np.float32)
    for hp in range(16):
        for p_ in range(128):
            qdF[p_, hp * BS:(hp + 1) * BS] = qd[2 * hp + p_ // 64]
    kdT = np.zeros((128, 2 * HID), np.float32)
    for par in range(2):
        for m in range(HID):
            kdT[:, par * HID + m] = kd[m // 64, par * 128:(par + 1) * 128]
    Darg = np.zeros((128, 512), np.float32)
    for cj in range(2):
        for p_ in range(128):
            tj = cj * 128 + p_
            row = np.arange(BS) - tj
            Darg[p_, cj * BS:(cj + 1) * BS] = np.where(row >= 0, row, 1e9)
    bdv = np.zeros((128, 16), np.float32)
    for ghp in range(16):
        bdv[0:64, ghp] = bd[2 * ghp]
        bdv[64:128, ghp] = bd[2 * ghp + 1]
    return (qdF.astype(ml_dtypes.bfloat16), kdT.astype(ml_dtypes.bfloat16),
            Darg, bdv, [float(-s) for s in slope])


class Prog:
    ENGINES = ("tensor", "vector", "scalar", "gpsimd", "sync")
    DMA_ENG = {"gpsimd", "sync"}

    def __init__(self):
        self.ops = []
        self.buf = {}
        self._pend = set()

    def add(self, eng, fn, deps=(), self_deps=()):
        dd = tuple(sorted({d for d in deps if d is not None}))
        sd = tuple(sorted({d for d in self_deps if d is not None}))
        self.ops.append([eng, fn, dd, sd])
        i = len(self.ops) - 1
        for n in self._pend:
            self._b(n)["r"].append(i)
        self._pend.clear()
        return i

    def _b(self, n):
        return self.buf.setdefault(n, {"w": [], "r": []})

    def rdep(self, *names):
        out = []
        for n in names:
            out += self._b(n)["w"]
        self._pend.update(names)
        return tuple(out)

    def wdep(self, *names):
        out = []
        for n in names:
            b = self._b(n)
            out += b["w"] + b["r"]
        return tuple(out)

    def note_r(self, i, *names):
        for n in names:
            self._b(n)["r"].append(i)

    def note_w(self, i, *names):
        for n in names:
            self._b(n)["w"].append(i)

    def finalize(self, nc, sems):
        eng_ops = {e: [] for e in self.ENGINES}
        for i, (e, fn, dd, sd) in enumerate(self.ops):
            eng_ops[e].append(i)
        SYNC_SELF = {"vector", "scalar"}
        marked = [self.ops[i][0] in self.DMA_ENG for i in range(len(self.ops))]
        for i, (e, fn, dd, sd) in enumerate(self.ops):
            for d in dd:
                de = self.ops[d][0]
                if de != e or (e in SYNC_SELF and de not in self.DMA_ENG):
                    marked[d] = True
            for d in sd:
                marked[d] = True
        inc_count = [0] * len(self.ops)
        for e in self.ENGINES:
            c = 0
            for i in eng_ops[e]:
                if marked[i]:
                    c += 16 if e in self.DMA_ENG else 1
                inc_count[i] = c

        def emit(eng_obj, e):
            last_wait = {}
            for i in eng_ops[e]:
                need = {}
                for d in self.ops[i][2]:
                    de = self.ops[d][0]
                    if de == e and e not in ("vector", "scalar"):
                        continue
                    v = inc_count[d]
                    if v > need.get(de, -1):
                        need[de] = v
                for d in self.ops[i][3]:
                    de = self.ops[d][0]
                    v = inc_count[d]
                    if v > need.get(de, -1):
                        need[de] = v
                for de, v in sorted(need.items()):
                    if v > last_wait.get(de, 0):
                        eng_obj.wait_ge(sems[de], v)
                        last_wait[de] = v
                inst = self.ops[i][1](eng_obj)
                if marked[i]:
                    inst.then_inc(sems[e], 16 if e in self.DMA_ENG else 1)
            if e == "sync" and eng_ops[e]:
                total = inc_count[eng_ops[e][-1]]
                if total:
                    eng_obj.wait_ge(sems[e], total)

        with nc.Block() as blk:
            @blk.tensor
            def _(t):
                emit(t, "tensor")

            @blk.vector
            def _(v):
                emit(v, "vector")

            @blk.scalar
            def _(s):
                emit(s, "scalar")

            @blk.gpsimd
            def _(g):
                emit(g, "gpsimd")

            @blk.sync
            def _(s):
                emit(s, "sync")


def build_nc():
    if _nc_cache[0] is not None:
        return _nc_cache[0]
    import ml_dtypes
    qdF_h, kdT_h, Darg_h, bdv_h, neg_slope = _host_consts()
    nc = bass.Bass("TRN2", target_bir_lowering=False)

    def din(name, shape):
        return nc.dram_tensor("i_" + name, shape, F32, kind="ExternalInput").ap()

    xm_h = din("xm", [T, HID])
    xp_h = din("xp", [T, HID])
    wq_h = din("wq", [16, 128, 2048])
    wk_h = din("wk", [16, 128, 2048])
    wv_h = din("wv", [16, 128, 2048])
    wg_h = din("wg", [16, 128, 2048])
    wo_h = din("wo", [16, 128, 2048])
    wr_h = din("wr", [128, 112])
    wup_h = din("wup", [NE, 16, 128, 2048])
    wgt_h = din("wgt", [NE, 16, 128, 2048])
    wdn_h = din("wdn", [NE, 16, 128, 2048])
    wsu_h = din("wsu", [16, 128, 2048])
    wsg_h = din("wsg", [16, 128, 2048])
    wsd_h = din("wsd", [16, 128, 2048])
    bupP_h = din("bupP", [128, 112])
    bgtP_h = din("bgtP", [128, 112])
    bdnT_h = din("bdnT", [7, 2048])
    bsuP_h = din("bsuP", [128, 16])
    bsgP_h = din("bsgP", [128, 16])
    bsdP_h = din("bsdP", [128, 16])
    nwP_h = din("nwP", [128, 16])
    br_h = din("br", [7, 1])
    gb_h = din("gb", [1, 7])
    y_h = nc.dram_tensor("y", [T, HID], F32, kind="ExternalOutput").ap()

    ident_f_h = nc.inline_tensor(np.eye(128, dtype=np.float32), "identf_inl").ap()
    ident_b_h = nc.inline_tensor(np.eye(128, dtype=ml_dtypes.bfloat16), "identb_inl").ap()
    onesb_h = nc.inline_tensor(np.ones((128, 1), dtype=ml_dtypes.bfloat16), "onesb_inl").ap()
    _onesrf = np.zeros((33, 128), dtype=np.float32)
    _onesrf[0, :] = 1.0
    _onesrf[32, :] = 1.0
    onesrf_h = nc.inline_tensor(_onesrf, "onesrf_inl").ap()
    onesrb_h = nc.inline_tensor(np.ones((1, 128), dtype=ml_dtypes.bfloat16), "onesrb_inl").ap()
    qdF_hh = nc.inline_tensor(qdF_h, "qdF_inl").ap()
    kdT_hh = nc.inline_tensor(kdT_h, "kdT_inl").ap()
    Darg_hh = nc.inline_tensor(Darg_h, "Darg_inl").ap()
    bdv_hh = nc.inline_tensor(bdv_h, "bdv_inl").ap()


    p = Prog()
    open_ctx = []

    def sb(name, shp, dt):
        g = nc.sbuf_tensor(name, shp, dt)
        h = g.__enter__()
        open_ctx.append((name, g))
        return h

    psum_ctx = []

    def psum(name, shp, dt=F32):
        g = nc.psum_tensor(name, shp, dt)
        h = g.__enter__()
        psum_ctx.append(g)
        return h

    ident_f = sb("ident_f", [128, 128], F32)
    ident_b = sb("ident_b", [128, 128], BF16)
    onesb = sb("onesb", [128, 1], BF16)
    onesrf = sb("onesrf", [33, 128], F32)
    onesrb = sb("onesrb", [1, 128], BF16)
    nwP = sb("nwP", [128, 16], F32)
    bupP = sb("bupP", [128, 112], F32)
    bgtP = sb("bgtP", [128, 112], F32)
    bsuP = sb("bsuP", [128, 16], F32)
    bsgP = sb("bsgP", [128, 16], F32)
    bsdP = sb("bsdP", [128, 16], F32)
    brS = sb("brS", [7, 1], F32)
    gbS = sb("gbS", [1, 7], F32)
    gb_bc = sb("gb_bc", [128, 7], F32)
    wrS = sb("wrS", [128, 112], BF16)
    statc = sb("statc", [128, 32], F32)
    epsc = sb("epsc", [128, 1], F32)
    routb = sb("routb", [128, 48], F32)
    R1 = sb("R1", [128, 16384], BF16)
    R2 = sb("R2", [128, 16384], BF16)
    R3 = sb("R3", [128, 16384], BF16)
    NRING = 2
    ring = [sb(f"ring{i}", [128, 2048], BF16) for i in range(NRING)]

    PA = psum("PA", [128, 1024])
    PC = psum("PC", [128, 512])
    PD = psum("PD", [128, 1024])
    PE_ = psum("PE", [128, 1024])
    PT = psum("PT", [128, 1024], BF16)

    xstage = [sb(f"xstage{i}", [128, 2048], F32) for i in range(2)]

    bph = []

    def sb_b(name, shp, dt):
        g = nc.sbuf_tensor(name, shp, dt)
        h = g.__enter__()
        bph.append(g)
        return h

    Darg = sb_b("Darg", [128, 512], F32)
    qdFg = sb_b("qdFg", [128, 1024], BF16)
    kdTg = sb_b("kdTg", [128, 1024], BF16)
    bdv = sb_b("bdv", [128, 16], F32)
    S_sb = sb_b("S_sb", [128, 512], F32)
    S_bf = sb_b("S_bf", [128, 512], BF16)
    qqd = sb_b("qqd", [128, 2048], BF16)
    ddT = sb_b("ddT", [128, 2048], BF16)
    awT = sb_b("awT", [128, 2048], BF16)
    q_fm = sb_b("q_fm", [128, 4096], BF16)
    k_fm = sb_b("k_fm", [128, 4096], BF16)
    tmp_fm = sb_b("tmp_fm", [128, 4096], BF16)
    k_tm = sb_b("k_tm", [128, 4096], BF16)
    v_tm = sb_b("v_tm", [128, 4096], BF16)

    for dst, src, key in [(ident_f, ident_f_h, "identf"), (ident_b, ident_b_h, "identb"),
                          (onesb, onesb_h, "onesb"), (onesrf, onesrf_h, "onesrf"),
                          (onesrb, onesrb_h, "onesrb"), (nwP, nwP_h, "nwP"),
                          (bupP, bupP_h, "bupP"), (bgtP, bgtP_h, "bgtP"),
                          (bsuP, bsuP_h, "bsuP"), (bsgP, bsgP_h, "bsgP"),
                          (bsdP, bsdP_h, "bsdP"), (brS, br_h, "brS"),
                          (gbS, gb_h, "gbS"), (Darg, Darg_hh, "Darg"),
                          (bdv, bdv_hh, "bdv")]:
        i = p.add("sync", lambda s, d=dst, so=src: s.dma_start(out=d[:, :], in_=so))
        p.note_w(i, key)
    i = p.add("vector", lambda v: v.memset(epsc[:, 0:1], EPS))
    p.note_w(i, "epsc")
    i = p.add("gpsimd", lambda g: g.dma_start(out=wrS[:, :], in_=wr_h))
    p.note_w(i, "wrS")
    i = p.add("tensor", lambda t: t.matmul(PC[0:128, 0:7], onesrf[0:1, :], gbS[0:1, :],
                                           start=True, stop=True),
              deps=p.rdep("onesrf", "gbS") + p.wdep("PC"))
    p.note_w(i, "PC")
    i2 = p.add("vector", lambda v: v.tensor_copy(gb_bc[:, :], PC[:, 0:7]),
               deps=p.rdep("PC") + p.wdep("gb_bc"))
    p.note_r(i2, "PC")
    p.note_w(i2, "gb_bc")

    ring_i = [0]

    def ring_load(src_ap):
        s = ring_i[0] % NRING
        ring_i[0] += 1
        key = ("ring", s)
        i = p.add("gpsimd", lambda g, d=ring[s], so=src_ap: g.dma_start(out=d[:, :], in_=so),
                  deps=p.wdep(key))
        p.note_w(i, key)
        return ring[s], key

    rot = [0]
    PROJ = [("PD", PD), ("PE", PE_)]

    def next_psum():
        r = PROJ[rot[0] % 2]
        rot[0] += 1
        return r

    def mm_group(P_t, pkey, slab, skey, rhs, rkey, extra=(), kc_keys=None):
        first_deps = p.wdep(pkey) + p.rdep(skey, rkey) + tuple(extra)
        last = None
        for kc in range(16):
            dd = first_deps if kc == 0 else ()
            if kc_keys is not None:
                dd = dd + p.rdep(kc_keys(kc))
            for hh in range(2):
                st = (kc == 0)
                sp = (kc == 15)
                last = p.add("tensor", lambda t, P=P_t, sl=slab, kc=kc, hh=hh, st=st, sp=sp, rb=rhs:
                             t.matmul(P[:, hh * 512:(hh + 1) * 512],
                                      sl[:, kc * 128:(kc + 1) * 128],
                                      rb[:, kc * 1024 + hh * 512:kc * 1024 + (hh + 1) * 512],
                                      start=st, stop=sp),
                             deps=dd if hh == 0 else ())
        p.note_r(last, skey, rkey)
        p.note_w(last, pkey)
        if kc_keys is not None:
            for kc in range(16):
                p.note_r(last, kc_keys(kc))
        return last

    def transpose_batch(src, src_cols, src_key, src_deps, dst_ap_fn, dst_key, ident):
        tids = []
        for j, c0 in enumerate(src_cols):
            ti = p.add("tensor", lambda t, j=j, c0=c0: t.transpose(
                PT[:, j * 128:(j + 1) * 128], src[:, c0:c0 + 128], ident[:, :]),
                deps=(p.wdep("PT") + tuple(src_deps) + p.rdep(src_key)) if j == 0 else ())
            tids.append(ti)
        p.note_r(tids[-1], src_key)
        p.note_w(tids[-1], "PT")
        i2 = p.add("vector", lambda v: v.tensor_copy(
            dst_ap_fn(), PT[:, 0:len(src_cols) * 128].rearrange("p (a f) -> p a f", f=128)),
            deps=(tids[-1],) + p.wdep(dst_key))
        p.note_r(i2, "PT")
        p.note_w(i2, dst_key)
        return i2

    def stage_A(jobs):
        for tc in range(8):
          for k, (x_hbm, dst, dst_key) in enumerate(jobs):
              so = 16 * k
              lo = 2048 * k
              i = p.add("sync", lambda s, k=k, tc=tc, xh=x_hbm: s.dma_start(
                  out=xstage[k][:, :], in_=xh[tc * 128:(tc + 1) * 128, :]),
                  deps=p.wdep(("xs", k)))
              p.note_w(i, ("xs", k))
              i = p.add("vector", lambda v, k=k, so=so: v.reduce_sum(
                  statc[:, so:so + 1], xstage[k][:, :], axis=mybir.AxisListType.X),
                  deps=p.rdep(("xs", k)) + p.wdep(("st_sum", k)))
              p.note_r(i, ("xs", k))
              p.note_w(i, ("st_sum", k))
              i = p.add("scalar", lambda s, k=k, so=so: s.activation(
                  R3[:, 4096 + 2048 * k:6144 + 2048 * k], xstage[k][:, :], AF.Square,
                  accum_out=statc[:, so + 2:so + 3]),
                  deps=p.rdep(("xs", k)) + p.wdep(("st_sq", k), ("R3dump", k)))
              p.note_r(i, ("xs", k))
              p.note_w(i, ("st_sq", k))
              p.note_w(i, ("R3dump", k))
              i = p.add("scalar", lambda s, so=so: s.activation(
                  statc[:, so + 4:so + 5], statc[:, so:so + 1], AF.Identity, scale=1.0 / 2048.0),
                  deps=p.rdep(("st_sum", k)) + p.wdep(("st_m", k)))
              p.note_w(i, ("st_m", k))
              i = p.add("scalar", lambda s, so=so: s.activation(
                  statc[:, so + 6:so + 7], statc[:, so + 2:so + 3], AF.Identity, scale=1.0 / 2048.0),
                  deps=p.rdep(("st_sq", k)) + p.wdep(("st_e2", k)))
              p.note_w(i, ("st_e2", k))
              i = p.add("vector", lambda v, so=so: v.tensor_mul(
                  statc[:, so + 8:so + 9], statc[:, so + 4:so + 5], statc[:, so + 4:so + 5]),
                  deps=p.rdep(("st_m", k)) + p.wdep(("st_msq", k)))
              p.note_w(i, ("st_msq", k))
              i = p.add("vector", lambda v, so=so: v.tensor_sub(
                  statc[:, so + 10:so + 11], statc[:, so + 6:so + 7], statc[:, so + 8:so + 9]),
                  deps=p.rdep(("st_e2", k), ("st_msq", k)) + p.wdep(("st_var", k)))
              p.note_w(i, ("st_var", k))
              i = p.add("scalar", lambda s, so=so: s.activation(
                  statc[:, so + 12:so + 13], statc[:, so + 10:so + 11], AF.Sqrt,
                  bias=epsc[:, 0:1]),
                  deps=p.rdep(("st_var", k), "epsc") + p.wdep(("st_sd", k)))
              p.note_w(i, ("st_sd", k))
              i = p.add("vector", lambda v, so=so: v.reciprocal(
                  statc[:, so + 14:so + 15], statc[:, so + 12:so + 13]),
                  deps=p.rdep(("st_sd", k)) + p.wdep(("st_a", k)))
              p.note_w(i, ("st_a", k))
              i = p.add("vector", lambda v, so=so: v.scalar_tensor_tensor(
                  statc[:, so + 8:so + 9], statc[:, so + 4:so + 5], -1.0,
                  statc[:, so + 14:so + 15], ALU.mult, ALU.mult),
                  deps=p.rdep(("st_a", k), ("st_m", k)) + p.wdep(("st_msq", k)))
              p.note_w(i, ("st_b", k))
              i = p.add("scalar", lambda s, k=k, so=so, lo=lo: s.activation(
                  R3[:, lo:lo + 2048], xstage[k][:, :], AF.Identity,
                  bias=statc[:, so + 8:so + 9], scale=statc[:, so + 14:so + 15]),
                  deps=p.rdep(("xs", k), ("st_a", k), ("st_b", k)) + p.wdep(("lntm", k)))
              p.note_r(i, ("xs", k))
              p.note_w(i, ("lntm", k))
              for q4 in range(4):
                  transpose_batch(
                      R3, [lo + (q4 * 4 + j) * 128 for j in range(4)], ("lntm", k), (),
                      lambda tc=tc, q4=q4, dst=dst: dst.ap().rearrange(
                          "p (fc t) -> p fc t", t=1024)[:, q4 * 4:(q4 + 1) * 4,
                                                        tc * 128:(tc + 1) * 128],
                      dst_key, ident_b)

    stage_A([(xm_h, R1, "R1"), (xp_h, R2, "R2")])

    def project(slab, skey, rhs, rkey, dst, dst_key, dst_col, extra=()):
        pkey, P_t = next_psum()
        mm_group(P_t, pkey, slab, skey, rhs, rkey, extra=extra)
        ev = p.add("scalar", lambda s, P=P_t, d=dst, dc=dst_col: s.activation(
            d[:, dc:dc + 1024], P[:, :], AF.Silu),
            deps=p.rdep(pkey) + p.wdep(dst_key))
        p.note_r(ev, pkey)
        p.note_w(ev, dst_key)
        return ev

    for g in range(4):
        i = p.add("sync", lambda s, g=g: s.dma_start(
            out=qdFg[:, :], in_=qdF_hh[:, g * 1024:(g + 1) * 1024]),
            deps=p.wdep("qdFg"))
        p.note_w(i, "qdFg")
        i = p.add("sync", lambda s, g=g: s.dma_start(
            out=kdTg[:, :].rearrange("p (a m) -> p a m", m=512),
            in_=kdT_hh.rearrange("p (a m) -> p a m", m=2048)[:, :, g * 512:(g + 1) * 512]),
            deps=p.wdep("kdTg"))
        p.note_w(i, "kdTg")

        pend = []
        for mcl in range(4):
            mc = g * 4 + mcl
            slab, skey = ring_load(wk_h[mc, :, :])
            project(slab, skey, R1, "R1", k_fm, ("kf", g, mcl), mcl * 1024)
            evp = project(slab, skey, R2, "R2", tmp_fm, ("tm", mcl), mcl * 1024)
            for th in pend:
                th()
            pend = [lambda q4=q4, mcl=mcl, evp=evp: transpose_batch(
                tmp_fm, [mcl * 1024 + (q4 * 4 + j) * 128 for j in range(4)],
                ("tm", mcl), (evp,),
                lambda q4=q4, mcl=mcl: k_tm.ap().rearrange(
                    "p (tc j) -> p tc j", j=512)[:, q4 * 4:(q4 + 1) * 4,
                                                 mcl * 128:(mcl + 1) * 128],
                "ktm", ident_b) for q4 in range(2)]
        for mcl in range(4):
            mc = g * 4 + mcl
            slab, skey = ring_load(wv_h[mc, :, :])
            project(slab, skey, R1, "R1", q_fm, ("vmain", g, mcl), mcl * 1024)
            evp = project(slab, skey, R2, "R2", tmp_fm, ("tm", mcl), mcl * 1024)
            for th in pend:
                th()
            pend = [lambda q4=q4, mcl=mcl, evp=evp: transpose_batch(
                tmp_fm, [mcl * 1024 + (q4 * 4 + j) * 128 for j in range(4)],
                ("tm", mcl), (evp,),
                lambda q4=q4, mcl=mcl: v_tm.ap().rearrange(
                    "p (tc j) -> p tc j", j=512)[:, q4 * 4:(q4 + 1) * 4,
                                                 mcl * 128:(mcl + 1) * 128],
                "vtm", ident_b) for q4 in range(2)]
        for th in pend:
            th()
        for tc in range(8):
            i = p.add("vector", lambda v, tc=tc: v.tensor_tensor(
                k_tm[:, tc * 512:(tc + 1) * 512], k_tm[:, tc * 512:(tc + 1) * 512],
                kdTg[:, (tc % 2) * 512:((tc % 2) + 1) * 512], ALU.mult),
                deps=p.rdep("ktm", "kdTg") + p.wdep(("kkd", tc)))
            p.note_r(i, "kdTg")
            p.note_w(i, "ktm")
            p.note_w(i, ("kkd", tc))
        i = p.add("vector", lambda v: v.memset(S_sb[:, :], 0.0), deps=p.wdep("S"))
        p.note_w(i, "S")
        for c in range(4):
            ghp = g * 4 + c
            for pb in range(4):
                mm1 = p.add("tensor", lambda t, pb=pb, c=c: t.matmul(
                    PC[:, 256:384],
                    k_tm[:, (2 * pb) * 512 + c * 128:(2 * pb) * 512 + (c + 1) * 128],
                    v_tm[:, (2 * pb) * 512 + c * 128:(2 * pb) * 512 + (c + 1) * 128],
                    start=True, stop=False),
                    deps=p.rdep("ktm", "vtm") + p.wdep("PC"))
                mm2 = p.add("tensor", lambda t, pb=pb, c=c: t.matmul(
                    PC[:, 256:384],
                    k_tm[:, (2 * pb + 1) * 512 + c * 128:(2 * pb + 1) * 512 + (c + 1) * 128],
                    v_tm[:, (2 * pb + 1) * 512 + c * 128:(2 * pb + 1) * 512 + (c + 1) * 128],
                    start=False, stop=True))
                p.note_r(mm2, "ktm", "vtm")
                p.note_w(mm2, "PC")
                dec = p.add("vector", lambda v, c=c, ghp=ghp: v.tensor_scalar_mul(
                    S_sb[:, c * 128:(c + 1) * 128], S_sb[:, c * 128:(c + 1) * 128],
                    bdv[:, ghp:ghp + 1]),
                    deps=p.rdep("bdv") + p.wdep(("Sc", c)))
                p.note_w(dec, ("Sc", c))
                ad = p.add("vector", lambda v, c=c: v.tensor_tensor(
                    S_sb[:, c * 128:(c + 1) * 128], S_sb[:, c * 128:(c + 1) * 128],
                    PC[:, 256:384], ALU.add),
                    deps=(mm2,) + p.wdep(("Sc", c)))
                p.note_r(ad, "PC")
                p.note_w(ad, ("Sc", c))
            cst = p.add("scalar", lambda s, c=c: s.activation(
                S_bf[:, c * 128:(c + 1) * 128], S_sb[:, c * 128:(c + 1) * 128], AF.Identity),
                deps=p.rdep(("Sc", c)) + p.wdep(("Sbf", c)))
            p.note_r(cst, ("Sc", c))
            p.note_w(cst, ("Sbf", c))
        pref_done = tuple(p._b("PC")["r"])
        for mcl in range(4):
            for q4 in range(2):
                transpose_batch(
                    k_fm, [mcl * 1024 + (q4 * 4 + j) * 128 for j in range(4)],
                    ("kf", g, mcl), pref_done,
                    lambda q4=q4, mcl=mcl: k_tm.ap().rearrange(
                        "p (tc j) -> p tc j", j=512)[:, q4 * 4:(q4 + 1) * 4,
                                                     mcl * 128:(mcl + 1) * 128],
                    "ktm", ident_b)
                transpose_batch(
                    q_fm, [mcl * 1024 + (q4 * 4 + j) * 128 for j in range(4)],
                    ("vmain", g, mcl), pref_done,
                    lambda q4=q4, mcl=mcl: v_tm.ap().rearrange(
                        "p (tc j) -> p tc j", j=512)[:, q4 * 4:(q4 + 1) * 4,
                                                     mcl * 128:(mcl + 1) * 128],
                    "vtm", ident_b)
        for tc in range(8):
            i = p.add("vector", lambda v, tc=tc: v.tensor_tensor(
                k_tm[:, tc * 512:(tc + 1) * 512], k_tm[:, tc * 512:(tc + 1) * 512],
                kdTg[:, (tc % 2) * 512:((tc % 2) + 1) * 512], ALU.mult),
                deps=p.rdep("ktm", "kdTg") + p.wdep(("kkd", tc)))
            p.note_r(i, "kdTg")
            p.note_w(i, "ktm")
            p.note_w(i, ("kkd", tc))
        for mcl in range(4):
            mc = g * 4 + mcl
            slab, skey = ring_load(wq_h[mc, :, :])
            project(slab, skey, R1, "R1", q_fm, ("vmain", g, mcl), mcl * 1024)
        def emit_aw(c, blk):
            par = (c * 4 + blk) % 2
            AWP, awk = (PA, "PA") if par == 0 else (PD, "PD")
            awm = []
            for hs in range(2):
                for cj in range(2):
                    b0 = hs * 64
                    tp = (64, 0) if hs == 1 else None
                    i = p.add("tensor", lambda t, c=c, blk=blk, hs=hs, cj=cj, b0=b0,
                              tp=tp, AWP=AWP: t.matmul(
                        AWP[:, hs * 512 + cj * 256:hs * 512 + (cj + 1) * 256],
                        k_fm[b0:b0 + 64, c * 1024 + blk * 256 + cj * 128:
                             c * 1024 + blk * 256 + (cj + 1) * 128],
                        q_fm[b0:b0 + 64, c * 1024 + blk * 256:c * 1024 + (blk + 1) * 256],
                        start=True, stop=True, tile_position=tp),
                        deps=(p.rdep(("kf", g, c), ("vmain", g, c)) + p.wdep(awk))
                             if (hs == 0 and cj == 0) else ())
                    awm.append(i)
            p.note_r(awm[-1], ("kf", g, c), ("vmain", g, c))
            p.note_w(awm[-1], awk)
            return awm[-1]

        for c in range(4):
            ghp = g * 4 + c
            hA, hB = 2 * ghp, 2 * ghp + 1
            do = (c % 2) * 1024
            dA = p.add("scalar", lambda s, sl=neg_slope[hA], do=do: s.activation(
                ddT[:, do:do + 512], Darg[:, :], AF.Exp, scale=sl),
                deps=p.rdep("Darg") + p.wdep(("ddT", c % 2)))
            p.note_w(dA, ("ddT", c % 2))
            dB = p.add("scalar", lambda s, sl=neg_slope[hB], do=do: s.activation(
                ddT[:, do + 512:do + 1024], Darg[:, :], AF.Exp, scale=sl),
                deps=p.wdep(("ddT", c % 2)))
            p.note_w(dB, ("ddT", c % 2))
            qq = []
            for i4 in range(4):
                i = p.add("vector", lambda v, i4=i4, c=c, do=do: v.tensor_tensor(
                    qqd[:, do + i4 * 256:do + (i4 + 1) * 256],
                    q_fm[:, c * 1024 + i4 * 256:c * 1024 + (i4 + 1) * 256],
                    qdFg[:, c * 256:(c + 1) * 256], ALU.mult),
                    deps=p.rdep(("vmain", g, c), "qdFg") +
                         (p.wdep(("qqd", c % 2)) if i4 == 0 else ()))
                qq.append(i)
            p.note_r(qq[0], ("vmain", g, c))
            p.note_r(qq[-1], "qdFg")
            p.note_w(qq[-1], ("qqd", c % 2))
            aw_last = emit_aw(c, 0)
            for blk in range(4):
                par = (c * 4 + blk) % 2
                AWP, awk = (PA, "PA") if par == 0 else (PD, "PD")
                IOP, iok = (PC, "PC") if par == 0 else (PE_, "PE")
                sbase = 256 if par == 0 else 512
                ao = par * 1024
                ev = p.add("vector", lambda v, AWP=AWP, ao=ao, do=do: v.tensor_tensor(
                    awT[:, ao:ao + 1024], AWP[:, :], ddT[:, do:do + 1024], ALU.mult),
                    deps=(aw_last, dA, dB) + p.wdep(("awT", par)))
                p.note_r(ev, awk)
                p.note_r(ev, ("ddT", c % 2))
                p.note_w(ev, ("awT", par))
                if blk < 3:
                    aw_last = emit_aw(c, blk + 1)
                last_io = None
                for hs in range(2):
                    b0 = hs * 64
                    tpc = (0, 64) if hs == 1 else None
                    for cj in range(2):
                        i = p.add("tensor", lambda t, c=c, blk=blk, hs=hs, cj=cj, b0=b0,
                                  tpc=tpc, IOP=IOP, ao=ao: t.matmul(
                            IOP[b0:b0 + 64, 0:256],
                            v_tm[:, (2 * blk + cj) * 512 + c * 128 + b0:
                                 (2 * blk + cj) * 512 + c * 128 + b0 + 64],
                            awT[:, ao + hs * 512 + cj * 256:ao + hs * 512 + (cj + 1) * 256],
                            start=(cj == 0), stop=False, tile_position=tpc),
                            deps=(ev,) + ((p.rdep("vtm") + p.wdep(iok))
                                          if (hs == 0 and cj == 0) else ()))
                        last_io = i
                    tpi = (64, 64) if hs == 1 else None
                    i = p.add("tensor", lambda t, c=c, blk=blk, hs=hs, b0=b0, tpi=tpi,
                              IOP=IOP, do=do: t.matmul(
                        IOP[b0:b0 + 64, 0:256],
                        S_bf[b0:b0 + 64, c * 128 + b0:c * 128 + b0 + 64],
                        qqd[b0:b0 + 64, do + blk * 256:do + (blk + 1) * 256],
                        start=False, stop=True, tile_position=tpi),
                        deps=p.rdep(("Sbf", c), ("qqd", c % 2)))
                    last_io = i
                p.note_r(last_io, "vtm", ("awT", par), ("Sbf", c), ("qqd", c % 2))
                p.note_w(last_io, iok)
                ae = p.add("scalar", lambda s, ghp=ghp, blk=blk, IOP=IOP: s.activation(
                    R3[:, ghp * 1024 + blk * 256:ghp * 1024 + (blk + 1) * 256],
                    IOP[:, 0:256], AF.Identity),
                    deps=(last_io,) + p.wdep("R3a"))
                p.note_r(ae, iok)
                p.note_w(ae, "R3a")
                mm1 = p.add("tensor", lambda t, blk=blk, c=c, IOP=IOP, sbase=sbase: t.matmul(
                    IOP[:, sbase:sbase + 128],
                    k_tm[:, (2 * blk) * 512 + c * 128:(2 * blk) * 512 + (c + 1) * 128],
                    v_tm[:, (2 * blk) * 512 + c * 128:(2 * blk) * 512 + (c + 1) * 128],
                    start=True, stop=False),
                    deps=p.rdep("ktm", "vtm") + p.wdep(iok))
                mm2 = p.add("tensor", lambda t, blk=blk, c=c, IOP=IOP, sbase=sbase: t.matmul(
                    IOP[:, sbase:sbase + 128],
                    k_tm[:, (2 * blk + 1) * 512 + c * 128:(2 * blk + 1) * 512 + (c + 1) * 128],
                    v_tm[:, (2 * blk + 1) * 512 + c * 128:(2 * blk + 1) * 512 + (c + 1) * 128],
                    start=False, stop=True))
                p.note_r(mm2, "ktm", "vtm")
                p.note_w(mm2, iok)
                if blk < 3:
                    dec = p.add("vector", lambda v, c=c, ghp=ghp: v.tensor_scalar_mul(
                        S_sb[:, c * 128:(c + 1) * 128], S_sb[:, c * 128:(c + 1) * 128],
                        bdv[:, ghp:ghp + 1]), deps=p.rdep("bdv") + p.wdep(("Sc", c)))
                    p.note_w(dec, ("Sc", c))
                    ad = p.add("vector", lambda v, c=c, IOP=IOP, sbase=sbase: v.tensor_tensor(
                        S_sb[:, c * 128:(c + 1) * 128], S_sb[:, c * 128:(c + 1) * 128],
                        IOP[:, sbase:sbase + 128], ALU.add), deps=(mm2,) + p.wdep(("Sc", c)))
                    p.note_r(ad, iok)
                    p.note_w(ad, ("Sc", c))
                    cst = p.add("scalar", lambda s, c=c: s.activation(
                        S_bf[:, c * 128:(c + 1) * 128], S_sb[:, c * 128:(c + 1) * 128],
                        AF.Identity), deps=p.rdep(("Sc", c)) + p.wdep(("Sbf", c)))
                    p.note_r(cst, ("Sc", c))
                    p.note_w(cst, ("Sbf", c))
                else:
                    ad = p.add("vector", lambda v, IOP=IOP, sbase=sbase: v.tensor_copy(
                        statc[:, 0:1], IOP[:, sbase:sbase + 1]), deps=(mm2,))
                    p.note_r(ad, iok)

    for g_ in reversed(bph):
        g_.__exit__(None, None, None)

    cph = []

    def sb_c(name, shp, dt):
        g = nc.sbuf_tensor(name, shp, dt)
        h = g.__enter__()
        cph.append(g)
        return h

    acc = sb_c("acc", [128, 16384], F32)
    rowb = sb_c("rowb", [65, 2048], F32)
    probsTb = sb_c("probsTb", [7, 1024], BF16)
    bc1 = sb_c("bc1", [128, 1024], BF16)
    bcB = sb_c("bcB", [128, 1024], BF16)
    tmpxb = sb_c("tmpxb", [128, 1024], BF16)
    bdnT = sb_c("bdnT", [7, 2048], BF16)
    i = p.add("gpsimd", lambda g: g.dma_start(out=bdnT[:, :], in_=bdnT_h))
    p.note_w(i, "bdnT")

    for mc in range(16):
        sq = p.add("scalar", lambda s, mc=mc: s.activation(
            tmpxb[:, :], R3[:, mc * 1024:(mc + 1) * 1024], AF.Square),
            deps=p.rdep("R3a") + p.wdep("tmpxb"))
        p.note_r(sq, "R3a")
        p.note_w(sq, "tmpxb")
        p.note_w(sq, ("sqd", mc))
        for hh in range(2):
            i = p.add("tensor", lambda t, hh=hh, mc=mc: t.matmul(
                PA[0:1, hh * 512:(hh + 1) * 512], onesb[:, 0:1],
                tmpxb[:, hh * 512:(hh + 1) * 512],
                start=(mc == 0), stop=(mc == 15)),
                deps=(sq,) + ((p.rdep("onesb") + p.wdep("PA")) if mc == 0 and hh == 0 else ()))
        p.note_r(i, "tmpxb")
    p.note_w(i, "PA")
    i = p.add("scalar", lambda s: s.activation(rowb[64:65, 0:1024], PA[0:1, :], AF.Sqrt,
                                               scale=1.0 / 2048.0, bias=epsc[0:1, 0:1]),
              deps=p.rdep("PA", "epsc") + p.wdep("row1"))
    p.note_r(i, "PA")
    p.note_w(i, "row1")
    i = p.add("vector", lambda v: v.reciprocal(rowb[0:1, 0:1024], rowb[64:65, 0:1024]),
              deps=p.rdep("row1") + p.wdep("row0"))
    p.note_w(i, "row0")
    for hh in range(2):
        i = p.add("tensor", lambda t, hh=hh: t.matmul(
            PA[:, hh * 512:(hh + 1) * 512], onesrf[0:1, :], rowb[0:1, hh * 512:(hh + 1) * 512],
            start=True, stop=True),
            deps=p.rdep("row0", "onesrf") + (p.wdep("PA") if hh == 0 else ()))
    p.note_w(i, "PA")
    i2 = p.add("vector", lambda v: v.tensor_copy(bc1[:, :], PA[:, :]),
               deps=(i,) + p.wdep("bc1"))
    p.note_r(i2, "PA")
    p.note_w(i2, "bc1")

    for mc in range(16):
        slab, skey = ring_load(wg_h[mc, :, :])
        pkey, P_t = next_psum()
        mm_group(P_t, pkey, slab, skey, R1, "R1")
        sg = p.add("scalar", lambda s, P=P_t: s.activation(bcB[:, :], P[:, :], AF.Sigmoid),
                   deps=p.rdep(pkey) + p.wdep("bcB"))
        p.note_r(sg, pkey)
        p.note_w(sg, "bcB")
        v3 = p.add("vector", lambda v, mc=mc: v.tensor_tensor(
            R3[:, mc * 1024:(mc + 1) * 1024], R3[:, mc * 1024:(mc + 1) * 1024],
            bcB[:, :], ALU.mult),
            deps=(sg,) + p.rdep(("sqd", mc)) + p.wdep("R3a") + p.rdep("R3a"))
        p.note_r(v3, "bcB")
        p.note_w(v3, "R3a")
        v1 = p.add("vector", lambda v, mc=mc: v.tensor_scalar_mul(
            R3[:, mc * 1024:(mc + 1) * 1024], R3[:, mc * 1024:(mc + 1) * 1024],
            nwP[:, mc:mc + 1]),
            deps=(v3,) + p.rdep("nwP"))
        v2 = p.add("vector", lambda v, mc=mc: v.tensor_tensor(
            R3[:, mc * 1024:(mc + 1) * 1024], R3[:, mc * 1024:(mc + 1) * 1024],
            bc1[:, :], ALU.mult), deps=(v1,) + p.rdep("bc1"))
        p.note_w(v2, ("ga", mc))
        p.note_w(v2, "R3a")

    for mc in range(16):
        slab, skey = ring_load(wo_h[mc, :, :])
        pkey, P_t = next_psum()
        mm_group(P_t, pkey, slab, skey, R3, "R3a", kc_keys=lambda kc: ("ga", kc))
        ev = p.add("vector", lambda v, P=P_t, mc=mc: v.tensor_copy(
            acc[:, mc * 1024:(mc + 1) * 1024], P[:, :]),
            deps=p.rdep(pkey) + p.wdep(("acc", mc)))
        p.note_r(ev, pkey)
        p.note_w(ev, ("acc", mc))

    for tc in range(8):
        k = tc % 2
        ld = p.add("sync", lambda s, k=k, tc=tc: s.dma_start(
            out=xstage[k][:, :], in_=xm_h[tc * 128:(tc + 1) * 128, :]),
            deps=p.wdep(("xs", k)))
        p.note_w(ld, ("xs", k))
        for q4 in range(4):
            tids = []
            for j in range(4):
                fc = q4 * 4 + j
                ti = p.add("tensor", lambda t, k=k, fc=fc, j=j: t.transpose(
                    PC[:, j * 128:(j + 1) * 128], xstage[k][:, fc * 128:(fc + 1) * 128],
                    ident_f[:, :]),
                    deps=((ld,) + p.rdep("identf") + p.wdep("PC")) if j == 0 else ())
                tids.append(ti)
            p.note_r(tids[-1], ("xs", k))
            p.note_w(tids[-1], "PC")
            av = p.add("vector", lambda v, q4=q4, tc=tc: v.tensor_tensor(
                acc.ap().rearrange("p (fc t) -> p fc t", t=1024)[:, q4 * 4:(q4 + 1) * 4,
                                                                 tc * 128:(tc + 1) * 128],
                acc.ap().rearrange("p (fc t) -> p fc t", t=1024)[:, q4 * 4:(q4 + 1) * 4,
                                                                 tc * 128:(tc + 1) * 128],
                PC[:, 0:512].rearrange("p (a f) -> p a f", f=128), ALU.add),
                deps=(tids[-1],) + p.rdep(*[("acc", q4 * 4 + j) for j in range(4)]))
            p.note_r(av, "PC")
            for j in range(4):
                p.note_w(av, ("acc", q4 * 4 + j))

    acc_all = p.rdep(*[("acc", mc) for mc in range(16)])

    for pass_i, (func, dst_row) in enumerate([(AF.Square, 2), (AF.Identity, 1)]):
        for mc in range(16):
            sqb, sqk = (tmpxb, "tmpxb") if mc % 2 == 0 else (bc1, "bc1")
            cv = p.add("scalar", lambda s, mc=mc, func=func, sqb=sqb: s.activation(
                sqb[:, :], acc[:, mc * 1024:(mc + 1) * 1024], func),
                deps=acc_all + p.wdep(sqk) + p.rdep(("acc", mc)))
            p.note_r(cv, ("acc", mc))
            p.note_w(cv, sqk)
            for hh in range(2):
                i = p.add("tensor", lambda t, hh=hh, mc=mc, sqb=sqb: t.matmul(
                    PA[0:1, hh * 512:(hh + 1) * 512], onesb[:, 0:1],
                    sqb[:, hh * 512:(hh + 1) * 512],
                    start=(mc == 0), stop=(mc == 15)),
                    deps=(cv,) + (p.wdep("PA") if mc == 0 and hh == 0 else ()))
            p.note_r(i, sqk)
        p.note_w(i, "PA")
        dst_ap = (rowb[64:65, 0:1024] if dst_row == 2 else rowb[0:1, 1024:2048])
        i = p.add("scalar", lambda s, dst_ap=dst_ap: s.activation(
            dst_ap, PA[0:1, :], AF.Identity, scale=1.0 / 2048.0),
            deps=p.rdep("PA") + p.wdep(f"rowL{dst_row}"))
        p.note_r(i, "PA")
        p.note_w(i, f"rowL{dst_row}")
    i = p.add("vector", lambda v: v.tensor_mul(rowb[64:65, 1024:2048], rowb[0:1, 1024:2048],
                                               rowb[0:1, 1024:2048]),
              deps=p.rdep("rowL1") + p.wdep("rowL3"))
    p.note_w(i, "rowL3")
    i = p.add("vector", lambda v: v.tensor_sub(rowb[64:65, 0:1024], rowb[64:65, 0:1024],
                                               rowb[64:65, 1024:2048]),
              deps=p.rdep("rowL2", "rowL3") + p.wdep("rowL4"))
    p.note_w(i, "rowL4")
    i = p.add("scalar", lambda s: s.activation(rowb[32:33, 1024:2048], rowb[64:65, 0:1024],
                                               AF.Sqrt, bias=epsc[64:65, 0:1]),
              deps=p.rdep("rowL4", "epsc") + p.wdep("rowL3s"))
    p.note_w(i, "rowL3s")
    i = p.add("vector", lambda v: v.reciprocal(rowb[0:1, 0:1024], rowb[32:33, 1024:2048]),
              deps=p.rdep("rowL3s") + p.wdep("rowL0"))
    p.note_w(i, "rowL0")
    i = p.add("vector", lambda v: v.scalar_tensor_tensor(
        rowb[32:33, 0:1024], rowb[0:1, 1024:2048], -1.0, rowb[0:1, 0:1024],
        ALU.mult, ALU.mult),
        deps=p.rdep("rowL0", "rowL1") + p.wdep("rowL32"))
    p.note_w(i, "rowL32")
    for src_row, dst, dkey in [(0, bc1, "bc1"), (32, bcB, "bcB")]:
        for hh in range(2):
            i = p.add("tensor", lambda t, hh=hh, src_row=src_row: t.matmul(
                PA[:, hh * 512:(hh + 1) * 512], onesrf[src_row:src_row + 1, :],
                rowb[src_row:src_row + 1, hh * 512:hh * 512 + 512], start=True, stop=True),
                deps=p.rdep(f"rowL{src_row}", "onesrf") + (p.wdep("PA") if hh == 0 else ()))
            p.note_r(i, f"rowL{src_row}")
        p.note_w(i, "PA")
        i2 = p.add("vector", lambda v, dst=dst: v.tensor_copy(dst[:, :], PA[:, :]),
                   deps=(i,) + p.wdep(dkey))
        p.note_r(i2, "PA")
        p.note_w(i2, dkey)
    for mc in range(16):
        m1 = p.add("vector", lambda v, mc=mc: v.tensor_tensor(
            R1[:, mc * 1024:(mc + 1) * 1024], acc[:, mc * 1024:(mc + 1) * 1024],
            bc1[:, :], ALU.mult),
            deps=p.rdep(("acc", mc), "bc1") + p.wdep("R1"))
        p.note_r(m1, ("acc", mc), "bc1")
        p.note_w(m1, "R1")
        m2 = p.add("vector", lambda v, mc=mc: v.tensor_tensor(
            R1[:, mc * 1024:(mc + 1) * 1024], R1[:, mc * 1024:(mc + 1) * 1024],
            bcB[:, :], ALU.add),
            deps=(m1,) + p.rdep("bcB"))
        p.note_r(m2, "bcB")
        p.note_w(m2, ("ln2", mc))
        p.note_w(m2, "R1")
    ln2_all = p.rdep(*[("ln2", mc) for mc in range(16)])

    for kc in range(16):
        for hh in range(2):
            i = p.add("tensor", lambda t, kc=kc, hh=hh: t.matmul(
                PD[0:7, hh * 512:(hh + 1) * 512], wrS[:, kc * 7:(kc + 1) * 7],
                R1[:, kc * 1024 + hh * 512:kc * 1024 + (hh + 1) * 512],
                start=(kc == 0), stop=(kc == 15)),
                deps=(p.rdep(("ln2", kc)) + ((p.rdep("wrS") + p.wdep("PD"))
                      if kc == 0 and hh == 0 else ())) if hh == 0 else ())
    p.note_r(i, "R1")
    p.note_w(i, "PD")
    i = p.add("scalar", lambda s: s.activation(rowb[0:7, 0:1024], PD[0:7, :], AF.Identity,
                                               bias=brS[0:7, 0:1]),
              deps=p.rdep("PD", "brS") + p.wdep("rowG", "rowL0", "rowL1", "rowL2",
                                               "rowL3", "rowL3s", "rowL4", "row0", "row1",
                                               "rowL32"))
    p.note_r(i, "PD")
    p.note_w(i, "rowG")
    for tc in range(8):
        tg = p.add("tensor", lambda t, tc=tc: t.transpose(
            PC[:, 0:7], rowb[0:7, tc * 128:(tc + 1) * 128], ident_f[0:7, 0:7]),
            deps=p.rdep("rowG", "identf") + p.wdep("PC"))
        p.note_w(tg, "PC")
        ge = p.add("vector", lambda v: v.tensor_copy(routb[:, 0:7], PC[:, 0:7]),
                   deps=(tg,) + p.wdep("rt"))
        p.note_r(ge, "PC")
        p.note_w(ge, "rt")
        bg = p.add("vector", lambda v: v.tensor_tensor(routb[:, 8:15], routb[:, 0:7],
                                                       gb_bc[:, :], ALU.add),
                   deps=(ge,) + p.rdep("gb_bc"))
        m1 = p.add("vector", lambda v: v.reduce_max(routb[:, 16:17], routb[:, 8:15],
                                                    axis=mybir.AxisListType.X), deps=(bg,))
        e1 = p.add("vector", lambda v: v.tensor_single_scalar(
            routb[:, 17:24], routb[:, 8:15], routb[:, 16:17], ALU.is_equal),
            self_deps=(m1,))
        mk = p.add("vector", lambda v: v.scalar_tensor_tensor(
            routb[:, 24:31], routb[:, 17:24], -1e30, routb[:, 8:15], ALU.mult, ALU.add),
            deps=(e1,))
        m2 = p.add("vector", lambda v: v.reduce_max(routb[:, 31:32], routb[:, 24:31],
                                                    axis=mybir.AxisListType.X), deps=(mk,))
        e2 = p.add("vector", lambda v: v.tensor_single_scalar(
            routb[:, 32:39], routb[:, 24:31], routb[:, 31:32], ALU.is_equal),
            self_deps=(m2,))
        t1 = p.add("vector", lambda v: v.tensor_tensor(routb[:, 39:46], routb[:, 0:7],
                                                       routb[:, 17:24], ALU.mult), deps=(e1, ge))
        g1 = p.add("vector", lambda v: v.reduce_sum(routb[:, 46:47], routb[:, 39:46],
                                                    axis=mybir.AxisListType.X), deps=(t1,))
        t2 = p.add("vector", lambda v: v.tensor_tensor(routb[:, 39:46], routb[:, 0:7],
                                                       routb[:, 32:39], ALU.mult), deps=(e2, g1))
        g2 = p.add("vector", lambda v: v.reduce_sum(routb[:, 47:48], routb[:, 39:46],
                                                    axis=mybir.AxisListType.X), deps=(t2,))
        d12 = p.add("vector", lambda v: v.tensor_sub(routb[:, 16:17], routb[:, 46:47],
                                                     routb[:, 47:48]), deps=(g1, g2, e1))
        p1 = p.add("scalar", lambda s: s.activation(routb[:, 31:32], routb[:, 16:17],
                                                    AF.Sigmoid), deps=(d12, e2))
        p2 = p.add("scalar", lambda s: s.activation(routb[:, 24:25], routb[:, 16:17],
                                                    AF.Sigmoid, scale=-1.0), deps=(d12, mk))
        pr = p.add("vector", lambda v: v.tensor_scalar_mul(
            routb[:, 39:46], routb[:, 32:39], routb[:, 24:25]), deps=(p2, t2))
        pr2 = p.add("vector", lambda v: v.scalar_tensor_tensor(
            routb[:, 39:46], routb[:, 17:24], routb[:, 31:32], routb[:, 39:46],
            ALU.mult, ALU.add), deps=(pr, p1))
        p.note_w(pr2, "probs")
        tp_ = p.add("tensor", lambda t: t.transpose(
            PC[0:7, 0:128], routb[:, 39:46], ident_f[:, :]),
            deps=(pr2,) + p.wdep("PC"))
        p.note_w(tp_, "PC")
        pe = p.add("vector", lambda v, tc=tc: v.tensor_copy(
            probsTb[0:7, tc * 128:(tc + 1) * 128], PC[0:7, 0:128]),
            deps=(tp_,) + p.wdep("probsTb"))
        p.note_r(pe, "PC")
        p.note_w(pe, "probsTb")

    def expert_chain(e, up_h, gt_h, dn_h, bup_col, bgt_col, use_probs, bsd_col=None,
                     add_bdown=False):
        if use_probs:
            ro = 1024 * (e % 2)
            dk = ("prow", e % 2)
            dm = p.add("gpsimd", lambda g, e=e, ro=ro: g.dma_start(
                out=rowb[0:1, ro:ro + 1024], in_=probsTb.ap()[e:e + 1, 0:1024]),
                deps=p.rdep("probsTb") + p.wdep(dk, "rowL0", "rowG", "rowL1"))
            p.note_w(dm, dk)
            for hh in range(2):
                i = p.add("tensor", lambda t, hh=hh, ro=ro: t.matmul(
                    PA[:, hh * 512:(hh + 1) * 512], onesrf[0:1, :],
                    rowb[0:1, ro + hh * 512:ro + (hh + 1) * 512], start=True, stop=True),
                    deps=(p.rdep(dk, "onesrf") + p.wdep("PA")) if hh == 0 else ())
            p.note_w(i, "PA")
            be = p.add("vector", lambda v: v.tensor_copy(bc1[:, :], PA[:, :]),
                       deps=(i,) + p.wdep("bc1"))
            p.note_r(be, "PA")
            p.note_w(be, "bc1")
            p.note_r(be, dk)
        for mc in range(16):
            slab, skey = ring_load(up_h[mc, :, :])
            pkey, P_t = next_psum()
            mm_group(P_t, pkey, slab, skey, R1, "R1", kc_keys=lambda kc: ("ln2", kc))
            ev = p.add("scalar", lambda s, P=P_t, mc=mc, bup_col=bup_col: s.activation(
                R2[:, mc * 1024:(mc + 1) * 1024], P[:, :], AF.Identity,
                bias=bupP[:, bup_col + mc:bup_col + mc + 1] if bup_col is not None
                else bsuP[:, mc:mc + 1]),
                deps=p.rdep(pkey) + p.wdep("R2"))
            p.note_r(ev, pkey)
            p.note_w(ev, "R2")
            p.note_w(ev, ("u", mc))
        u_all = p.rdep(*[("u", mc) for mc in range(16)])
        for mc in range(16):
            slab, skey = ring_load(gt_h[mc, :, :])
            pkey, P_t = next_psum()
            mm_group(P_t, pkey, slab, skey, R2, "R2", kc_keys=lambda kc: ("u", kc))
            sv = p.add("scalar", lambda s, P=P_t, mc=mc, bgt_col=bgt_col: s.activation(
                R3[:, mc * 1024:(mc + 1) * 1024], P[:, :], AF.Silu,
                bias=bgtP[:, bgt_col + mc:bgt_col + mc + 1] if bgt_col is not None
                else bsgP[:, mc:mc + 1]),
                deps=p.rdep(pkey) + p.wdep("R3a"))
            p.note_r(sv, pkey)
            p.note_w(sv, "R3a")
            gv = p.add("vector", lambda v, mc=mc: v.tensor_tensor(
                R3[:, mc * 1024:(mc + 1) * 1024], R3[:, mc * 1024:(mc + 1) * 1024],
                R2[:, mc * 1024:(mc + 1) * 1024], ALU.mult),
                deps=(sv,) + p.rdep(("u", mc)))
            p.note_r(gv, "R2")
            last = gv
            if use_probs:
                last = p.add("vector", lambda v, mc=mc: v.tensor_tensor(
                    R3[:, mc * 1024:(mc + 1) * 1024], R3[:, mc * 1024:(mc + 1) * 1024],
                    bc1[:, :], ALU.mult), deps=(gv,) + p.rdep("bc1"))
                p.note_r(last, "bc1")
            p.note_w(last, ("g", mc))
            p.note_w(last, "R3a")
        g_all = p.rdep(*[("g", mc) for mc in range(16)])
        for mc in range(16):
            slab, skey = ring_load(dn_h[mc, :, :])
            pkey, P_t = next_psum()
            first_deps = p.wdep(pkey) + p.rdep(skey, "R3a")
            last = None
            for kc in range(16):
                dd0 = (first_deps if kc == 0 else ()) + p.rdep(("g", kc))
                for hh in range(2):
                    last = p.add("tensor", lambda t, P=P_t, sl=slab, kc=kc, hh=hh:
                                 t.matmul(P[:, hh * 512:(hh + 1) * 512],
                                          sl[:, kc * 128:(kc + 1) * 128],
                                          R3[:, kc * 1024 + hh * 512:kc * 1024 + (hh + 1) * 512],
                                          start=(kc == 0), stop=(kc == 15 and not add_bdown)),
                                 deps=dd0 if hh == 0 else ())
            if add_bdown:
                for hh in range(2):
                    last = p.add("tensor", lambda t, P=P_t, mc=mc, hh=hh: t.matmul(
                        P[:, hh * 512:(hh + 1) * 512], bdnT[0:7, mc * 128:(mc + 1) * 128],
                        probsTb[0:7, hh * 512:(hh + 1) * 512], start=False, stop=True),
                        deps=p.rdep("bdnT", "probsTb"))
            p.note_r(last, skey, "R3a")
            p.note_w(last, pkey)
            if bsd_col is None:
                av = p.add("vector", lambda v, P=P_t, mc=mc: v.tensor_tensor(
                    acc[:, mc * 1024:(mc + 1) * 1024], acc[:, mc * 1024:(mc + 1) * 1024],
                    P[:, :], ALU.add), deps=p.rdep(pkey, ("acc", mc)) + p.wdep(("acc", mc)))
            else:
                av = p.add("vector", lambda v, P=P_t, mc=mc: v.scalar_tensor_tensor(
                    acc[:, mc * 1024:(mc + 1) * 1024], P[:, :], bsdP[:, mc:mc + 1],
                    acc[:, mc * 1024:(mc + 1) * 1024], ALU.add, ALU.add),
                    deps=p.rdep(pkey, ("acc", mc), "bsdP") + p.wdep(("acc", mc)))
            p.note_r(av, pkey)
            p.note_w(av, ("acc", mc))

    for e in range(NE):
        expert_chain(e, wup_h[e], wgt_h[e], wdn_h[e], e * 16, e * 16, True)
    expert_chain(None, wsu_h, wsg_h, wsd_h, None, None, False, bsd_col=0, add_bdown=True)

    for tc in range(8):
        k = tc % 2
        evs = []
        for q4 in range(4):
            tids = []
            for j in range(4):
                fc = q4 * 4 + j
                ti = p.add("tensor", lambda t, fc=fc, j=j, tc=tc: t.transpose(
                    PC[:, j * 128:(j + 1) * 128],
                    acc[:, fc * 1024 + tc * 128:fc * 1024 + (tc + 1) * 128],
                    ident_f[:, :]),
                    deps=(p.rdep(*[("acc", q4 * 4 + jj) for jj in range(4)]) + p.wdep("PC"))
                         if j == 0 else ())
                tids.append(ti)
            for jj in range(4):
                p.note_r(tids[-1], ("acc", q4 * 4 + jj))
            p.note_w(tids[-1], "PC")
            ev = p.add("vector", lambda v, q4=q4, k=k: v.tensor_copy(
                xstage[k][:, q4 * 512:(q4 + 1) * 512], PC[:, 0:512]),
                deps=(tids[-1],) + p.wdep(("xs", k)))
            p.note_r(ev, "PC")
            p.note_w(ev, ("xs", k))
            evs.append(ev)
        st_ = p.add("sync", lambda s, k=k, tc=tc: s.dma_start(
            out=y_h[tc * 128:(tc + 1) * 128, :], in_=xstage[k][:, :]),
            deps=tuple(evs))
        p.note_r(st_, ("xs", k))

    sems = {}
    sem_ctx = []
    for e in Prog.ENGINES:
        c = nc.semaphore(f"sem_{e}")
        sem_ctx.append(c)
        sems[e] = c.__enter__()
    p.finalize(nc, sems)
    for c in reversed(sem_ctx):
        c.__exit__(None, None, None)
    for _, g_ in reversed(open_ctx):
        try:
            g_.__exit__(None, None, None)
        except Exception:
            pass
    for g_ in reversed(cph):
        g_.__exit__(None, None, None)
    for g_ in reversed(psum_ctx):
        g_.__exit__(None, None, None)
    _nc_cache[0] = nc
    return nc


def _prep_host(inputs):
    def tile_w(Wt):
        K, M = Wt.shape
        return np.ascontiguousarray(
            Wt.reshape(16, 128, M // 128, 128).transpose(2, 1, 0, 3).reshape(M // 128, 128, 2048),
            dtype=np.float32)

    def col_pack(v):
        return np.ascontiguousarray(np.asarray(v, np.float32).reshape(16, 128).T)

    qkv = np.asarray(inputs["L0_attn_qkv_w"], np.float32).reshape(H, 3, D, HID)
    com = {}
    com["wq"] = tile_w(np.ascontiguousarray(qkv[:, 0].reshape(HID, HID).T))
    com["wk"] = tile_w(np.ascontiguousarray(qkv[:, 1].reshape(HID, HID).T))
    com["wv"] = tile_w(np.ascontiguousarray(qkv[:, 2].reshape(HID, HID).T))
    com["wg"] = tile_w(np.ascontiguousarray(np.asarray(inputs["L0_attn_gate_w"], np.float32).T))
    com["wo"] = tile_w(np.ascontiguousarray(np.asarray(inputs["L0_attn_out_w"], np.float32).T))
    Wr = np.asarray(inputs["L0_ffn_Wr"], np.float32)
    wr_t = np.zeros((128, 112), np.float32)
    for kc in range(16):
        wr_t[:, kc * 7:(kc + 1) * 7] = Wr[kc * 128:(kc + 1) * 128, :]
    com["wr"] = wr_t
    com["wup"] = np.stack([tile_w(np.asarray(inputs["L0_ffn_Wup"][e], np.float32)) for e in range(NE)])
    com["wgt"] = np.stack([tile_w(np.asarray(inputs["L0_ffn_Wgate"][e], np.float32)) for e in range(NE)])
    com["wdn"] = np.stack([tile_w(np.asarray(inputs["L0_ffn_Wdown"][e], np.float32)) for e in range(NE)])
    com["wsu"] = tile_w(np.asarray(inputs["L0_ffn_Wsu"], np.float32))
    com["wsg"] = tile_w(np.asarray(inputs["L0_ffn_Wsg"], np.float32))
    com["wsd"] = tile_w(np.asarray(inputs["L0_ffn_Wsd"], np.float32))
    com["bupP"] = np.concatenate([col_pack(np.asarray(inputs["L0_ffn_bup"][e])) for e in range(NE)], 1)
    com["bgtP"] = np.concatenate([col_pack(np.asarray(inputs["L0_ffn_bgate"][e])) for e in range(NE)], 1)
    com["bdnT"] = np.ascontiguousarray(np.asarray(inputs["L0_ffn_bdown"], np.float32))
    com["bsuP"] = col_pack(inputs["L0_ffn_bsu"])
    com["bsgP"] = col_pack(inputs["L0_ffn_bsg"])
    com["bsdP"] = col_pack(inputs["L0_ffn_bsd"])
    com["nwP"] = col_pack(inputs["L0_attn_norm_w"])
    com["br"] = np.ascontiguousarray(np.asarray(inputs["L0_ffn_br"], np.float32).reshape(7, 1))
    com["gb"] = np.ascontiguousarray(np.asarray(inputs["L0_ffn_gate_bias"], np.float32).reshape(1, 7))
    return com


def kernel(**inputs):
    nc = build_nc()
    com = _prep_host(inputs)
    x = np.asarray(inputs["L0_attn_hidden_states"], np.float32)
    zeros = np.zeros((T, HID), np.float32)
    in_maps = []
    for core in range(8):
        b, half = core // 2, core % 2
        m = {"i_" + k: v for k, v in com.items()}
        m["i_xm"] = np.ascontiguousarray(x[b, half * T:(half + 1) * T])
        m["i_xp"] = zeros if half == 0 else np.ascontiguousarray(x[b, 0:T])
        in_maps.append(m)
    res = run_bass_kernel_spmd(nc, in_maps, core_ids=list(range(8)))
    out = np.empty((4, 2048, HID), np.float32)
    for core in range(8):
        b, half = core // 2, core % 2
        out[b, half * T:(half + 1) * T] = res.results[core]["y"]
    return out


# version 16
# speedup vs baseline: 1.0025x; 1.0025x over previous
import numpy as np
import concourse.bass as bass
import concourse.mybir as mybir
from concourse.bass_utils import run_bass_kernel_spmd

F32 = mybir.dt.float32
BF16 = mybir.dt.bfloat16
AF = mybir.ActivationFunctionType
ALU = mybir.AluOpType

H, D, BS, HID, NE = 32, 64, 256, 2048, 7
T = 1024
EPS = 1e-5

_nc_cache = [None]


def _host_consts():
    import ml_dtypes
    base = 1.0 / 2.0 ** (8.0 / H)
    rate = base ** (np.arange(H, dtype=np.float64) + 1.0)
    factor = 1.0 - 0.0 / (H - 1 + 1e-05) + 1e-05
    slope = rate * factor
    r1 = np.arange(1, BS + 1, dtype=np.float64)
    qd = np.exp(-slope[:, None] * r1[None, :])
    kd = np.exp(-slope[:, None] * (BS - r1[None, :]))
    bd = np.exp(-slope * BS)
    qdF = np.zeros((128, 16 * BS), np.float32)
    for hp in range(16):
        for p_ in range(128):
            qdF[p_, hp * BS:(hp + 1) * BS] = qd[2 * hp + p_ // 64]
    kdT = np.zeros((128, 2 * HID), np.float32)
    for par in range(2):
        for m in range(HID):
            kdT[:, par * HID + m] = kd[m // 64, par * 128:(par + 1) * 128]
    Darg = np.zeros((128, 512), np.float32)
    for cj in range(2):
        for p_ in range(128):
            tj = cj * 128 + p_
            row = np.arange(BS) - tj
            Darg[p_, cj * BS:(cj + 1) * BS] = np.where(row >= 0, row, 1e9)
    bdv = np.zeros((128, 16), np.float32)
    for ghp in range(16):
        bdv[0:64, ghp] = bd[2 * ghp]
        bdv[64:128, ghp] = bd[2 * ghp + 1]
    return (qdF.astype(ml_dtypes.bfloat16), kdT.astype(ml_dtypes.bfloat16),
            Darg, bdv, [float(-s) for s in slope])


class Prog:
    ENGINES = ("tensor", "vector", "scalar", "gpsimd", "sync")
    DMA_ENG = {"gpsimd", "sync"}

    def __init__(self):
        self.ops = []
        self.buf = {}
        self._pend = set()

    def add(self, eng, fn, deps=(), self_deps=()):
        dd = tuple(sorted({d for d in deps if d is not None}))
        sd = tuple(sorted({d for d in self_deps if d is not None}))
        self.ops.append([eng, fn, dd, sd])
        i = len(self.ops) - 1
        for n in self._pend:
            self._b(n)["r"].append(i)
        self._pend.clear()
        return i

    def _b(self, n):
        return self.buf.setdefault(n, {"w": [], "r": []})

    def rdep(self, *names):
        out = []
        for n in names:
            out += self._b(n)["w"]
        self._pend.update(names)
        return tuple(out)

    def wdep(self, *names):
        out = []
        for n in names:
            b = self._b(n)
            out += b["w"] + b["r"]
        return tuple(out)

    def note_r(self, i, *names):
        for n in names:
            self._b(n)["r"].append(i)

    def note_w(self, i, *names):
        for n in names:
            self._b(n)["w"].append(i)

    def finalize(self, nc, sems):
        NSUB = 4
        eng_ops = {e: [] for e in self.ENGINES}
        for i, (e, fn, dd, sd) in enumerate(self.ops):
            eng_ops[e].append(i)
        SYNC_SELF = {"vector", "scalar"}
        marked = [self.ops[i][0] in self.DMA_ENG for i in range(len(self.ops))]
        for i, (e, fn, dd, sd) in enumerate(self.ops):
            for d in dd:
                de = self.ops[d][0]
                if de != e or (e in SYNC_SELF and de not in self.DMA_ENG):
                    marked[d] = True
            for d in sd:
                marked[d] = True
        inc_info = [None] * len(self.ops)
        for e in self.ENGINES:
            if e in self.DMA_ENG:
                k = 0
                for i in eng_ops[e]:
                    sub = k % NSUB
                    rank = k // NSUB
                    inc_info[i] = (sub, 16 * (rank + 1))
                    k += 1
            else:
                c = 0
                for i in eng_ops[e]:
                    if marked[i]:
                        c += 1
                    inc_info[i] = (None, c)

        def emit(eng_obj, e):
            last_wait = {}
            dma_k = 0
            for i in eng_ops[e]:
                need = {}
                for d in self.ops[i][2]:
                    de = self.ops[d][0]
                    if de == e and e not in ("vector", "scalar"):
                        continue
                    sub, v = inc_info[d]
                    key = (de, sub)
                    if v > need.get(key, -1):
                        need[key] = v
                for d in self.ops[i][3]:
                    de = self.ops[d][0]
                    sub, v = inc_info[d]
                    key = (de, sub)
                    if v > need.get(key, -1):
                        need[key] = v
                if e in self.DMA_ENG:
                    sub = dma_k % NSUB
                    rank = dma_k // NSUB
                    if rank >= 1:
                        key = (e, sub)
                        v = 16 * rank
                        if v > need.get(key, -1):
                            need[key] = v
                    dma_k += 1
                for (de, sub), v in sorted(need.items(), key=lambda x: (str(x[0][0]), -1 if x[0][1] is None else x[0][1], x[1])):
                    sh = sems[de][sub] if sub is not None else sems[de]
                    kk = (de, sub)
                    if v > last_wait.get(kk, 0):
                        eng_obj.wait_ge(sh, v)
                        last_wait[kk] = v
                inst = self.ops[i][1](eng_obj)
                if e in self.DMA_ENG:
                    sub, v = inc_info[i]
                    inst.then_inc(sems[e][sub], 16)
                elif marked[i]:
                    inst.then_inc(sems[e], 1)
            if e == "sync":
                for sub in range(NSUB):
                    tot = 0
                    k = 0
                    for i in eng_ops[e]:
                        if k % NSUB == sub:
                            tot = inc_info[i][1]
                        k += 1
                    if tot:
                        eng_obj.wait_ge(sems[e][sub], tot)

        with nc.Block() as blk:
            @blk.tensor
            def _(t):
                emit(t, "tensor")

            @blk.vector
            def _(v):
                emit(v, "vector")

            @blk.scalar
            def _(s):
                emit(s, "scalar")

            @blk.gpsimd
            def _(g):
                emit(g, "gpsimd")

            @blk.sync
            def _(s):
                emit(s, "sync")

def build_nc():
    if _nc_cache[0] is not None:
        return _nc_cache[0]
    import ml_dtypes
    qdF_h, kdT_h, Darg_h, bdv_h, neg_slope = _host_consts()
    nc = bass.Bass("TRN2", target_bir_lowering=False)

    def din(name, shape):
        return nc.dram_tensor("i_" + name, shape, F32, kind="ExternalInput").ap()

    xm_h = din("xm", [T, HID])
    xp_h = din("xp", [T, HID])
    wq_h = din("wq", [16, 128, 2048])
    wk_h = din("wk", [16, 128, 2048])
    wv_h = din("wv", [16, 128, 2048])
    wg_h = din("wg", [16, 128, 2048])
    wo_h = din("wo", [16, 128, 2048])
    wr_h = din("wr", [128, 112])
    wup_h = din("wup", [NE, 16, 128, 2048])
    wgt_h = din("wgt", [NE, 16, 128, 2048])
    wdn_h = din("wdn", [NE, 16, 128, 2048])
    wsu_h = din("wsu", [16, 128, 2048])
    wsg_h = din("wsg", [16, 128, 2048])
    wsd_h = din("wsd", [16, 128, 2048])
    bupP_h = din("bupP", [128, 112])
    bgtP_h = din("bgtP", [128, 112])
    bdnT_h = din("bdnT", [7, 2048])
    bsuP_h = din("bsuP", [128, 16])
    bsgP_h = din("bsgP", [128, 16])
    bsdP_h = din("bsdP", [128, 16])
    nwP_h = din("nwP", [128, 16])
    br_h = din("br", [7, 1])
    gb_h = din("gb", [1, 7])
    y_h = nc.dram_tensor("y", [T, HID], F32, kind="ExternalOutput").ap()

    ident_f_h = nc.inline_tensor(np.eye(128, dtype=np.float32), "identf_inl").ap()
    ident_b_h = nc.inline_tensor(np.eye(128, dtype=ml_dtypes.bfloat16), "identb_inl").ap()
    onesb_h = nc.inline_tensor(np.ones((128, 1), dtype=ml_dtypes.bfloat16), "onesb_inl").ap()
    _onesrf = np.zeros((33, 128), dtype=np.float32)
    _onesrf[0, :] = 1.0
    _onesrf[32, :] = 1.0
    onesrf_h = nc.inline_tensor(_onesrf, "onesrf_inl").ap()
    onesrb_h = nc.inline_tensor(np.ones((1, 128), dtype=ml_dtypes.bfloat16), "onesrb_inl").ap()
    qdF_hh = nc.inline_tensor(qdF_h, "qdF_inl").ap()
    kdT_hh = nc.inline_tensor(kdT_h, "kdT_inl").ap()
    Darg_hh = nc.inline_tensor(Darg_h, "Darg_inl").ap()
    bdv_hh = nc.inline_tensor(bdv_h, "bdv_inl").ap()


    p = Prog()
    open_ctx = []

    def sb(name, shp, dt):
        g = nc.sbuf_tensor(name, shp, dt)
        h = g.__enter__()
        open_ctx.append((name, g))
        return h

    psum_ctx = []

    def psum(name, shp, dt=F32):
        g = nc.psum_tensor(name, shp, dt)
        h = g.__enter__()
        psum_ctx.append(g)
        return h

    ident_f = sb("ident_f", [128, 128], F32)
    ident_b = sb("ident_b", [128, 128], BF16)
    onesb = sb("onesb", [128, 1], BF16)
    onesrf = sb("onesrf", [33, 128], F32)
    onesrb = sb("onesrb", [1, 128], BF16)
    nwP = sb("nwP", [128, 16], F32)
    bupP = sb("bupP", [128, 112], F32)
    bgtP = sb("bgtP", [128, 112], F32)
    bsuP = sb("bsuP", [128, 16], F32)
    bsgP = sb("bsgP", [128, 16], F32)
    bsdP = sb("bsdP", [128, 16], F32)
    brS = sb("brS", [7, 1], F32)
    gbS = sb("gbS", [1, 7], F32)
    gb_bc = sb("gb_bc", [128, 7], F32)
    wrS = sb("wrS", [128, 112], BF16)
    statc = sb("statc", [128, 32], F32)
    epsc = sb("epsc", [128, 1], F32)
    routb = sb("routb", [128, 48], F32)
    R1 = sb("R1", [128, 16384], BF16)
    R2 = sb("R2", [128, 16384], BF16)
    R3 = sb("R3", [128, 16384], BF16)
    NRING = 2
    ring = [sb(f"ring{i}", [128, 2048], BF16) for i in range(NRING)]

    PA = psum("PA", [128, 1024])
    PC = psum("PC", [128, 512])
    PD = psum("PD", [128, 1024])
    PE_ = psum("PE", [128, 1024])
    PT = psum("PT", [128, 1024], BF16)

    xstage = [sb(f"xstage{i}", [128, 2048], F32) for i in range(2)]

    bph = []

    def sb_b(name, shp, dt):
        g = nc.sbuf_tensor(name, shp, dt)
        h = g.__enter__()
        bph.append(g)
        return h

    Darg = sb_b("Darg", [128, 512], F32)
    qdFg = sb_b("qdFg", [128, 1024], BF16)
    kdTg = sb_b("kdTg", [128, 1024], BF16)
    bdv = sb_b("bdv", [128, 16], F32)
    S_sb = sb_b("S_sb", [128, 512], F32)
    S_bf = sb_b("S_bf", [128, 512], BF16)
    qqd = sb_b("qqd", [128, 2048], BF16)
    ddT = sb_b("ddT", [128, 2048], BF16)
    awT = sb_b("awT", [128, 2048], BF16)
    q_fm = sb_b("q_fm", [128, 4096], BF16)
    k_fm = sb_b("k_fm", [128, 4096], BF16)
    tmp_fm = sb_b("tmp_fm", [128, 4096], BF16)
    k_tm = sb_b("k_tm", [128, 4096], BF16)
    v_tm = sb_b("v_tm", [128, 4096], BF16)

    for dst, src, key in [(ident_f, ident_f_h, "identf"), (ident_b, ident_b_h, "identb"),
                          (onesb, onesb_h, "onesb"), (onesrf, onesrf_h, "onesrf"),
                          (onesrb, onesrb_h, "onesrb"), (nwP, nwP_h, "nwP"),
                          (bupP, bupP_h, "bupP"), (bgtP, bgtP_h, "bgtP"),
                          (bsuP, bsuP_h, "bsuP"), (bsgP, bsgP_h, "bsgP"),
                          (bsdP, bsdP_h, "bsdP"), (brS, br_h, "brS"),
                          (gbS, gb_h, "gbS"), (Darg, Darg_hh, "Darg"),
                          (bdv, bdv_hh, "bdv")]:
        i = p.add("sync", lambda s, d=dst, so=src: s.dma_start(out=d[:, :], in_=so))
        p.note_w(i, key)
    i = p.add("vector", lambda v: v.memset(epsc[:, 0:1], EPS))
    p.note_w(i, "epsc")
    i = p.add("gpsimd", lambda g: g.dma_start(out=wrS[:, :], in_=wr_h))
    p.note_w(i, "wrS")
    i = p.add("tensor", lambda t: t.matmul(PC[0:128, 0:7], onesrf[0:1, :], gbS[0:1, :],
                                           start=True, stop=True),
              deps=p.rdep("onesrf", "gbS") + p.wdep("PC"))
    p.note_w(i, "PC")
    i2 = p.add("vector", lambda v: v.tensor_copy(gb_bc[:, :], PC[:, 0:7]),
               deps=p.rdep("PC") + p.wdep("gb_bc"))
    p.note_r(i2, "PC")
    p.note_w(i2, "gb_bc")

    ring_i = [0]

    def ring_load(src_ap):
        s = ring_i[0] % NRING
        ring_i[0] += 1
        key = ("ring", s)
        i = p.add("gpsimd", lambda g, d=ring[s], so=src_ap: g.dma_start(out=d[:, :], in_=so),
                  deps=p.wdep(key))
        p.note_w(i, key)
        return ring[s], key

    rot = [0]
    PROJ = [("PD", PD), ("PE", PE_)]

    def next_psum():
        r = PROJ[rot[0] % 2]
        rot[0] += 1
        return r

    def mm_group(P_t, pkey, slab, skey, rhs, rkey, extra=(), kc_keys=None):
        first_deps = p.wdep(pkey) + p.rdep(skey, rkey) + tuple(extra)
        last = None
        for kc in range(16):
            dd = first_deps if kc == 0 else ()
            if kc_keys is not None:
                dd = dd + p.rdep(kc_keys(kc))
            for hh in range(2):
                st = (kc == 0)
                sp = (kc == 15)
                last = p.add("tensor", lambda t, P=P_t, sl=slab, kc=kc, hh=hh, st=st, sp=sp, rb=rhs:
                             t.matmul(P[:, hh * 512:(hh + 1) * 512],
                                      sl[:, kc * 128:(kc + 1) * 128],
                                      rb[:, kc * 1024 + hh * 512:kc * 1024 + (hh + 1) * 512],
                                      start=st, stop=sp),
                             deps=dd if hh == 0 else ())
        p.note_r(last, skey, rkey)
        p.note_w(last, pkey)
        if kc_keys is not None:
            for kc in range(16):
                p.note_r(last, kc_keys(kc))
        return last

    def transpose_batch(src, src_cols, src_key, src_deps, dst_ap_fn, dst_key, ident):
        tids = []
        for j, c0 in enumerate(src_cols):
            ti = p.add("tensor", lambda t, j=j, c0=c0: t.transpose(
                PT[:, j * 128:(j + 1) * 128], src[:, c0:c0 + 128], ident[:, :]),
                deps=(p.wdep("PT") + tuple(src_deps) + p.rdep(src_key)) if j == 0 else ())
            tids.append(ti)
        p.note_r(tids[-1], src_key)
        p.note_w(tids[-1], "PT")
        i2 = p.add("vector", lambda v: v.tensor_copy(
            dst_ap_fn(), PT[:, 0:len(src_cols) * 128].rearrange("p (a f) -> p a f", f=128)),
            deps=(tids[-1],) + p.wdep(dst_key))
        p.note_r(i2, "PT")
        p.note_w(i2, dst_key)
        return i2

    def stage_A(jobs):
        for tc in range(8):
          for k, (x_hbm, dst, dst_key) in enumerate(jobs):
              so = 16 * k
              lo = 2048 * k
              i = p.add("sync", lambda s, k=k, tc=tc, xh=x_hbm: s.dma_start(
                  out=xstage[k][:, :], in_=xh[tc * 128:(tc + 1) * 128, :]),
                  deps=p.wdep(("xs", k)))
              p.note_w(i, ("xs", k))
              i = p.add("vector", lambda v, k=k, so=so: v.reduce_sum(
                  statc[:, so:so + 1], xstage[k][:, :], axis=mybir.AxisListType.X),
                  deps=p.rdep(("xs", k)) + p.wdep(("st_sum", k)))
              p.note_r(i, ("xs", k))
              p.note_w(i, ("st_sum", k))
              i = p.add("scalar", lambda s, k=k, so=so: s.activation(
                  R3[:, 4096 + 2048 * k:6144 + 2048 * k], xstage[k][:, :], AF.Square,
                  accum_out=statc[:, so + 2:so + 3]),
                  deps=p.rdep(("xs", k)) + p.wdep(("st_sq", k), ("R3dump", k)))
              p.note_r(i, ("xs", k))
              p.note_w(i, ("st_sq", k))
              p.note_w(i, ("R3dump", k))
              i = p.add("scalar", lambda s, so=so: s.activation(
                  statc[:, so + 4:so + 5], statc[:, so:so + 1], AF.Identity, scale=1.0 / 2048.0),
                  deps=p.rdep(("st_sum", k)) + p.wdep(("st_m", k)))
              p.note_w(i, ("st_m", k))
              i = p.add("scalar", lambda s, so=so: s.activation(
                  statc[:, so + 6:so + 7], statc[:, so + 2:so + 3], AF.Identity, scale=1.0 / 2048.0),
                  deps=p.rdep(("st_sq", k)) + p.wdep(("st_e2", k)))
              p.note_w(i, ("st_e2", k))
              i = p.add("vector", lambda v, so=so: v.tensor_mul(
                  statc[:, so + 8:so + 9], statc[:, so + 4:so + 5], statc[:, so + 4:so + 5]),
                  deps=p.rdep(("st_m", k)) + p.wdep(("st_msq", k)))
              p.note_w(i, ("st_msq", k))
              i = p.add("vector", lambda v, so=so: v.tensor_sub(
                  statc[:, so + 10:so + 11], statc[:, so + 6:so + 7], statc[:, so + 8:so + 9]),
                  deps=p.rdep(("st_e2", k), ("st_msq", k)) + p.wdep(("st_var", k)))
              p.note_w(i, ("st_var", k))
              i = p.add("scalar", lambda s, so=so: s.activation(
                  statc[:, so + 12:so + 13], statc[:, so + 10:so + 11], AF.Sqrt,
                  bias=epsc[:, 0:1]),
                  deps=p.rdep(("st_var", k), "epsc") + p.wdep(("st_sd", k)))
              p.note_w(i, ("st_sd", k))
              i = p.add("vector", lambda v, so=so: v.reciprocal(
                  statc[:, so + 14:so + 15], statc[:, so + 12:so + 13]),
                  deps=p.rdep(("st_sd", k)) + p.wdep(("st_a", k)))
              p.note_w(i, ("st_a", k))
              i = p.add("vector", lambda v, so=so: v.scalar_tensor_tensor(
                  statc[:, so + 8:so + 9], statc[:, so + 4:so + 5], -1.0,
                  statc[:, so + 14:so + 15], ALU.mult, ALU.mult),
                  deps=p.rdep(("st_a", k), ("st_m", k)) + p.wdep(("st_msq", k)))
              p.note_w(i, ("st_b", k))
              i = p.add("scalar", lambda s, k=k, so=so, lo=lo: s.activation(
                  R3[:, lo:lo + 2048], xstage[k][:, :], AF.Identity,
                  bias=statc[:, so + 8:so + 9], scale=statc[:, so + 14:so + 15]),
                  deps=p.rdep(("xs", k), ("st_a", k), ("st_b", k)) + p.wdep(("lntm", k)))
              p.note_r(i, ("xs", k))
              p.note_w(i, ("lntm", k))
              for q4 in range(4):
                  transpose_batch(
                      R3, [lo + (q4 * 4 + j) * 128 for j in range(4)], ("lntm", k), (),
                      lambda tc=tc, q4=q4, dst=dst: dst.ap().rearrange(
                          "p (fc t) -> p fc t", t=1024)[:, q4 * 4:(q4 + 1) * 4,
                                                        tc * 128:(tc + 1) * 128],
                      dst_key, ident_b)

    stage_A([(xm_h, R1, "R1"), (xp_h, R2, "R2")])

    def project(slab, skey, rhs, rkey, dst, dst_key, dst_col, extra=()):
        pkey, P_t = next_psum()
        mm_group(P_t, pkey, slab, skey, rhs, rkey, extra=extra)
        ev = p.add("scalar", lambda s, P=P_t, d=dst, dc=dst_col: s.activation(
            d[:, dc:dc + 1024], P[:, :], AF.Silu),
            deps=p.rdep(pkey) + p.wdep(dst_key))
        p.note_r(ev, pkey)
        p.note_w(ev, dst_key)
        return ev

    for g in range(4):
        i = p.add("sync", lambda s, g=g: s.dma_start(
            out=qdFg[:, :], in_=qdF_hh[:, g * 1024:(g + 1) * 1024]),
            deps=p.wdep("qdFg"))
        p.note_w(i, "qdFg")
        i = p.add("sync", lambda s, g=g: s.dma_start(
            out=kdTg[:, :].rearrange("p (a m) -> p a m", m=512),
            in_=kdT_hh.rearrange("p (a m) -> p a m", m=2048)[:, :, g * 512:(g + 1) * 512]),
            deps=p.wdep("kdTg"))
        p.note_w(i, "kdTg")

        pend = []
        for mcl in range(4):
            mc = g * 4 + mcl
            slab, skey = ring_load(wk_h[mc, :, :])
            project(slab, skey, R1, "R1", k_fm, ("kf", g, mcl), mcl * 1024)
            evp = project(slab, skey, R2, "R2", tmp_fm, ("tm", mcl), mcl * 1024)
            for th in pend:
                th()
            pend = [lambda q4=q4, mcl=mcl, evp=evp: transpose_batch(
                tmp_fm, [mcl * 1024 + (q4 * 4 + j) * 128 for j in range(4)],
                ("tm", mcl), (evp,),
                lambda q4=q4, mcl=mcl: k_tm.ap().rearrange(
                    "p (tc j) -> p tc j", j=512)[:, q4 * 4:(q4 + 1) * 4,
                                                 mcl * 128:(mcl + 1) * 128],
                "ktm", ident_b) for q4 in range(2)]
        for mcl in range(4):
            mc = g * 4 + mcl
            slab, skey = ring_load(wv_h[mc, :, :])
            project(slab, skey, R1, "R1", q_fm, ("vmain", g, mcl), mcl * 1024)
            evp = project(slab, skey, R2, "R2", tmp_fm, ("tm", mcl), mcl * 1024)
            for th in pend:
                th()
            pend = [lambda q4=q4, mcl=mcl, evp=evp: transpose_batch(
                tmp_fm, [mcl * 1024 + (q4 * 4 + j) * 128 for j in range(4)],
                ("tm", mcl), (evp,),
                lambda q4=q4, mcl=mcl: v_tm.ap().rearrange(
                    "p (tc j) -> p tc j", j=512)[:, q4 * 4:(q4 + 1) * 4,
                                                 mcl * 128:(mcl + 1) * 128],
                "vtm", ident_b) for q4 in range(2)]
        for th in pend:
            th()
        for tc in range(8):
            i = p.add("vector", lambda v, tc=tc: v.tensor_tensor(
                k_tm[:, tc * 512:(tc + 1) * 512], k_tm[:, tc * 512:(tc + 1) * 512],
                kdTg[:, (tc % 2) * 512:((tc % 2) + 1) * 512], ALU.mult),
                deps=p.rdep("ktm", "kdTg") + p.wdep(("kkd", tc)))
            p.note_r(i, "kdTg")
            p.note_w(i, "ktm")
            p.note_w(i, ("kkd", tc))
        i = p.add("vector", lambda v: v.memset(S_sb[:, :], 0.0), deps=p.wdep("S"))
        p.note_w(i, "S")
        for c in range(4):
            ghp = g * 4 + c
            for pb in range(4):
                mm1 = p.add("tensor", lambda t, pb=pb, c=c: t.matmul(
                    PC[:, 256:384],
                    k_tm[:, (2 * pb) * 512 + c * 128:(2 * pb) * 512 + (c + 1) * 128],
                    v_tm[:, (2 * pb) * 512 + c * 128:(2 * pb) * 512 + (c + 1) * 128],
                    start=True, stop=False),
                    deps=p.rdep("ktm", "vtm") + p.wdep("PC"))
                mm2 = p.add("tensor", lambda t, pb=pb, c=c: t.matmul(
                    PC[:, 256:384],
                    k_tm[:, (2 * pb + 1) * 512 + c * 128:(2 * pb + 1) * 512 + (c + 1) * 128],
                    v_tm[:, (2 * pb + 1) * 512 + c * 128:(2 * pb + 1) * 512 + (c + 1) * 128],
                    start=False, stop=True))
                p.note_r(mm2, "ktm", "vtm")
                p.note_w(mm2, "PC")
                dec = p.add("vector", lambda v, c=c, ghp=ghp: v.tensor_scalar_mul(
                    S_sb[:, c * 128:(c + 1) * 128], S_sb[:, c * 128:(c + 1) * 128],
                    bdv[:, ghp:ghp + 1]),
                    deps=p.rdep("bdv") + p.wdep(("Sc", c)))
                p.note_w(dec, ("Sc", c))
                ad = p.add("vector", lambda v, c=c: v.tensor_tensor(
                    S_sb[:, c * 128:(c + 1) * 128], S_sb[:, c * 128:(c + 1) * 128],
                    PC[:, 256:384], ALU.add),
                    deps=(mm2,) + p.wdep(("Sc", c)))
                p.note_r(ad, "PC")
                p.note_w(ad, ("Sc", c))
            cst = p.add("scalar", lambda s, c=c: s.activation(
                S_bf[:, c * 128:(c + 1) * 128], S_sb[:, c * 128:(c + 1) * 128], AF.Identity),
                deps=p.rdep(("Sc", c)) + p.wdep(("Sbf", c)))
            p.note_r(cst, ("Sc", c))
            p.note_w(cst, ("Sbf", c))
        pref_done = tuple(p._b("PC")["r"])
        for mcl in range(4):
            for q4 in range(2):
                transpose_batch(
                    k_fm, [mcl * 1024 + (q4 * 4 + j) * 128 for j in range(4)],
                    ("kf", g, mcl), pref_done,
                    lambda q4=q4, mcl=mcl: k_tm.ap().rearrange(
                        "p (tc j) -> p tc j", j=512)[:, q4 * 4:(q4 + 1) * 4,
                                                     mcl * 128:(mcl + 1) * 128],
                    "ktm", ident_b)
                transpose_batch(
                    q_fm, [mcl * 1024 + (q4 * 4 + j) * 128 for j in range(4)],
                    ("vmain", g, mcl), pref_done,
                    lambda q4=q4, mcl=mcl: v_tm.ap().rearrange(
                        "p (tc j) -> p tc j", j=512)[:, q4 * 4:(q4 + 1) * 4,
                                                     mcl * 128:(mcl + 1) * 128],
                    "vtm", ident_b)
        for tc in range(8):
            i = p.add("vector", lambda v, tc=tc: v.tensor_tensor(
                k_tm[:, tc * 512:(tc + 1) * 512], k_tm[:, tc * 512:(tc + 1) * 512],
                kdTg[:, (tc % 2) * 512:((tc % 2) + 1) * 512], ALU.mult),
                deps=p.rdep("ktm", "kdTg") + p.wdep(("kkd", tc)))
            p.note_r(i, "kdTg")
            p.note_w(i, "ktm")
            p.note_w(i, ("kkd", tc))
        for mcl in range(4):
            mc = g * 4 + mcl
            slab, skey = ring_load(wq_h[mc, :, :])
            project(slab, skey, R1, "R1", q_fm, ("vmain", g, mcl), mcl * 1024)
        def emit_aw(c, blk):
            par = (c * 4 + blk) % 2
            AWP, awk = (PA, "PA") if par == 0 else (PD, "PD")
            awm = []
            for hs in range(2):
                for cj in range(2):
                    b0 = hs * 64
                    tp = (64, 0) if hs == 1 else None
                    i = p.add("tensor", lambda t, c=c, blk=blk, hs=hs, cj=cj, b0=b0,
                              tp=tp, AWP=AWP: t.matmul(
                        AWP[:, hs * 512 + cj * 256:hs * 512 + (cj + 1) * 256],
                        k_fm[b0:b0 + 64, c * 1024 + blk * 256 + cj * 128:
                             c * 1024 + blk * 256 + (cj + 1) * 128],
                        q_fm[b0:b0 + 64, c * 1024 + blk * 256:c * 1024 + (blk + 1) * 256],
                        start=True, stop=True, tile_position=tp),
                        deps=(p.rdep(("kf", g, c), ("vmain", g, c)) + p.wdep(awk))
                             if (hs == 0 and cj == 0) else ())
                    awm.append(i)
            p.note_r(awm[-1], ("kf", g, c), ("vmain", g, c))
            p.note_w(awm[-1], awk)
            return awm[-1]

        for c in range(4):
            ghp = g * 4 + c
            hA, hB = 2 * ghp, 2 * ghp + 1
            do = (c % 2) * 1024
            dA = p.add("scalar", lambda s, sl=neg_slope[hA], do=do: s.activation(
                ddT[:, do:do + 512], Darg[:, :], AF.Exp, scale=sl),
                deps=p.rdep("Darg") + p.wdep(("ddT", c % 2)))
            p.note_w(dA, ("ddT", c % 2))
            dB = p.add("scalar", lambda s, sl=neg_slope[hB], do=do: s.activation(
                ddT[:, do + 512:do + 1024], Darg[:, :], AF.Exp, scale=sl),
                deps=p.wdep(("ddT", c % 2)))
            p.note_w(dB, ("ddT", c % 2))
            qq = []
            for i4 in range(4):
                i = p.add("vector", lambda v, i4=i4, c=c, do=do: v.tensor_tensor(
                    qqd[:, do + i4 * 256:do + (i4 + 1) * 256],
                    q_fm[:, c * 1024 + i4 * 256:c * 1024 + (i4 + 1) * 256],
                    qdFg[:, c * 256:(c + 1) * 256], ALU.mult),
                    deps=p.rdep(("vmain", g, c), "qdFg") +
                         (p.wdep(("qqd", c % 2)) if i4 == 0 else ()))
                qq.append(i)
            p.note_r(qq[0], ("vmain", g, c))
            p.note_r(qq[-1], "qdFg")
            p.note_w(qq[-1], ("qqd", c % 2))
            aw_last = emit_aw(c, 0)
            for blk in range(4):
                par = (c * 4 + blk) % 2
                AWP, awk = (PA, "PA") if par == 0 else (PD, "PD")
                IOP, iok = (PC, "PC") if par == 0 else (PE_, "PE")
                sbase = 256 if par == 0 else 512
                ao = par * 1024
                ev = p.add("vector", lambda v, AWP=AWP, ao=ao, do=do: v.tensor_tensor(
                    awT[:, ao:ao + 1024], AWP[:, :], ddT[:, do:do + 1024], ALU.mult),
                    deps=(aw_last, dA, dB) + p.wdep(("awT", par)))
                p.note_r(ev, awk)
                p.note_r(ev, ("ddT", c % 2))
                p.note_w(ev, ("awT", par))
                if blk < 3:
                    aw_last = emit_aw(c, blk + 1)
                last_io = None
                for hs in range(2):
                    b0 = hs * 64
                    tpc = (0, 64) if hs == 1 else None
                    for cj in range(2):
                        i = p.add("tensor", lambda t, c=c, blk=blk, hs=hs, cj=cj, b0=b0,
                                  tpc=tpc, IOP=IOP, ao=ao: t.matmul(
                            IOP[b0:b0 + 64, 0:256],
                            v_tm[:, (2 * blk + cj) * 512 + c * 128 + b0:
                                 (2 * blk + cj) * 512 + c * 128 + b0 + 64],
                            awT[:, ao + hs * 512 + cj * 256:ao + hs * 512 + (cj + 1) * 256],
                            start=(cj == 0), stop=False, tile_position=tpc),
                            deps=(ev,) + ((p.rdep("vtm") + p.wdep(iok))
                                          if (hs == 0 and cj == 0) else ()))
                        last_io = i
                    tpi = (64, 64) if hs == 1 else None
                    i = p.add("tensor", lambda t, c=c, blk=blk, hs=hs, b0=b0, tpi=tpi,
                              IOP=IOP, do=do: t.matmul(
                        IOP[b0:b0 + 64, 0:256],
                        S_bf[b0:b0 + 64, c * 128 + b0:c * 128 + b0 + 64],
                        qqd[b0:b0 + 64, do + blk * 256:do + (blk + 1) * 256],
                        start=False, stop=True, tile_position=tpi),
                        deps=p.rdep(("Sbf", c), ("qqd", c % 2)))
                    last_io = i
                p.note_r(last_io, "vtm", ("awT", par), ("Sbf", c), ("qqd", c % 2))
                p.note_w(last_io, iok)
                ae = p.add("scalar", lambda s, ghp=ghp, blk=blk, IOP=IOP: s.activation(
                    R3[:, ghp * 1024 + blk * 256:ghp * 1024 + (blk + 1) * 256],
                    IOP[:, 0:256], AF.Identity),
                    deps=(last_io,) + p.wdep("R3a"))
                p.note_r(ae, iok)
                p.note_w(ae, "R3a")
                mm1 = p.add("tensor", lambda t, blk=blk, c=c, IOP=IOP, sbase=sbase: t.matmul(
                    IOP[:, sbase:sbase + 128],
                    k_tm[:, (2 * blk) * 512 + c * 128:(2 * blk) * 512 + (c + 1) * 128],
                    v_tm[:, (2 * blk) * 512 + c * 128:(2 * blk) * 512 + (c + 1) * 128],
                    start=True, stop=False),
                    deps=p.rdep("ktm", "vtm") + p.wdep(iok))
                mm2 = p.add("tensor", lambda t, blk=blk, c=c, IOP=IOP, sbase=sbase: t.matmul(
                    IOP[:, sbase:sbase + 128],
                    k_tm[:, (2 * blk + 1) * 512 + c * 128:(2 * blk + 1) * 512 + (c + 1) * 128],
                    v_tm[:, (2 * blk + 1) * 512 + c * 128:(2 * blk + 1) * 512 + (c + 1) * 128],
                    start=False, stop=True))
                p.note_r(mm2, "ktm", "vtm")
                p.note_w(mm2, iok)
                if blk < 3:
                    dec = p.add("vector", lambda v, c=c, ghp=ghp: v.tensor_scalar_mul(
                        S_sb[:, c * 128:(c + 1) * 128], S_sb[:, c * 128:(c + 1) * 128],
                        bdv[:, ghp:ghp + 1]), deps=p.rdep("bdv") + p.wdep(("Sc", c)))
                    p.note_w(dec, ("Sc", c))
                    ad = p.add("vector", lambda v, c=c, IOP=IOP, sbase=sbase: v.tensor_tensor(
                        S_sb[:, c * 128:(c + 1) * 128], S_sb[:, c * 128:(c + 1) * 128],
                        IOP[:, sbase:sbase + 128], ALU.add), deps=(mm2,) + p.wdep(("Sc", c)))
                    p.note_r(ad, iok)
                    p.note_w(ad, ("Sc", c))
                    cst = p.add("scalar", lambda s, c=c: s.activation(
                        S_bf[:, c * 128:(c + 1) * 128], S_sb[:, c * 128:(c + 1) * 128],
                        AF.Identity), deps=p.rdep(("Sc", c)) + p.wdep(("Sbf", c)))
                    p.note_r(cst, ("Sc", c))
                    p.note_w(cst, ("Sbf", c))
                else:
                    ad = p.add("vector", lambda v, IOP=IOP, sbase=sbase: v.tensor_copy(
                        statc[:, 0:1], IOP[:, sbase:sbase + 1]), deps=(mm2,))
                    p.note_r(ad, iok)

    for g_ in reversed(bph):
        g_.__exit__(None, None, None)

    cph = []

    def sb_c(name, shp, dt):
        g = nc.sbuf_tensor(name, shp, dt)
        h = g.__enter__()
        cph.append(g)
        return h

    acc = sb_c("acc", [128, 16384], F32)
    rowb = sb_c("rowb", [65, 2048], F32)
    probsTb = sb_c("probsTb", [7, 1024], BF16)
    bc1 = sb_c("bc1", [128, 1024], BF16)
    bcB = sb_c("bcB", [128, 1024], BF16)
    tmpxb = sb_c("tmpxb", [128, 1024], BF16)
    bdnT = sb_c("bdnT", [7, 2048], BF16)
    i = p.add("gpsimd", lambda g: g.dma_start(out=bdnT[:, :], in_=bdnT_h))
    p.note_w(i, "bdnT")

    for mc in range(16):
        sq = p.add("scalar", lambda s, mc=mc: s.activation(
            tmpxb[:, :], R3[:, mc * 1024:(mc + 1) * 1024], AF.Square),
            deps=p.rdep("R3a") + p.wdep("tmpxb"))
        p.note_r(sq, "R3a")
        p.note_w(sq, "tmpxb")
        p.note_w(sq, ("sqd", mc))
        for hh in range(2):
            i = p.add("tensor", lambda t, hh=hh, mc=mc: t.matmul(
                PA[0:1, hh * 512:(hh + 1) * 512], onesb[:, 0:1],
                tmpxb[:, hh * 512:(hh + 1) * 512],
                start=(mc == 0), stop=(mc == 15)),
                deps=(sq,) + ((p.rdep("onesb") + p.wdep("PA")) if mc == 0 and hh == 0 else ()))
        p.note_r(i, "tmpxb")
    p.note_w(i, "PA")
    i = p.add("scalar", lambda s: s.activation(rowb[64:65, 0:1024], PA[0:1, :], AF.Sqrt,
                                               scale=1.0 / 2048.0, bias=epsc[0:1, 0:1]),
              deps=p.rdep("PA", "epsc") + p.wdep("row1"))
    p.note_r(i, "PA")
    p.note_w(i, "row1")
    i = p.add("vector", lambda v: v.reciprocal(rowb[0:1, 0:1024], rowb[64:65, 0:1024]),
              deps=p.rdep("row1") + p.wdep("row0"))
    p.note_w(i, "row0")
    for hh in range(2):
        i = p.add("tensor", lambda t, hh=hh: t.matmul(
            PA[:, hh * 512:(hh + 1) * 512], onesrf[0:1, :], rowb[0:1, hh * 512:(hh + 1) * 512],
            start=True, stop=True),
            deps=p.rdep("row0", "onesrf") + (p.wdep("PA") if hh == 0 else ()))
    p.note_w(i, "PA")
    i2 = p.add("vector", lambda v: v.tensor_copy(bc1[:, :], PA[:, :]),
               deps=(i,) + p.wdep("bc1"))
    p.note_r(i2, "PA")
    p.note_w(i2, "bc1")

    for mc in range(16):
        slab, skey = ring_load(wg_h[mc, :, :])
        pkey, P_t = next_psum()
        mm_group(P_t, pkey, slab, skey, R1, "R1")
        sg = p.add("scalar", lambda s, P=P_t: s.activation(bcB[:, :], P[:, :], AF.Sigmoid),
                   deps=p.rdep(pkey) + p.wdep("bcB"))
        p.note_r(sg, pkey)
        p.note_w(sg, "bcB")
        v3 = p.add("vector", lambda v, mc=mc: v.tensor_tensor(
            R3[:, mc * 1024:(mc + 1) * 1024], R3[:, mc * 1024:(mc + 1) * 1024],
            bcB[:, :], ALU.mult),
            deps=(sg,) + p.rdep(("sqd", mc)) + p.wdep("R3a") + p.rdep("R3a"))
        p.note_r(v3, "bcB")
        p.note_w(v3, "R3a")
        v1 = p.add("vector", lambda v, mc=mc: v.tensor_scalar_mul(
            R3[:, mc * 1024:(mc + 1) * 1024], R3[:, mc * 1024:(mc + 1) * 1024],
            nwP[:, mc:mc + 1]),
            deps=(v3,) + p.rdep("nwP"))
        v2 = p.add("vector", lambda v, mc=mc: v.tensor_tensor(
            R3[:, mc * 1024:(mc + 1) * 1024], R3[:, mc * 1024:(mc + 1) * 1024],
            bc1[:, :], ALU.mult), deps=(v1,) + p.rdep("bc1"))
        p.note_w(v2, ("ga", mc))
        p.note_w(v2, "R3a")

    for mc in range(16):
        slab, skey = ring_load(wo_h[mc, :, :])
        pkey, P_t = next_psum()
        mm_group(P_t, pkey, slab, skey, R3, "R3a", kc_keys=lambda kc: ("ga", kc))
        ev = p.add("vector", lambda v, P=P_t, mc=mc: v.tensor_copy(
            acc[:, mc * 1024:(mc + 1) * 1024], P[:, :]),
            deps=p.rdep(pkey) + p.wdep(("acc", mc)))
        p.note_r(ev, pkey)
        p.note_w(ev, ("acc", mc))

    for tc in range(8):
        k = tc % 2
        ld = p.add("sync", lambda s, k=k, tc=tc: s.dma_start(
            out=xstage[k][:, :], in_=xm_h[tc * 128:(tc + 1) * 128, :]),
            deps=p.wdep(("xs", k)))
        p.note_w(ld, ("xs", k))
        for q4 in range(4):
            tids = []
            for j in range(4):
                fc = q4 * 4 + j
                ti = p.add("tensor", lambda t, k=k, fc=fc, j=j: t.transpose(
                    PC[:, j * 128:(j + 1) * 128], xstage[k][:, fc * 128:(fc + 1) * 128],
                    ident_f[:, :]),
                    deps=((ld,) + p.rdep("identf") + p.wdep("PC")) if j == 0 else ())
                tids.append(ti)
            p.note_r(tids[-1], ("xs", k))
            p.note_w(tids[-1], "PC")
            av = p.add("vector", lambda v, q4=q4, tc=tc: v.tensor_tensor(
                acc.ap().rearrange("p (fc t) -> p fc t", t=1024)[:, q4 * 4:(q4 + 1) * 4,
                                                                 tc * 128:(tc + 1) * 128],
                acc.ap().rearrange("p (fc t) -> p fc t", t=1024)[:, q4 * 4:(q4 + 1) * 4,
                                                                 tc * 128:(tc + 1) * 128],
                PC[:, 0:512].rearrange("p (a f) -> p a f", f=128), ALU.add),
                deps=(tids[-1],) + p.rdep(*[("acc", q4 * 4 + j) for j in range(4)]))
            p.note_r(av, "PC")
            for j in range(4):
                p.note_w(av, ("acc", q4 * 4 + j))

    acc_all = p.rdep(*[("acc", mc) for mc in range(16)])

    for pass_i, (func, dst_row) in enumerate([(AF.Square, 2), (AF.Identity, 1)]):
        for mc in range(16):
            sqb, sqk = (tmpxb, "tmpxb") if mc % 2 == 0 else (bc1, "bc1")
            cv = p.add("scalar", lambda s, mc=mc, func=func, sqb=sqb: s.activation(
                sqb[:, :], acc[:, mc * 1024:(mc + 1) * 1024], func),
                deps=acc_all + p.wdep(sqk) + p.rdep(("acc", mc)))
            p.note_r(cv, ("acc", mc))
            p.note_w(cv, sqk)
            for hh in range(2):
                i = p.add("tensor", lambda t, hh=hh, mc=mc, sqb=sqb: t.matmul(
                    PA[0:1, hh * 512:(hh + 1) * 512], onesb[:, 0:1],
                    sqb[:, hh * 512:(hh + 1) * 512],
                    start=(mc == 0), stop=(mc == 15)),
                    deps=(cv,) + (p.wdep("PA") if mc == 0 and hh == 0 else ()))
            p.note_r(i, sqk)
        p.note_w(i, "PA")
        dst_ap = (rowb[64:65, 0:1024] if dst_row == 2 else rowb[0:1, 1024:2048])
        i = p.add("scalar", lambda s, dst_ap=dst_ap: s.activation(
            dst_ap, PA[0:1, :], AF.Identity, scale=1.0 / 2048.0),
            deps=p.rdep("PA") + p.wdep(f"rowL{dst_row}"))
        p.note_r(i, "PA")
        p.note_w(i, f"rowL{dst_row}")
    i = p.add("vector", lambda v: v.tensor_mul(rowb[64:65, 1024:2048], rowb[0:1, 1024:2048],
                                               rowb[0:1, 1024:2048]),
              deps=p.rdep("rowL1") + p.wdep("rowL3"))
    p.note_w(i, "rowL3")
    i = p.add("vector", lambda v: v.tensor_sub(rowb[64:65, 0:1024], rowb[64:65, 0:1024],
                                               rowb[64:65, 1024:2048]),
              deps=p.rdep("rowL2", "rowL3") + p.wdep("rowL4"))
    p.note_w(i, "rowL4")
    i = p.add("scalar", lambda s: s.activation(rowb[32:33, 1024:2048], rowb[64:65, 0:1024],
                                               AF.Sqrt, bias=epsc[64:65, 0:1]),
              deps=p.rdep("rowL4", "epsc") + p.wdep("rowL3s"))
    p.note_w(i, "rowL3s")
    i = p.add("vector", lambda v: v.reciprocal(rowb[0:1, 0:1024], rowb[32:33, 1024:2048]),
              deps=p.rdep("rowL3s") + p.wdep("rowL0"))
    p.note_w(i, "rowL0")
    i = p.add("vector", lambda v: v.scalar_tensor_tensor(
        rowb[32:33, 0:1024], rowb[0:1, 1024:2048], -1.0, rowb[0:1, 0:1024],
        ALU.mult, ALU.mult),
        deps=p.rdep("rowL0", "rowL1") + p.wdep("rowL32"))
    p.note_w(i, "rowL32")
    for src_row, dst, dkey in [(0, bc1, "bc1"), (32, bcB, "bcB")]:
        for hh in range(2):
            i = p.add("tensor", lambda t, hh=hh, src_row=src_row: t.matmul(
                PA[:, hh * 512:(hh + 1) * 512], onesrf[src_row:src_row + 1, :],
                rowb[src_row:src_row + 1, hh * 512:hh * 512 + 512], start=True, stop=True),
                deps=p.rdep(f"rowL{src_row}", "onesrf") + (p.wdep("PA") if hh == 0 else ()))
            p.note_r(i, f"rowL{src_row}")
        p.note_w(i, "PA")
        i2 = p.add("vector", lambda v, dst=dst: v.tensor_copy(dst[:, :], PA[:, :]),
                   deps=(i,) + p.wdep(dkey))
        p.note_r(i2, "PA")
        p.note_w(i2, dkey)
    for mc in range(16):
        m1 = p.add("vector", lambda v, mc=mc: v.tensor_tensor(
            R1[:, mc * 1024:(mc + 1) * 1024], acc[:, mc * 1024:(mc + 1) * 1024],
            bc1[:, :], ALU.mult),
            deps=p.rdep(("acc", mc), "bc1") + p.wdep("R1"))
        p.note_r(m1, ("acc", mc), "bc1")
        p.note_w(m1, "R1")
        m2 = p.add("vector", lambda v, mc=mc: v.tensor_tensor(
            R1[:, mc * 1024:(mc + 1) * 1024], R1[:, mc * 1024:(mc + 1) * 1024],
            bcB[:, :], ALU.add),
            deps=(m1,) + p.rdep("bcB"))
        p.note_r(m2, "bcB")
        p.note_w(m2, ("ln2", mc))
        p.note_w(m2, "R1")
    ln2_all = p.rdep(*[("ln2", mc) for mc in range(16)])

    for kc in range(16):
        for hh in range(2):
            i = p.add("tensor", lambda t, kc=kc, hh=hh: t.matmul(
                PD[0:7, hh * 512:(hh + 1) * 512], wrS[:, kc * 7:(kc + 1) * 7],
                R1[:, kc * 1024 + hh * 512:kc * 1024 + (hh + 1) * 512],
                start=(kc == 0), stop=(kc == 15)),
                deps=(p.rdep(("ln2", kc)) + ((p.rdep("wrS") + p.wdep("PD"))
                      if kc == 0 and hh == 0 else ())) if hh == 0 else ())
    p.note_r(i, "R1")
    p.note_w(i, "PD")
    i = p.add("scalar", lambda s: s.activation(rowb[0:7, 0:1024], PD[0:7, :], AF.Identity,
                                               bias=brS[0:7, 0:1]),
              deps=p.rdep("PD", "brS") + p.wdep("rowG", "rowL0", "rowL1", "rowL2",
                                               "rowL3", "rowL3s", "rowL4", "row0", "row1",
                                               "rowL32"))
    p.note_r(i, "PD")
    p.note_w(i, "rowG")
    for tc in range(8):
        tg = p.add("tensor", lambda t, tc=tc: t.transpose(
            PC[:, 0:7], rowb[0:7, tc * 128:(tc + 1) * 128], ident_f[0:7, 0:7]),
            deps=p.rdep("rowG", "identf") + p.wdep("PC"))
        p.note_w(tg, "PC")
        ge = p.add("vector", lambda v: v.tensor_copy(routb[:, 0:7], PC[:, 0:7]),
                   deps=(tg,) + p.wdep("rt"))
        p.note_r(ge, "PC")
        p.note_w(ge, "rt")
        bg = p.add("vector", lambda v: v.tensor_tensor(routb[:, 8:15], routb[:, 0:7],
                                                       gb_bc[:, :], ALU.add),
                   deps=(ge,) + p.rdep("gb_bc"))
        m1 = p.add("vector", lambda v: v.reduce_max(routb[:, 16:17], routb[:, 8:15],
                                                    axis=mybir.AxisListType.X), deps=(bg,))
        e1 = p.add("vector", lambda v: v.tensor_single_scalar(
            routb[:, 17:24], routb[:, 8:15], routb[:, 16:17], ALU.is_equal),
            self_deps=(m1,))
        mk = p.add("vector", lambda v: v.scalar_tensor_tensor(
            routb[:, 24:31], routb[:, 17:24], -1e30, routb[:, 8:15], ALU.mult, ALU.add),
            deps=(e1,))
        m2 = p.add("vector", lambda v: v.reduce_max(routb[:, 31:32], routb[:, 24:31],
                                                    axis=mybir.AxisListType.X), deps=(mk,))
        e2 = p.add("vector", lambda v: v.tensor_single_scalar(
            routb[:, 32:39], routb[:, 24:31], routb[:, 31:32], ALU.is_equal),
            self_deps=(m2,))
        t1 = p.add("vector", lambda v: v.tensor_tensor(routb[:, 39:46], routb[:, 0:7],
                                                       routb[:, 17:24], ALU.mult), deps=(e1, ge))
        g1 = p.add("vector", lambda v: v.reduce_sum(routb[:, 46:47], routb[:, 39:46],
                                                    axis=mybir.AxisListType.X), deps=(t1,))
        t2 = p.add("vector", lambda v: v.tensor_tensor(routb[:, 39:46], routb[:, 0:7],
                                                       routb[:, 32:39], ALU.mult), deps=(e2, g1))
        g2 = p.add("vector", lambda v: v.reduce_sum(routb[:, 47:48], routb[:, 39:46],
                                                    axis=mybir.AxisListType.X), deps=(t2,))
        d12 = p.add("vector", lambda v: v.tensor_sub(routb[:, 16:17], routb[:, 46:47],
                                                     routb[:, 47:48]), deps=(g1, g2, e1))
        p1 = p.add("scalar", lambda s: s.activation(routb[:, 31:32], routb[:, 16:17],
                                                    AF.Sigmoid), deps=(d12, e2))
        p2 = p.add("scalar", lambda s: s.activation(routb[:, 24:25], routb[:, 16:17],
                                                    AF.Sigmoid, scale=-1.0), deps=(d12, mk))
        pr = p.add("vector", lambda v: v.tensor_scalar_mul(
            routb[:, 39:46], routb[:, 32:39], routb[:, 24:25]), deps=(p2, t2))
        pr2 = p.add("vector", lambda v: v.scalar_tensor_tensor(
            routb[:, 39:46], routb[:, 17:24], routb[:, 31:32], routb[:, 39:46],
            ALU.mult, ALU.add), deps=(pr, p1))
        p.note_w(pr2, "probs")
        tp_ = p.add("tensor", lambda t: t.transpose(
            PC[0:7, 0:128], routb[:, 39:46], ident_f[:, :]),
            deps=(pr2,) + p.wdep("PC"))
        p.note_w(tp_, "PC")
        pe = p.add("vector", lambda v, tc=tc: v.tensor_copy(
            probsTb[0:7, tc * 128:(tc + 1) * 128], PC[0:7, 0:128]),
            deps=(tp_,) + p.wdep("probsTb"))
        p.note_r(pe, "PC")
        p.note_w(pe, "probsTb")

    def expert_chain(e, up_h, gt_h, dn_h, bup_col, bgt_col, use_probs, bsd_col=None,
                     add_bdown=False):
        if use_probs:
            ro = 1024 * (e % 2)
            dk = ("prow", e % 2)
            dm = p.add("gpsimd", lambda g, e=e, ro=ro: g.dma_start(
                out=rowb[0:1, ro:ro + 1024], in_=probsTb.ap()[e:e + 1, 0:1024]),
                deps=p.rdep("probsTb") + p.wdep(dk, "rowL0", "rowG", "rowL1"))
            p.note_w(dm, dk)
            for hh in range(2):
                i = p.add("tensor", lambda t, hh=hh, ro=ro: t.matmul(
                    PA[:, hh * 512:(hh + 1) * 512], onesrf[0:1, :],
                    rowb[0:1, ro + hh * 512:ro + (hh + 1) * 512], start=True, stop=True),
                    deps=(p.rdep(dk, "onesrf") + p.wdep("PA")) if hh == 0 else ())
            p.note_w(i, "PA")
            be = p.add("vector", lambda v: v.tensor_copy(bc1[:, :], PA[:, :]),
                       deps=(i,) + p.wdep("bc1"))
            p.note_r(be, "PA")
            p.note_w(be, "bc1")
            p.note_r(be, dk)
        for mc in range(16):
            slab, skey = ring_load(up_h[mc, :, :])
            pkey, P_t = next_psum()
            mm_group(P_t, pkey, slab, skey, R1, "R1", kc_keys=lambda kc: ("ln2", kc))
            ev = p.add("scalar", lambda s, P=P_t, mc=mc, bup_col=bup_col: s.activation(
                R2[:, mc * 1024:(mc + 1) * 1024], P[:, :], AF.Identity,
                bias=bupP[:, bup_col + mc:bup_col + mc + 1] if bup_col is not None
                else bsuP[:, mc:mc + 1]),
                deps=p.rdep(pkey) + p.wdep("R2"))
            p.note_r(ev, pkey)
            p.note_w(ev, "R2")
            p.note_w(ev, ("u", mc))
        u_all = p.rdep(*[("u", mc) for mc in range(16)])
        for mc in range(16):
            slab, skey = ring_load(gt_h[mc, :, :])
            pkey, P_t = next_psum()
            mm_group(P_t, pkey, slab, skey, R2, "R2", kc_keys=lambda kc: ("u", kc))
            sv = p.add("scalar", lambda s, P=P_t, mc=mc, bgt_col=bgt_col: s.activation(
                R3[:, mc * 1024:(mc + 1) * 1024], P[:, :], AF.Silu,
                bias=bgtP[:, bgt_col + mc:bgt_col + mc + 1] if bgt_col is not None
                else bsgP[:, mc:mc + 1]),
                deps=p.rdep(pkey) + p.wdep("R3a"))
            p.note_r(sv, pkey)
            p.note_w(sv, "R3a")
            gv = p.add("vector", lambda v, mc=mc: v.tensor_tensor(
                R3[:, mc * 1024:(mc + 1) * 1024], R3[:, mc * 1024:(mc + 1) * 1024],
                R2[:, mc * 1024:(mc + 1) * 1024], ALU.mult),
                deps=(sv,) + p.rdep(("u", mc)))
            p.note_r(gv, "R2")
            last = gv
            if use_probs:
                last = p.add("vector", lambda v, mc=mc: v.tensor_tensor(
                    R3[:, mc * 1024:(mc + 1) * 1024], R3[:, mc * 1024:(mc + 1) * 1024],
                    bc1[:, :], ALU.mult), deps=(gv,) + p.rdep("bc1"))
                p.note_r(last, "bc1")
            p.note_w(last, ("g", mc))
            p.note_w(last, "R3a")
        g_all = p.rdep(*[("g", mc) for mc in range(16)])
        for mc in range(16):
            slab, skey = ring_load(dn_h[mc, :, :])
            pkey, P_t = next_psum()
            first_deps = p.wdep(pkey) + p.rdep(skey, "R3a")
            last = None
            for kc in range(16):
                dd0 = (first_deps if kc == 0 else ()) + p.rdep(("g", kc))
                for hh in range(2):
                    last = p.add("tensor", lambda t, P=P_t, sl=slab, kc=kc, hh=hh:
                                 t.matmul(P[:, hh * 512:(hh + 1) * 512],
                                          sl[:, kc * 128:(kc + 1) * 128],
                                          R3[:, kc * 1024 + hh * 512:kc * 1024 + (hh + 1) * 512],
                                          start=(kc == 0), stop=(kc == 15 and not add_bdown)),
                                 deps=dd0 if hh == 0 else ())
            if add_bdown:
                for hh in range(2):
                    last = p.add("tensor", lambda t, P=P_t, mc=mc, hh=hh: t.matmul(
                        P[:, hh * 512:(hh + 1) * 512], bdnT[0:7, mc * 128:(mc + 1) * 128],
                        probsTb[0:7, hh * 512:(hh + 1) * 512], start=False, stop=True),
                        deps=p.rdep("bdnT", "probsTb"))
            p.note_r(last, skey, "R3a")
            p.note_w(last, pkey)
            if bsd_col is None:
                av = p.add("vector", lambda v, P=P_t, mc=mc: v.tensor_tensor(
                    acc[:, mc * 1024:(mc + 1) * 1024], acc[:, mc * 1024:(mc + 1) * 1024],
                    P[:, :], ALU.add), deps=p.rdep(pkey, ("acc", mc)) + p.wdep(("acc", mc)))
            else:
                av = p.add("vector", lambda v, P=P_t, mc=mc: v.scalar_tensor_tensor(
                    acc[:, mc * 1024:(mc + 1) * 1024], P[:, :], bsdP[:, mc:mc + 1],
                    acc[:, mc * 1024:(mc + 1) * 1024], ALU.add, ALU.add),
                    deps=p.rdep(pkey, ("acc", mc), "bsdP") + p.wdep(("acc", mc)))
            p.note_r(av, pkey)
            p.note_w(av, ("acc", mc))

    for e in range(NE):
        expert_chain(e, wup_h[e], wgt_h[e], wdn_h[e], e * 16, e * 16, True)
    expert_chain(None, wsu_h, wsg_h, wsd_h, None, None, False, bsd_col=0, add_bdown=True)

    for tc in range(8):
        k = tc % 2
        evs = []
        for q4 in range(4):
            tids = []
            for j in range(4):
                fc = q4 * 4 + j
                ti = p.add("tensor", lambda t, fc=fc, j=j, tc=tc: t.transpose(
                    PC[:, j * 128:(j + 1) * 128],
                    acc[:, fc * 1024 + tc * 128:fc * 1024 + (tc + 1) * 128],
                    ident_f[:, :]),
                    deps=(p.rdep(*[("acc", q4 * 4 + jj) for jj in range(4)]) + p.wdep("PC"))
                         if j == 0 else ())
                tids.append(ti)
            for jj in range(4):
                p.note_r(tids[-1], ("acc", q4 * 4 + jj))
            p.note_w(tids[-1], "PC")
            ev = p.add("vector", lambda v, q4=q4, k=k: v.tensor_copy(
                xstage[k][:, q4 * 512:(q4 + 1) * 512], PC[:, 0:512]),
                deps=(tids[-1],) + p.wdep(("xs", k)))
            p.note_r(ev, "PC")
            p.note_w(ev, ("xs", k))
            evs.append(ev)
        st_ = p.add("sync", lambda s, k=k, tc=tc: s.dma_start(
            out=y_h[tc * 128:(tc + 1) * 128, :], in_=xstage[k][:, :]),
            deps=tuple(evs))
        p.note_r(st_, ("xs", k))

    sems = {}
    sem_ctx = []
    for e in Prog.ENGINES:
        if e in Prog.DMA_ENG:
            hs = []
            for s_ in range(4):
                c = nc.semaphore(f"sem_{e}{s_}")
                sem_ctx.append(c)
                hs.append(c.__enter__())
            sems[e] = hs
        else:
            c = nc.semaphore(f"sem_{e}")
            sem_ctx.append(c)
            sems[e] = c.__enter__()
    p.finalize(nc, sems)
    for c in reversed(sem_ctx):
        c.__exit__(None, None, None)
    for _, g_ in reversed(open_ctx):
        try:
            g_.__exit__(None, None, None)
        except Exception:
            pass
    for g_ in reversed(cph):
        g_.__exit__(None, None, None)
    for g_ in reversed(psum_ctx):
        g_.__exit__(None, None, None)
    _nc_cache[0] = nc
    return nc


def _prep_host(inputs):
    def tile_w(Wt):
        K, M = Wt.shape
        return np.ascontiguousarray(
            Wt.reshape(16, 128, M // 128, 128).transpose(2, 1, 0, 3).reshape(M // 128, 128, 2048),
            dtype=np.float32)

    def col_pack(v):
        return np.ascontiguousarray(np.asarray(v, np.float32).reshape(16, 128).T)

    qkv = np.asarray(inputs["L0_attn_qkv_w"], np.float32).reshape(H, 3, D, HID)
    com = {}
    com["wq"] = tile_w(np.ascontiguousarray(qkv[:, 0].reshape(HID, HID).T))
    com["wk"] = tile_w(np.ascontiguousarray(qkv[:, 1].reshape(HID, HID).T))
    com["wv"] = tile_w(np.ascontiguousarray(qkv[:, 2].reshape(HID, HID).T))
    com["wg"] = tile_w(np.ascontiguousarray(np.asarray(inputs["L0_attn_gate_w"], np.float32).T))
    com["wo"] = tile_w(np.ascontiguousarray(np.asarray(inputs["L0_attn_out_w"], np.float32).T))
    Wr = np.asarray(inputs["L0_ffn_Wr"], np.float32)
    wr_t = np.zeros((128, 112), np.float32)
    for kc in range(16):
        wr_t[:, kc * 7:(kc + 1) * 7] = Wr[kc * 128:(kc + 1) * 128, :]
    com["wr"] = wr_t
    com["wup"] = np.stack([tile_w(np.asarray(inputs["L0_ffn_Wup"][e], np.float32)) for e in range(NE)])
    com["wgt"] = np.stack([tile_w(np.asarray(inputs["L0_ffn_Wgate"][e], np.float32)) for e in range(NE)])
    com["wdn"] = np.stack([tile_w(np.asarray(inputs["L0_ffn_Wdown"][e], np.float32)) for e in range(NE)])
    com["wsu"] = tile_w(np.asarray(inputs["L0_ffn_Wsu"], np.float32))
    com["wsg"] = tile_w(np.asarray(inputs["L0_ffn_Wsg"], np.float32))
    com["wsd"] = tile_w(np.asarray(inputs["L0_ffn_Wsd"], np.float32))
    com["bupP"] = np.concatenate([col_pack(np.asarray(inputs["L0_ffn_bup"][e])) for e in range(NE)], 1)
    com["bgtP"] = np.concatenate([col_pack(np.asarray(inputs["L0_ffn_bgate"][e])) for e in range(NE)], 1)
    com["bdnT"] = np.ascontiguousarray(np.asarray(inputs["L0_ffn_bdown"], np.float32))
    com["bsuP"] = col_pack(inputs["L0_ffn_bsu"])
    com["bsgP"] = col_pack(inputs["L0_ffn_bsg"])
    com["bsdP"] = col_pack(inputs["L0_ffn_bsd"])
    com["nwP"] = col_pack(inputs["L0_attn_norm_w"])
    com["br"] = np.ascontiguousarray(np.asarray(inputs["L0_ffn_br"], np.float32).reshape(7, 1))
    com["gb"] = np.ascontiguousarray(np.asarray(inputs["L0_ffn_gate_bias"], np.float32).reshape(1, 7))
    return com


def kernel(**inputs):
    nc = build_nc()
    com = _prep_host(inputs)
    x = np.asarray(inputs["L0_attn_hidden_states"], np.float32)
    zeros = np.zeros((T, HID), np.float32)
    in_maps = []
    for core in range(8):
        b, half = core // 2, core % 2
        m = {"i_" + k: v for k, v in com.items()}
        m["i_xm"] = np.ascontiguousarray(x[b, half * T:(half + 1) * T])
        m["i_xp"] = zeros if half == 0 else np.ascontiguousarray(x[b, 0:T])
        in_maps.append(m)
    res = run_bass_kernel_spmd(nc, in_maps, core_ids=list(range(8)))
    out = np.empty((4, 2048, HID), np.float32)
    for core in range(8):
        b, half = core // 2, core % 2
        out[b, half * T:(half + 1) * T] = res.results[core]["y"]
    return out


# version 17
# speedup vs baseline: 1.1047x; 1.1020x over previous
import numpy as np
import concourse.bass as bass
import concourse.mybir as mybir
from concourse.bass_utils import run_bass_kernel_spmd

F32 = mybir.dt.float32
BF16 = mybir.dt.bfloat16
AF = mybir.ActivationFunctionType
ALU = mybir.AluOpType

H, D, BS, HID, NE = 32, 64, 256, 2048, 7
T = 1024
EPS = 1e-5

_nc_cache = [None]


def _host_consts():
    import ml_dtypes
    base = 1.0 / 2.0 ** (8.0 / H)
    rate = base ** (np.arange(H, dtype=np.float64) + 1.0)
    factor = 1.0 - 0.0 / (H - 1 + 1e-05) + 1e-05
    slope = rate * factor
    r1 = np.arange(1, BS + 1, dtype=np.float64)
    qd = np.exp(-slope[:, None] * r1[None, :])
    kd = np.exp(-slope[:, None] * (BS - r1[None, :]))
    bd = np.exp(-slope * BS)
    qdF = np.zeros((128, 16 * BS), np.float32)
    for hp in range(16):
        for p_ in range(128):
            qdF[p_, hp * BS:(hp + 1) * BS] = qd[2 * hp + p_ // 64]
    kdT = np.zeros((128, 2 * HID), np.float32)
    for par in range(2):
        for m in range(HID):
            kdT[:, par * HID + m] = kd[m // 64, par * 128:(par + 1) * 128]
    Darg = np.zeros((128, 512), np.float32)
    for cj in range(2):
        for p_ in range(128):
            tj = cj * 128 + p_
            row = np.arange(BS) - tj
            Darg[p_, cj * BS:(cj + 1) * BS] = np.where(row >= 0, row, 1e9)
    bdv = np.zeros((128, 16), np.float32)
    for ghp in range(16):
        bdv[0:64, ghp] = bd[2 * ghp]
        bdv[64:128, ghp] = bd[2 * ghp + 1]
    return (qdF.astype(ml_dtypes.bfloat16), kdT.astype(ml_dtypes.bfloat16),
            Darg, bdv, [float(-s) for s in slope])


class Prog:
    ENGINES = ("tensor", "vector", "scalar", "gpsimd", "sync")
    DMA_ENG = {"gpsimd", "sync"}

    def __init__(self):
        self.ops = []
        self.buf = {}
        self._pend = set()

    def add(self, eng, fn, deps=(), self_deps=()):
        dd = tuple(sorted({d for d in deps if d is not None}))
        sd = tuple(sorted({d for d in self_deps if d is not None}))
        self.ops.append([eng, fn, dd, sd])
        i = len(self.ops) - 1
        for n in self._pend:
            self._b(n)["r"].append(i)
        self._pend.clear()
        return i

    def _b(self, n):
        return self.buf.setdefault(n, {"w": [], "r": []})

    def rdep(self, *names):
        out = []
        for n in names:
            out += self._b(n)["w"]
        self._pend.update(names)
        return tuple(out)

    def wdep(self, *names):
        out = []
        for n in names:
            b = self._b(n)
            out += b["w"] + b["r"]
        return tuple(out)

    def note_r(self, i, *names):
        for n in names:
            self._b(n)["r"].append(i)

    def note_w(self, i, *names):
        for n in names:
            self._b(n)["w"].append(i)

    def finalize(self, nc, sems):
        NSUB = 4
        eng_ops = {e: [] for e in self.ENGINES}
        for i, (e, fn, dd, sd) in enumerate(self.ops):
            eng_ops[e].append(i)
        SYNC_SELF = {"vector", "scalar"}
        marked = [self.ops[i][0] in self.DMA_ENG for i in range(len(self.ops))]
        for i, (e, fn, dd, sd) in enumerate(self.ops):
            for d in dd:
                de = self.ops[d][0]
                if de != e or (e in SYNC_SELF and de not in self.DMA_ENG):
                    marked[d] = True
            for d in sd:
                marked[d] = True
        inc_info = [None] * len(self.ops)
        for e in self.ENGINES:
            if e in self.DMA_ENG:
                k = 0
                for i in eng_ops[e]:
                    sub = k % NSUB
                    rank = k // NSUB
                    inc_info[i] = (sub, 16 * (rank + 1))
                    k += 1
            else:
                c = 0
                for i in eng_ops[e]:
                    if marked[i]:
                        c += 1
                    inc_info[i] = (None, c)

        def emit(eng_obj, e):
            last_wait = {}
            dma_k = 0
            for i in eng_ops[e]:
                need = {}
                for d in self.ops[i][2]:
                    de = self.ops[d][0]
                    if de == e and e not in ("vector", "scalar"):
                        continue
                    sub, v = inc_info[d]
                    key = (de, sub)
                    if v > need.get(key, -1):
                        need[key] = v
                for d in self.ops[i][3]:
                    de = self.ops[d][0]
                    sub, v = inc_info[d]
                    key = (de, sub)
                    if v > need.get(key, -1):
                        need[key] = v
                if e in self.DMA_ENG:
                    sub = dma_k % NSUB
                    rank = dma_k // NSUB
                    if rank >= 1:
                        key = (e, sub)
                        v = 16 * rank
                        if v > need.get(key, -1):
                            need[key] = v
                    dma_k += 1
                for (de, sub), v in sorted(need.items(), key=lambda x: (str(x[0][0]), -1 if x[0][1] is None else x[0][1], x[1])):
                    sh = sems[de][sub] if sub is not None else sems[de]
                    kk = (de, sub)
                    if v > last_wait.get(kk, 0):
                        eng_obj.wait_ge(sh, v)
                        last_wait[kk] = v
                inst = self.ops[i][1](eng_obj)
                if e in self.DMA_ENG:
                    sub, v = inc_info[i]
                    inst.then_inc(sems[e][sub], 16)
                elif marked[i]:
                    inst.then_inc(sems[e], 1)
            if e == "sync":
                for sub in range(NSUB):
                    tot = 0
                    k = 0
                    for i in eng_ops[e]:
                        if k % NSUB == sub:
                            tot = inc_info[i][1]
                        k += 1
                    if tot:
                        eng_obj.wait_ge(sems[e][sub], tot)

        with nc.Block() as blk:
            @blk.tensor
            def _(t):
                emit(t, "tensor")

            @blk.vector
            def _(v):
                emit(v, "vector")

            @blk.scalar
            def _(s):
                emit(s, "scalar")

            @blk.gpsimd
            def _(g):
                emit(g, "gpsimd")

            @blk.sync
            def _(s):
                emit(s, "sync")

def build_nc():
    if _nc_cache[0] is not None:
        return _nc_cache[0]
    import ml_dtypes
    qdF_h, kdT_h, Darg_h, bdv_h, neg_slope = _host_consts()
    nc = bass.Bass("TRN2", target_bir_lowering=False)

    def din(name, shape):
        return nc.dram_tensor("i_" + name, shape, F32, kind="ExternalInput").ap()

    xm_h = din("xm", [T, HID])
    xp_h = din("xp", [T, HID])
    wq_h = din("wq", [16, 128, 2048])
    wk_h = din("wk", [16, 128, 2048])
    wv_h = din("wv", [16, 128, 2048])
    wg_h = din("wg", [16, 128, 2048])
    wo_h = din("wo", [16, 128, 2048])
    wr_h = din("wr", [128, 112])
    wup_h = din("wup", [NE, 16, 128, 2048])
    wgt_h = din("wgt", [NE, 16, 128, 2048])
    wdn_h = din("wdn", [NE, 16, 128, 2048])
    wsu_h = din("wsu", [16, 128, 2048])
    wsg_h = din("wsg", [16, 128, 2048])
    wsd_h = din("wsd", [16, 128, 2048])
    bupP_h = din("bupP", [128, 112])
    bgtP_h = din("bgtP", [128, 112])
    bdnT_h = din("bdnT", [7, 2048])
    bsuP_h = din("bsuP", [128, 16])
    bsgP_h = din("bsgP", [128, 16])
    bsdP_h = din("bsdP", [128, 16])
    nwP_h = din("nwP", [128, 16])
    br_h = din("br", [7, 1])
    gb_h = din("gb", [1, 7])
    y_h = nc.dram_tensor("y", [T, HID], F32, kind="ExternalOutput").ap()

    ident_f_h = nc.inline_tensor(np.eye(128, dtype=np.float32), "identf_inl").ap()
    ident_b_h = nc.inline_tensor(np.eye(128, dtype=ml_dtypes.bfloat16), "identb_inl").ap()
    onesb_h = nc.inline_tensor(np.ones((128, 1), dtype=ml_dtypes.bfloat16), "onesb_inl").ap()
    _onesrf = np.zeros((33, 128), dtype=np.float32)
    _onesrf[0, :] = 1.0
    _onesrf[32, :] = 1.0
    onesrf_h = nc.inline_tensor(_onesrf, "onesrf_inl").ap()
    onesrb_h = nc.inline_tensor(np.ones((1, 128), dtype=ml_dtypes.bfloat16), "onesrb_inl").ap()
    qdF_hh = nc.inline_tensor(qdF_h, "qdF_inl").ap()
    kdT_hh = nc.inline_tensor(kdT_h, "kdT_inl").ap()
    Darg_hh = nc.inline_tensor(Darg_h, "Darg_inl").ap()
    bdv_hh = nc.inline_tensor(bdv_h, "bdv_inl").ap()


    p = Prog()
    open_ctx = []

    def sb(name, shp, dt):
        g = nc.sbuf_tensor(name, shp, dt)
        h = g.__enter__()
        open_ctx.append((name, g))
        return h

    psum_ctx = []

    def psum(name, shp, dt=F32):
        g = nc.psum_tensor(name, shp, dt)
        h = g.__enter__()
        psum_ctx.append(g)
        return h

    ident_f = sb("ident_f", [128, 128], F32)
    ident_b = sb("ident_b", [128, 128], BF16)
    onesb = sb("onesb", [128, 1], BF16)
    onesrf = sb("onesrf", [33, 128], F32)
    onesrb = sb("onesrb", [1, 128], BF16)
    nwP = sb("nwP", [128, 16], F32)
    bupP = sb("bupP", [128, 112], F32)
    bgtP = sb("bgtP", [128, 112], F32)
    bsuP = sb("bsuP", [128, 16], F32)
    bsgP = sb("bsgP", [128, 16], F32)
    bsdP = sb("bsdP", [128, 16], F32)
    brS = sb("brS", [7, 1], F32)
    gbS = sb("gbS", [1, 7], F32)
    gb_bc = sb("gb_bc", [128, 7], F32)
    wrS = sb("wrS", [128, 112], BF16)
    statc = sb("statc", [128, 32], F32)
    epsc = sb("epsc", [128, 1], F32)
    routb = sb("routb", [128, 48], F32)
    R1 = sb("R1", [128, 16384], BF16)
    R2 = sb("R2", [128, 16384], BF16)
    R3 = sb("R3", [128, 16384], BF16)
    NRING = 2
    ring = [sb(f"ring{i}", [128, 2048], BF16) for i in range(NRING)]

    PA = psum("PA", [128, 1024])
    PC = psum("PC", [128, 512])
    PD = psum("PD", [128, 1024])
    PE_ = psum("PE", [128, 1024])
    PT = psum("PT", [128, 1024], BF16)

    xstage = [sb(f"xstage{i}", [128, 2048], F32) for i in range(2)]

    bph = []

    def sb_b(name, shp, dt):
        g = nc.sbuf_tensor(name, shp, dt)
        h = g.__enter__()
        bph.append(g)
        return h

    Darg = sb_b("Darg", [128, 512], F32)
    qdFg = sb_b("qdFg", [128, 1024], BF16)
    kdTg = sb_b("kdTg", [128, 1024], BF16)
    bdv = sb_b("bdv", [128, 16], F32)
    S_sb = sb_b("S_sb", [128, 512], F32)
    S_bf = sb_b("S_bf", [128, 512], BF16)
    qqd = sb_b("qqd", [128, 2048], BF16)
    ddT = sb_b("ddT", [128, 2048], BF16)
    awT = sb_b("awT", [128, 2048], BF16)
    q_fm = sb_b("q_fm", [128, 4096], BF16)
    k_fm = sb_b("k_fm", [128, 4096], BF16)
    tmp_fm = sb_b("tmp_fm", [128, 4096], BF16)
    k_tm = sb_b("k_tm", [128, 4096], BF16)
    v_tm = sb_b("v_tm", [128, 4096], BF16)

    for dst, src, key in [(ident_f, ident_f_h, "identf"), (ident_b, ident_b_h, "identb"),
                          (onesb, onesb_h, "onesb"), (onesrf, onesrf_h, "onesrf"),
                          (onesrb, onesrb_h, "onesrb"), (nwP, nwP_h, "nwP"),
                          (bupP, bupP_h, "bupP"), (bgtP, bgtP_h, "bgtP"),
                          (bsuP, bsuP_h, "bsuP"), (bsgP, bsgP_h, "bsgP"),
                          (bsdP, bsdP_h, "bsdP"), (brS, br_h, "brS"),
                          (gbS, gb_h, "gbS"), (Darg, Darg_hh, "Darg"),
                          (bdv, bdv_hh, "bdv")]:
        i = p.add("sync", lambda s, d=dst, so=src: s.dma_start(out=d[:, :], in_=so))
        p.note_w(i, key)
    i = p.add("vector", lambda v: v.memset(epsc[:, 0:1], EPS))
    p.note_w(i, "epsc")
    i = p.add("gpsimd", lambda g: g.dma_start(out=wrS[:, :], in_=wr_h))
    p.note_w(i, "wrS")
    i = p.add("tensor", lambda t: t.matmul(PC[0:128, 0:7], onesrf[0:1, :], gbS[0:1, :],
                                           start=True, stop=True),
              deps=p.rdep("onesrf", "gbS") + p.wdep("PC"))
    p.note_w(i, "PC")
    i2 = p.add("vector", lambda v: v.tensor_copy(gb_bc[:, :], PC[:, 0:7]),
               deps=p.rdep("PC") + p.wdep("gb_bc"))
    p.note_r(i2, "PC")
    p.note_w(i2, "gb_bc")

    ring_i = [0]

    def ring_load(src_ap):
        s = ring_i[0] % NRING
        ring_i[0] += 1
        key = ("ring", s)
        i = p.add("gpsimd", lambda g, d=ring[s], so=src_ap: g.dma_start(out=d[:, :], in_=so),
                  deps=p.wdep(key))
        p.note_w(i, key)
        return ring[s], key

    rot = [0]
    PROJ = [("PD", PD), ("PE", PE_)]

    def next_psum():
        r = PROJ[rot[0] % 2]
        rot[0] += 1
        return r

    def mm_group(P_t, pkey, slab, skey, rhs, rkey, extra=(), kc_keys=None):
        first_deps = p.wdep(pkey) + p.rdep(skey, rkey) + tuple(extra)
        last = None
        for kc in range(16):
            dd = first_deps if kc == 0 else ()
            if kc_keys is not None:
                dd = dd + p.rdep(kc_keys(kc))
            for hh in range(2):
                st = (kc == 0)
                sp = (kc == 15)
                last = p.add("tensor", lambda t, P=P_t, sl=slab, kc=kc, hh=hh, st=st, sp=sp, rb=rhs:
                             t.matmul(P[:, hh * 512:(hh + 1) * 512],
                                      sl[:, kc * 128:(kc + 1) * 128],
                                      rb[:, kc * 1024 + hh * 512:kc * 1024 + (hh + 1) * 512],
                                      start=st, stop=sp),
                             deps=dd if hh == 0 else ())
        p.note_r(last, skey, rkey)
        p.note_w(last, pkey)
        if kc_keys is not None:
            for kc in range(16):
                p.note_r(last, kc_keys(kc))
        return last

    def transpose_batch(src, src_cols, src_key, src_deps, dst_ap_fn, dst_key, ident):
        tids = []
        for j, c0 in enumerate(src_cols):
            ti = p.add("tensor", lambda t, j=j, c0=c0: t.transpose(
                PT[:, j * 128:(j + 1) * 128], src[:, c0:c0 + 128], ident[:, :]),
                deps=(p.wdep("PT") + tuple(src_deps) + p.rdep(src_key)) if j == 0 else ())
            tids.append(ti)
        p.note_r(tids[-1], src_key)
        p.note_w(tids[-1], "PT")
        i2 = p.add("vector", lambda v: v.tensor_copy(
            dst_ap_fn(), PT[:, 0:len(src_cols) * 128].rearrange("p (a f) -> p a f", f=128)),
            deps=(tids[-1],) + p.wdep(dst_key))
        p.note_r(i2, "PT")
        p.note_w(i2, dst_key)
        return i2

    def stage_A(jobs):
        for tc in range(8):
          for k, (x_hbm, dst, dst_key) in enumerate(jobs):
              so = 16 * k
              lo = 2048 * k
              i = p.add("sync", lambda s, k=k, tc=tc, xh=x_hbm: s.dma_start(
                  out=xstage[k][:, :], in_=xh[tc * 128:(tc + 1) * 128, :]),
                  deps=p.wdep(("xs", k)))
              p.note_w(i, ("xs", k))
              i = p.add("vector", lambda v, k=k, so=so: v.reduce_sum(
                  statc[:, so:so + 1], xstage[k][:, :], axis=mybir.AxisListType.X),
                  deps=p.rdep(("xs", k)) + p.wdep(("st_sum", k)))
              p.note_r(i, ("xs", k))
              p.note_w(i, ("st_sum", k))
              i = p.add("scalar", lambda s, k=k, so=so: s.activation(
                  R3[:, 4096 + 2048 * k:6144 + 2048 * k], xstage[k][:, :], AF.Square,
                  accum_out=statc[:, so + 2:so + 3]),
                  deps=p.rdep(("xs", k)) + p.wdep(("st_sq", k), ("R3dump", k)))
              p.note_r(i, ("xs", k))
              p.note_w(i, ("st_sq", k))
              p.note_w(i, ("R3dump", k))
              i = p.add("scalar", lambda s, so=so: s.activation(
                  statc[:, so + 4:so + 5], statc[:, so:so + 1], AF.Identity, scale=1.0 / 2048.0),
                  deps=p.rdep(("st_sum", k)) + p.wdep(("st_m", k)))
              p.note_w(i, ("st_m", k))
              i = p.add("scalar", lambda s, so=so: s.activation(
                  statc[:, so + 6:so + 7], statc[:, so + 2:so + 3], AF.Identity, scale=1.0 / 2048.0),
                  deps=p.rdep(("st_sq", k)) + p.wdep(("st_e2", k)))
              p.note_w(i, ("st_e2", k))
              i = p.add("vector", lambda v, so=so: v.tensor_mul(
                  statc[:, so + 8:so + 9], statc[:, so + 4:so + 5], statc[:, so + 4:so + 5]),
                  deps=p.rdep(("st_m", k)) + p.wdep(("st_msq", k)))
              p.note_w(i, ("st_msq", k))
              i = p.add("vector", lambda v, so=so: v.tensor_sub(
                  statc[:, so + 10:so + 11], statc[:, so + 6:so + 7], statc[:, so + 8:so + 9]),
                  deps=p.rdep(("st_e2", k), ("st_msq", k)) + p.wdep(("st_var", k)))
              p.note_w(i, ("st_var", k))
              i = p.add("scalar", lambda s, so=so: s.activation(
                  statc[:, so + 12:so + 13], statc[:, so + 10:so + 11], AF.Sqrt,
                  bias=epsc[:, 0:1]),
                  deps=p.rdep(("st_var", k), "epsc") + p.wdep(("st_sd", k)))
              p.note_w(i, ("st_sd", k))
              i = p.add("vector", lambda v, so=so: v.reciprocal(
                  statc[:, so + 14:so + 15], statc[:, so + 12:so + 13]),
                  deps=p.rdep(("st_sd", k)) + p.wdep(("st_a", k)))
              p.note_w(i, ("st_a", k))
              i = p.add("vector", lambda v, so=so: v.scalar_tensor_tensor(
                  statc[:, so + 8:so + 9], statc[:, so + 4:so + 5], -1.0,
                  statc[:, so + 14:so + 15], ALU.mult, ALU.mult),
                  deps=p.rdep(("st_a", k), ("st_m", k)) + p.wdep(("st_msq", k)))
              p.note_w(i, ("st_b", k))
              i = p.add("scalar", lambda s, k=k, so=so, lo=lo: s.activation(
                  R3[:, lo:lo + 2048], xstage[k][:, :], AF.Identity,
                  bias=statc[:, so + 8:so + 9], scale=statc[:, so + 14:so + 15]),
                  deps=p.rdep(("xs", k), ("st_a", k), ("st_b", k)) + p.wdep(("lntm", k)))
              p.note_r(i, ("xs", k))
              p.note_w(i, ("lntm", k))
              for q4 in range(4):
                  transpose_batch(
                      R3, [lo + (q4 * 4 + j) * 128 for j in range(4)], ("lntm", k), (),
                      lambda tc=tc, q4=q4, dst=dst: dst.ap().rearrange(
                          "p (fc t) -> p fc t", t=1024)[:, q4 * 4:(q4 + 1) * 4,
                                                        tc * 128:(tc + 1) * 128],
                      dst_key, ident_b)

    stage_A([(xm_h, R1, "R1"), (xp_h, R2, "R2")])

    def project(slab, skey, rhs, rkey, dst, dst_key, dst_col, extra=()):
        pkey, P_t = next_psum()
        mm_group(P_t, pkey, slab, skey, rhs, rkey, extra=extra)
        ev = p.add("scalar", lambda s, P=P_t, d=dst, dc=dst_col: s.activation(
            d[:, dc:dc + 1024], P[:, :], AF.Silu),
            deps=p.rdep(pkey) + p.wdep(dst_key))
        p.note_r(ev, pkey)
        p.note_w(ev, dst_key)
        return ev

    for g in range(4):
        i = p.add("sync", lambda s, g=g: s.dma_start(
            out=qdFg[:, :], in_=qdF_hh[:, g * 1024:(g + 1) * 1024]),
            deps=p.wdep("qdFg"))
        p.note_w(i, "qdFg")
        i = p.add("sync", lambda s, g=g: s.dma_start(
            out=kdTg[:, :].rearrange("p (a m) -> p a m", m=512),
            in_=kdT_hh.rearrange("p (a m) -> p a m", m=2048)[:, :, g * 512:(g + 1) * 512]),
            deps=p.wdep("kdTg"))
        p.note_w(i, "kdTg")

        pend = []
        for mcl in range(4):
            mc = g * 4 + mcl
            slab, skey = ring_load(wk_h[mc, :, :])
            project(slab, skey, R1, "R1", k_fm, ("kf", g, mcl), mcl * 1024)
            evp = project(slab, skey, R2, "R2", tmp_fm, ("tm", mcl), mcl * 1024)
            for th in pend:
                th()
            pend = [lambda q4=q4, mcl=mcl, evp=evp: transpose_batch(
                tmp_fm, [mcl * 1024 + (q4 * 4 + j) * 128 for j in range(4)],
                ("tm", mcl), (evp,),
                lambda q4=q4, mcl=mcl: k_tm.ap().rearrange(
                    "p (tc j) -> p tc j", j=512)[:, q4 * 4:(q4 + 1) * 4,
                                                 mcl * 128:(mcl + 1) * 128],
                "ktm", ident_b) for q4 in range(2)]
        for mcl in range(4):
            mc = g * 4 + mcl
            slab, skey = ring_load(wv_h[mc, :, :])
            project(slab, skey, R1, "R1", q_fm, ("vmain", g, mcl), mcl * 1024)
            evp = project(slab, skey, R2, "R2", tmp_fm, ("tm", mcl), mcl * 1024)
            for th in pend:
                th()
            pend = [lambda q4=q4, mcl=mcl, evp=evp: transpose_batch(
                tmp_fm, [mcl * 1024 + (q4 * 4 + j) * 128 for j in range(4)],
                ("tm", mcl), (evp,),
                lambda q4=q4, mcl=mcl: v_tm.ap().rearrange(
                    "p (tc j) -> p tc j", j=512)[:, q4 * 4:(q4 + 1) * 4,
                                                 mcl * 128:(mcl + 1) * 128],
                "vtm", ident_b) for q4 in range(2)]
        for th in pend:
            th()
        for tc in range(8):
            i = p.add("vector", lambda v, tc=tc: v.tensor_tensor(
                k_tm[:, tc * 512:(tc + 1) * 512], k_tm[:, tc * 512:(tc + 1) * 512],
                kdTg[:, (tc % 2) * 512:((tc % 2) + 1) * 512], ALU.mult),
                deps=p.rdep("ktm", "kdTg") + p.wdep(("kkd", tc)))
            p.note_r(i, "kdTg")
            p.note_w(i, "ktm")
            p.note_w(i, ("kkd", tc))
        i = p.add("vector", lambda v: v.memset(S_sb[:, :], 0.0),
                  deps=p.wdep("S", ("Sc", 0), ("Sc", 1), ("Sc", 2), ("Sc", 3)))
        p.note_w(i, "S", ("Sc", 0), ("Sc", 1), ("Sc", 2), ("Sc", 3))
        for c in range(4):
            ghp = g * 4 + c
            for pb in range(4):
                mm1 = p.add("tensor", lambda t, pb=pb, c=c: t.matmul(
                    PC[:, 256:384],
                    k_tm[:, (2 * pb) * 512 + c * 128:(2 * pb) * 512 + (c + 1) * 128],
                    v_tm[:, (2 * pb) * 512 + c * 128:(2 * pb) * 512 + (c + 1) * 128],
                    start=True, stop=False),
                    deps=p.rdep("ktm", "vtm") + p.wdep("PC"))
                mm2 = p.add("tensor", lambda t, pb=pb, c=c: t.matmul(
                    PC[:, 256:384],
                    k_tm[:, (2 * pb + 1) * 512 + c * 128:(2 * pb + 1) * 512 + (c + 1) * 128],
                    v_tm[:, (2 * pb + 1) * 512 + c * 128:(2 * pb + 1) * 512 + (c + 1) * 128],
                    start=False, stop=True))
                p.note_r(mm2, "ktm", "vtm")
                p.note_w(mm2, "PC")
                dec = p.add("vector", lambda v, c=c, ghp=ghp: v.tensor_scalar_mul(
                    S_sb[:, c * 128:(c + 1) * 128], S_sb[:, c * 128:(c + 1) * 128],
                    bdv[:, ghp:ghp + 1]),
                    deps=p.rdep("bdv") + p.wdep(("Sc", c)))
                p.note_w(dec, ("Sc", c))
                ad = p.add("vector", lambda v, c=c: v.tensor_tensor(
                    S_sb[:, c * 128:(c + 1) * 128], S_sb[:, c * 128:(c + 1) * 128],
                    PC[:, 256:384], ALU.add),
                    deps=(mm2,) + p.wdep(("Sc", c)))
                p.note_r(ad, "PC")
                p.note_w(ad, ("Sc", c))
            cst = p.add("scalar", lambda s, c=c: s.activation(
                S_bf[:, c * 128:(c + 1) * 128], S_sb[:, c * 128:(c + 1) * 128], AF.Identity),
                deps=p.rdep(("Sc", c)) + p.wdep(("Sbf", c)))
            p.note_r(cst, ("Sc", c))
            p.note_w(cst, ("Sbf", c))
        pref_done = tuple(p._b("PC")["r"])
        for mcl in range(4):
            for q4 in range(2):
                transpose_batch(
                    k_fm, [mcl * 1024 + (q4 * 4 + j) * 128 for j in range(4)],
                    ("kf", g, mcl), pref_done,
                    lambda q4=q4, mcl=mcl: k_tm.ap().rearrange(
                        "p (tc j) -> p tc j", j=512)[:, q4 * 4:(q4 + 1) * 4,
                                                     mcl * 128:(mcl + 1) * 128],
                    "ktm", ident_b)
                transpose_batch(
                    q_fm, [mcl * 1024 + (q4 * 4 + j) * 128 for j in range(4)],
                    ("vmain", g, mcl), pref_done,
                    lambda q4=q4, mcl=mcl: v_tm.ap().rearrange(
                        "p (tc j) -> p tc j", j=512)[:, q4 * 4:(q4 + 1) * 4,
                                                     mcl * 128:(mcl + 1) * 128],
                    "vtm", ident_b)
        for tc in range(8):
            i = p.add("vector", lambda v, tc=tc: v.tensor_tensor(
                k_tm[:, tc * 512:(tc + 1) * 512], k_tm[:, tc * 512:(tc + 1) * 512],
                kdTg[:, (tc % 2) * 512:((tc % 2) + 1) * 512], ALU.mult),
                deps=p.rdep("ktm", "kdTg") + p.wdep(("kkd", tc)))
            p.note_r(i, "kdTg")
            p.note_w(i, "ktm")
            p.note_w(i, ("kkd", tc))
        for mcl in range(4):
            mc = g * 4 + mcl
            slab, skey = ring_load(wq_h[mc, :, :])
            project(slab, skey, R1, "R1", q_fm, ("vmain", g, mcl), mcl * 1024)
        def emit_aw(c, blk):
            par = (c * 4 + blk) % 2
            AWP, awk = (PA, "PA") if par == 0 else (PD, "PD")
            awm = []
            for hs in range(2):
                for cj in range(2):
                    b0 = hs * 64
                    tp = (64, 0) if hs == 1 else None
                    i = p.add("tensor", lambda t, c=c, blk=blk, hs=hs, cj=cj, b0=b0,
                              tp=tp, AWP=AWP: t.matmul(
                        AWP[:, hs * 512 + cj * 256:hs * 512 + (cj + 1) * 256],
                        k_fm[b0:b0 + 64, c * 1024 + blk * 256 + cj * 128:
                             c * 1024 + blk * 256 + (cj + 1) * 128],
                        q_fm[b0:b0 + 64, c * 1024 + blk * 256:c * 1024 + (blk + 1) * 256],
                        start=True, stop=True, tile_position=tp),
                        deps=(p.rdep(("kf", g, c), ("vmain", g, c)) + p.wdep(awk))
                             if (hs == 0 and cj == 0) else ())
                    awm.append(i)
            p.note_r(awm[-1], ("kf", g, c), ("vmain", g, c))
            p.note_w(awm[-1], awk)
            return awm[-1]

        for c in range(4):
            ghp = g * 4 + c
            hA, hB = 2 * ghp, 2 * ghp + 1
            do = (c % 2) * 1024
            dA = p.add("scalar", lambda s, sl=neg_slope[hA], do=do: s.activation(
                ddT[:, do:do + 512], Darg[:, :], AF.Exp, scale=sl),
                deps=p.rdep("Darg") + p.wdep(("ddT", c % 2)))
            p.note_w(dA, ("ddT", c % 2))
            dB = p.add("scalar", lambda s, sl=neg_slope[hB], do=do: s.activation(
                ddT[:, do + 512:do + 1024], Darg[:, :], AF.Exp, scale=sl),
                deps=p.wdep(("ddT", c % 2)))
            p.note_w(dB, ("ddT", c % 2))
            qq = []
            for i4 in range(4):
                i = p.add("vector", lambda v, i4=i4, c=c, do=do: v.tensor_tensor(
                    qqd[:, do + i4 * 256:do + (i4 + 1) * 256],
                    q_fm[:, c * 1024 + i4 * 256:c * 1024 + (i4 + 1) * 256],
                    qdFg[:, c * 256:(c + 1) * 256], ALU.mult),
                    deps=p.rdep(("vmain", g, c), "qdFg") +
                         (p.wdep(("qqd", c % 2)) if i4 == 0 else ()))
                qq.append(i)
            p.note_r(qq[0], ("vmain", g, c))
            p.note_r(qq[-1], "qdFg")
            p.note_w(qq[-1], ("qqd", c % 2))
            aw_last = emit_aw(c, 0)
            for blk in range(4):
                par = (c * 4 + blk) % 2
                AWP, awk = (PA, "PA") if par == 0 else (PD, "PD")
                IOP, iok = (PC, "PC") if par == 0 else (PE_, "PE")
                sbase = 256 if par == 0 else 512
                ao = par * 1024
                ev = p.add("vector", lambda v, AWP=AWP, ao=ao, do=do: v.tensor_tensor(
                    awT[:, ao:ao + 1024], AWP[:, :], ddT[:, do:do + 1024], ALU.mult),
                    deps=(aw_last, dA, dB) + p.wdep(("awT", par)))
                p.note_r(ev, awk)
                p.note_r(ev, ("ddT", c % 2))
                p.note_w(ev, ("awT", par))
                if blk < 3:
                    aw_last = emit_aw(c, blk + 1)
                last_io = None
                for hs in range(2):
                    b0 = hs * 64
                    tpc = (0, 64) if hs == 1 else None
                    for cj in range(2):
                        i = p.add("tensor", lambda t, c=c, blk=blk, hs=hs, cj=cj, b0=b0,
                                  tpc=tpc, IOP=IOP, ao=ao: t.matmul(
                            IOP[b0:b0 + 64, 0:256],
                            v_tm[:, (2 * blk + cj) * 512 + c * 128 + b0:
                                 (2 * blk + cj) * 512 + c * 128 + b0 + 64],
                            awT[:, ao + hs * 512 + cj * 256:ao + hs * 512 + (cj + 1) * 256],
                            start=(cj == 0), stop=False, tile_position=tpc),
                            deps=(ev,) + ((p.rdep("vtm") + p.wdep(iok))
                                          if (hs == 0 and cj == 0) else ()))
                        last_io = i
                    tpi = (64, 64) if hs == 1 else None
                    i = p.add("tensor", lambda t, c=c, blk=blk, hs=hs, b0=b0, tpi=tpi,
                              IOP=IOP, do=do: t.matmul(
                        IOP[b0:b0 + 64, 0:256],
                        S_bf[b0:b0 + 64, c * 128 + b0:c * 128 + b0 + 64],
                        qqd[b0:b0 + 64, do + blk * 256:do + (blk + 1) * 256],
                        start=False, stop=True, tile_position=tpi),
                        deps=p.rdep(("Sbf", c), ("qqd", c % 2)))
                    last_io = i
                p.note_r(last_io, "vtm", ("awT", par), ("Sbf", c), ("qqd", c % 2))
                p.note_w(last_io, iok)
                ae = p.add("scalar", lambda s, ghp=ghp, blk=blk, IOP=IOP: s.activation(
                    R3[:, ghp * 1024 + blk * 256:ghp * 1024 + (blk + 1) * 256],
                    IOP[:, 0:256], AF.Identity),
                    deps=(last_io,) + p.wdep("R3a"))
                p.note_r(ae, iok)
                p.note_w(ae, "R3a")
                mm1 = p.add("tensor", lambda t, blk=blk, c=c, IOP=IOP, sbase=sbase: t.matmul(
                    IOP[:, sbase:sbase + 128],
                    k_tm[:, (2 * blk) * 512 + c * 128:(2 * blk) * 512 + (c + 1) * 128],
                    v_tm[:, (2 * blk) * 512 + c * 128:(2 * blk) * 512 + (c + 1) * 128],
                    start=True, stop=False),
                    deps=p.rdep("ktm", "vtm") + p.wdep(iok))
                mm2 = p.add("tensor", lambda t, blk=blk, c=c, IOP=IOP, sbase=sbase: t.matmul(
                    IOP[:, sbase:sbase + 128],
                    k_tm[:, (2 * blk + 1) * 512 + c * 128:(2 * blk + 1) * 512 + (c + 1) * 128],
                    v_tm[:, (2 * blk + 1) * 512 + c * 128:(2 * blk + 1) * 512 + (c + 1) * 128],
                    start=False, stop=True))
                p.note_r(mm2, "ktm", "vtm")
                p.note_w(mm2, iok)
                if blk < 3:
                    dec = p.add("vector", lambda v, c=c, ghp=ghp: v.tensor_scalar_mul(
                        S_sb[:, c * 128:(c + 1) * 128], S_sb[:, c * 128:(c + 1) * 128],
                        bdv[:, ghp:ghp + 1]), deps=p.rdep("bdv") + p.wdep(("Sc", c)))
                    p.note_w(dec, ("Sc", c))
                    ad = p.add("vector", lambda v, c=c, IOP=IOP, sbase=sbase: v.tensor_tensor(
                        S_sb[:, c * 128:(c + 1) * 128], S_sb[:, c * 128:(c + 1) * 128],
                        IOP[:, sbase:sbase + 128], ALU.add), deps=(mm2,) + p.wdep(("Sc", c)))
                    p.note_r(ad, iok)
                    p.note_w(ad, ("Sc", c))
                    cst = p.add("scalar", lambda s, c=c: s.activation(
                        S_bf[:, c * 128:(c + 1) * 128], S_sb[:, c * 128:(c + 1) * 128],
                        AF.Identity), deps=p.rdep(("Sc", c)) + p.wdep(("Sbf", c)))
                    p.note_r(cst, ("Sc", c))
                    p.note_w(cst, ("Sbf", c))
                else:
                    ad = p.add("vector", lambda v, IOP=IOP, sbase=sbase: v.tensor_copy(
                        statc[:, 0:1], IOP[:, sbase:sbase + 1]), deps=(mm2,))
                    p.note_r(ad, iok)

    for g_ in reversed(bph):
        g_.__exit__(None, None, None)

    cph = []

    def sb_c(name, shp, dt):
        g = nc.sbuf_tensor(name, shp, dt)
        h = g.__enter__()
        cph.append(g)
        return h

    acc = sb_c("acc", [128, 16384], F32)
    rowb = sb_c("rowb", [65, 2048], F32)
    probsTb = sb_c("probsTb", [7, 1024], BF16)
    bc1 = sb_c("bc1", [128, 1024], BF16)
    bcB = sb_c("bcB", [128, 1024], BF16)
    tmpxb = sb_c("tmpxb", [128, 1024], BF16)
    bdnT = sb_c("bdnT", [7, 2048], BF16)
    i = p.add("gpsimd", lambda g: g.dma_start(out=bdnT[:, :], in_=bdnT_h))
    p.note_w(i, "bdnT")

    for mc in range(16):
        sq = p.add("scalar", lambda s, mc=mc: s.activation(
            tmpxb[:, :], R3[:, mc * 1024:(mc + 1) * 1024], AF.Square),
            deps=p.rdep("R3a") + p.wdep("tmpxb"))
        p.note_r(sq, "R3a")
        p.note_w(sq, "tmpxb")
        p.note_w(sq, ("sqd", mc))
        for hh in range(2):
            i = p.add("tensor", lambda t, hh=hh, mc=mc: t.matmul(
                PA[0:1, hh * 512:(hh + 1) * 512], onesb[:, 0:1],
                tmpxb[:, hh * 512:(hh + 1) * 512],
                start=(mc == 0), stop=(mc == 15)),
                deps=(sq,) + ((p.rdep("onesb") + p.wdep("PA")) if mc == 0 and hh == 0 else ()))
        p.note_r(i, "tmpxb")
    p.note_w(i, "PA")
    i = p.add("scalar", lambda s: s.activation(rowb[64:65, 0:1024], PA[0:1, :], AF.Sqrt,
                                               scale=1.0 / 2048.0, bias=epsc[0:1, 0:1]),
              deps=p.rdep("PA", "epsc") + p.wdep("row1"))
    p.note_r(i, "PA")
    p.note_w(i, "row1")
    i = p.add("vector", lambda v: v.reciprocal(rowb[0:1, 0:1024], rowb[64:65, 0:1024]),
              deps=p.rdep("row1") + p.wdep("row0"))
    p.note_w(i, "row0")
    for hh in range(2):
        i = p.add("tensor", lambda t, hh=hh: t.matmul(
            PA[:, hh * 512:(hh + 1) * 512], onesrf[0:1, :], rowb[0:1, hh * 512:(hh + 1) * 512],
            start=True, stop=True),
            deps=p.rdep("row0", "onesrf") + (p.wdep("PA") if hh == 0 else ()))
    p.note_w(i, "PA")
    i2 = p.add("vector", lambda v: v.tensor_copy(bc1[:, :], PA[:, :]),
               deps=(i,) + p.wdep("bc1"))
    p.note_r(i2, "PA")
    p.note_w(i2, "bc1")

    for mc in range(16):
        slab, skey = ring_load(wg_h[mc, :, :])
        pkey, P_t = next_psum()
        mm_group(P_t, pkey, slab, skey, R1, "R1")
        sg = p.add("scalar", lambda s, P=P_t: s.activation(bcB[:, :], P[:, :], AF.Sigmoid),
                   deps=p.rdep(pkey) + p.wdep("bcB"))
        p.note_r(sg, pkey)
        p.note_w(sg, "bcB")
        v3 = p.add("vector", lambda v, mc=mc: v.tensor_tensor(
            R3[:, mc * 1024:(mc + 1) * 1024], R3[:, mc * 1024:(mc + 1) * 1024],
            bcB[:, :], ALU.mult),
            deps=(sg,) + p.rdep(("sqd", mc)) + p.wdep("R3a") + p.rdep("R3a"))
        p.note_r(v3, "bcB")
        p.note_w(v3, "R3a")
        v1 = p.add("vector", lambda v, mc=mc: v.tensor_scalar_mul(
            R3[:, mc * 1024:(mc + 1) * 1024], R3[:, mc * 1024:(mc + 1) * 1024],
            nwP[:, mc:mc + 1]),
            deps=(v3,) + p.rdep("nwP"))
        v2 = p.add("vector", lambda v, mc=mc: v.tensor_tensor(
            R3[:, mc * 1024:(mc + 1) * 1024], R3[:, mc * 1024:(mc + 1) * 1024],
            bc1[:, :], ALU.mult), deps=(v1,) + p.rdep("bc1"))
        p.note_w(v2, ("ga", mc))
        p.note_w(v2, "R3a")

    for mc in range(16):
        slab, skey = ring_load(wo_h[mc, :, :])
        pkey, P_t = next_psum()
        mm_group(P_t, pkey, slab, skey, R3, "R3a", kc_keys=lambda kc: ("ga", kc))
        ev = p.add("vector", lambda v, P=P_t, mc=mc: v.tensor_copy(
            acc[:, mc * 1024:(mc + 1) * 1024], P[:, :]),
            deps=p.rdep(pkey) + p.wdep(("acc", mc)))
        p.note_r(ev, pkey)
        p.note_w(ev, ("acc", mc))

    for tc in range(8):
        k = tc % 2
        ld = p.add("sync", lambda s, k=k, tc=tc: s.dma_start(
            out=xstage[k][:, :], in_=xm_h[tc * 128:(tc + 1) * 128, :]),
            deps=p.wdep(("xs", k)))
        p.note_w(ld, ("xs", k))
        for q4 in range(4):
            tids = []
            for j in range(4):
                fc = q4 * 4 + j
                ti = p.add("tensor", lambda t, k=k, fc=fc, j=j: t.transpose(
                    PC[:, j * 128:(j + 1) * 128], xstage[k][:, fc * 128:(fc + 1) * 128],
                    ident_f[:, :]),
                    deps=((ld,) + p.rdep("identf") + p.wdep("PC")) if j == 0 else ())
                tids.append(ti)
            p.note_r(tids[-1], ("xs", k))
            p.note_w(tids[-1], "PC")
            av = p.add("vector", lambda v, q4=q4, tc=tc: v.tensor_tensor(
                acc.ap().rearrange("p (fc t) -> p fc t", t=1024)[:, q4 * 4:(q4 + 1) * 4,
                                                                 tc * 128:(tc + 1) * 128],
                acc.ap().rearrange("p (fc t) -> p fc t", t=1024)[:, q4 * 4:(q4 + 1) * 4,
                                                                 tc * 128:(tc + 1) * 128],
                PC[:, 0:512].rearrange("p (a f) -> p a f", f=128), ALU.add),
                deps=(tids[-1],) + p.rdep(*[("acc", q4 * 4 + j) for j in range(4)]))
            p.note_r(av, "PC")
            for j in range(4):
                p.note_w(av, ("acc", q4 * 4 + j))

    acc_all = p.rdep(*[("acc", mc) for mc in range(16)])

    for pass_i, (func, dst_row) in enumerate([(AF.Square, 2), (AF.Identity, 1)]):
        for mc in range(16):
            sqb, sqk = (tmpxb, "tmpxb") if mc % 2 == 0 else (bc1, "bc1")
            cv = p.add("scalar", lambda s, mc=mc, func=func, sqb=sqb: s.activation(
                sqb[:, :], acc[:, mc * 1024:(mc + 1) * 1024], func),
                deps=acc_all + p.wdep(sqk) + p.rdep(("acc", mc)))
            p.note_r(cv, ("acc", mc))
            p.note_w(cv, sqk)
            for hh in range(2):
                i = p.add("tensor", lambda t, hh=hh, mc=mc, sqb=sqb: t.matmul(
                    PA[0:1, hh * 512:(hh + 1) * 512], onesb[:, 0:1],
                    sqb[:, hh * 512:(hh + 1) * 512],
                    start=(mc == 0), stop=(mc == 15)),
                    deps=(cv,) + (p.wdep("PA") if mc == 0 and hh == 0 else ()))
            p.note_r(i, sqk)
        p.note_w(i, "PA")
        dst_ap = (rowb[64:65, 0:1024] if dst_row == 2 else rowb[0:1, 1024:2048])
        i = p.add("scalar", lambda s, dst_ap=dst_ap: s.activation(
            dst_ap, PA[0:1, :], AF.Identity, scale=1.0 / 2048.0),
            deps=p.rdep("PA") + p.wdep(f"rowL{dst_row}"))
        p.note_r(i, "PA")
        p.note_w(i, f"rowL{dst_row}")
    i = p.add("vector", lambda v: v.tensor_mul(rowb[64:65, 1024:2048], rowb[0:1, 1024:2048],
                                               rowb[0:1, 1024:2048]),
              deps=p.rdep("rowL1") + p.wdep("rowL3"))
    p.note_w(i, "rowL3")
    i = p.add("vector", lambda v: v.tensor_sub(rowb[64:65, 0:1024], rowb[64:65, 0:1024],
                                               rowb[64:65, 1024:2048]),
              deps=p.rdep("rowL2", "rowL3") + p.wdep("rowL4"))
    p.note_w(i, "rowL4")
    i = p.add("scalar", lambda s: s.activation(rowb[32:33, 1024:2048], rowb[64:65, 0:1024],
                                               AF.Sqrt, bias=epsc[64:65, 0:1]),
              deps=p.rdep("rowL4", "epsc") + p.wdep("rowL3s"))
    p.note_w(i, "rowL3s")
    i = p.add("vector", lambda v: v.reciprocal(rowb[0:1, 0:1024], rowb[32:33, 1024:2048]),
              deps=p.rdep("rowL3s") + p.wdep("rowL0"))
    p.note_w(i, "rowL0")
    i = p.add("vector", lambda v: v.scalar_tensor_tensor(
        rowb[32:33, 0:1024], rowb[0:1, 1024:2048], -1.0, rowb[0:1, 0:1024],
        ALU.mult, ALU.mult),
        deps=p.rdep("rowL0", "rowL1") + p.wdep("rowL32"))
    p.note_w(i, "rowL32")
    for src_row, dst, dkey in [(0, bc1, "bc1"), (32, bcB, "bcB")]:
        for hh in range(2):
            i = p.add("tensor", lambda t, hh=hh, src_row=src_row: t.matmul(
                PA[:, hh * 512:(hh + 1) * 512], onesrf[src_row:src_row + 1, :],
                rowb[src_row:src_row + 1, hh * 512:hh * 512 + 512], start=True, stop=True),
                deps=p.rdep(f"rowL{src_row}", "onesrf") + (p.wdep("PA") if hh == 0 else ()))
            p.note_r(i, f"rowL{src_row}")
        p.note_w(i, "PA")
        i2 = p.add("vector", lambda v, dst=dst: v.tensor_copy(dst[:, :], PA[:, :]),
                   deps=(i,) + p.wdep(dkey))
        p.note_r(i2, "PA")
        p.note_w(i2, dkey)
    for mc in range(16):
        m1 = p.add("vector", lambda v, mc=mc: v.tensor_tensor(
            R1[:, mc * 1024:(mc + 1) * 1024], acc[:, mc * 1024:(mc + 1) * 1024],
            bc1[:, :], ALU.mult),
            deps=p.rdep(("acc", mc), "bc1") + p.wdep("R1"))
        p.note_r(m1, ("acc", mc), "bc1")
        p.note_w(m1, "R1")
        m2 = p.add("vector", lambda v, mc=mc: v.tensor_tensor(
            R1[:, mc * 1024:(mc + 1) * 1024], R1[:, mc * 1024:(mc + 1) * 1024],
            bcB[:, :], ALU.add),
            deps=(m1,) + p.rdep("bcB"))
        p.note_r(m2, "bcB")
        p.note_w(m2, ("ln2", mc))
        p.note_w(m2, "R1")
    ln2_all = p.rdep(*[("ln2", mc) for mc in range(16)])

    for kc in range(16):
        for hh in range(2):
            i = p.add("tensor", lambda t, kc=kc, hh=hh: t.matmul(
                PD[0:7, hh * 512:(hh + 1) * 512], wrS[:, kc * 7:(kc + 1) * 7],
                R1[:, kc * 1024 + hh * 512:kc * 1024 + (hh + 1) * 512],
                start=(kc == 0), stop=(kc == 15)),
                deps=(p.rdep(("ln2", kc)) + ((p.rdep("wrS") + p.wdep("PD"))
                      if kc == 0 and hh == 0 else ())) if hh == 0 else ())
    p.note_r(i, "R1")
    p.note_w(i, "PD")
    i = p.add("scalar", lambda s: s.activation(rowb[0:7, 0:1024], PD[0:7, :], AF.Identity,
                                               bias=brS[0:7, 0:1]),
              deps=p.rdep("PD", "brS") + p.wdep("rowG", "rowL0", "rowL1", "rowL2",
                                               "rowL3", "rowL3s", "rowL4", "row0", "row1",
                                               "rowL32"))
    p.note_r(i, "PD")
    p.note_w(i, "rowG")
    for tc in range(8):
        tg = p.add("tensor", lambda t, tc=tc: t.transpose(
            PC[:, 0:7], rowb[0:7, tc * 128:(tc + 1) * 128], ident_f[0:7, 0:7]),
            deps=p.rdep("rowG", "identf") + p.wdep("PC"))
        p.note_w(tg, "PC")
        ge = p.add("vector", lambda v: v.tensor_copy(routb[:, 0:7], PC[:, 0:7]),
                   deps=(tg,) + p.wdep("rt"))
        p.note_r(ge, "PC")
        p.note_w(ge, "rt")
        bg = p.add("vector", lambda v: v.tensor_tensor(routb[:, 8:15], routb[:, 0:7],
                                                       gb_bc[:, :], ALU.add),
                   deps=(ge,) + p.rdep("gb_bc"))
        m1 = p.add("vector", lambda v: v.reduce_max(routb[:, 16:17], routb[:, 8:15],
                                                    axis=mybir.AxisListType.X), deps=(bg,))
        e1 = p.add("vector", lambda v: v.tensor_single_scalar(
            routb[:, 17:24], routb[:, 8:15], routb[:, 16:17], ALU.is_equal),
            self_deps=(m1,))
        mk = p.add("vector", lambda v: v.scalar_tensor_tensor(
            routb[:, 24:31], routb[:, 17:24], -1e30, routb[:, 8:15], ALU.mult, ALU.add),
            deps=(e1,))
        m2 = p.add("vector", lambda v: v.reduce_max(routb[:, 31:32], routb[:, 24:31],
                                                    axis=mybir.AxisListType.X), deps=(mk,))
        e2 = p.add("vector", lambda v: v.tensor_single_scalar(
            routb[:, 32:39], routb[:, 24:31], routb[:, 31:32], ALU.is_equal),
            self_deps=(m2,))
        t1 = p.add("vector", lambda v: v.tensor_tensor(routb[:, 39:46], routb[:, 0:7],
                                                       routb[:, 17:24], ALU.mult), deps=(e1, ge))
        g1 = p.add("vector", lambda v: v.reduce_sum(routb[:, 46:47], routb[:, 39:46],
                                                    axis=mybir.AxisListType.X), deps=(t1,))
        t2 = p.add("vector", lambda v: v.tensor_tensor(routb[:, 39:46], routb[:, 0:7],
                                                       routb[:, 32:39], ALU.mult), deps=(e2, g1))
        g2 = p.add("vector", lambda v: v.reduce_sum(routb[:, 47:48], routb[:, 39:46],
                                                    axis=mybir.AxisListType.X), deps=(t2,))
        d12 = p.add("vector", lambda v: v.tensor_sub(routb[:, 16:17], routb[:, 46:47],
                                                     routb[:, 47:48]), deps=(g1, g2, e1))
        p1 = p.add("scalar", lambda s: s.activation(routb[:, 31:32], routb[:, 16:17],
                                                    AF.Sigmoid), deps=(d12, e2))
        p2 = p.add("scalar", lambda s: s.activation(routb[:, 24:25], routb[:, 16:17],
                                                    AF.Sigmoid, scale=-1.0), deps=(d12, mk))
        pr = p.add("vector", lambda v: v.tensor_scalar_mul(
            routb[:, 39:46], routb[:, 32:39], routb[:, 24:25]), deps=(p2, t2))
        pr2 = p.add("vector", lambda v: v.scalar_tensor_tensor(
            routb[:, 39:46], routb[:, 17:24], routb[:, 31:32], routb[:, 39:46],
            ALU.mult, ALU.add), deps=(pr, p1))
        p.note_w(pr2, "probs")
        tp_ = p.add("tensor", lambda t: t.transpose(
            PC[0:7, 0:128], routb[:, 39:46], ident_f[:, :]),
            deps=(pr2,) + p.wdep("PC"))
        p.note_w(tp_, "PC")
        pe = p.add("vector", lambda v, tc=tc: v.tensor_copy(
            probsTb[0:7, tc * 128:(tc + 1) * 128], PC[0:7, 0:128]),
            deps=(tp_,) + p.wdep("probsTb"))
        p.note_r(pe, "PC")
        p.note_w(pe, "probsTb")

    def expert_chain(e, up_h, gt_h, dn_h, bup_col, bgt_col, use_probs, bsd_col=None,
                     add_bdown=False):
        if use_probs:
            ro = 1024 * (e % 2)
            dk = ("prow", e % 2)
            dm = p.add("gpsimd", lambda g, e=e, ro=ro: g.dma_start(
                out=rowb[0:1, ro:ro + 1024], in_=probsTb.ap()[e:e + 1, 0:1024]),
                deps=p.rdep("probsTb") + p.wdep(dk, "rowL0", "rowG", "rowL1"))
            p.note_w(dm, dk)
            for hh in range(2):
                i = p.add("tensor", lambda t, hh=hh, ro=ro: t.matmul(
                    PA[:, hh * 512:(hh + 1) * 512], onesrf[0:1, :],
                    rowb[0:1, ro + hh * 512:ro + (hh + 1) * 512], start=True, stop=True),
                    deps=(p.rdep(dk, "onesrf") + p.wdep("PA")) if hh == 0 else ())
            p.note_w(i, "PA")
            be = p.add("vector", lambda v: v.tensor_copy(bc1[:, :], PA[:, :]),
                       deps=(i,) + p.wdep("bc1"))
            p.note_r(be, "PA")
            p.note_w(be, "bc1")
            p.note_r(be, dk)
        for mc in range(16):
            slab, skey = ring_load(up_h[mc, :, :])
            pkey, P_t = next_psum()
            mm_group(P_t, pkey, slab, skey, R1, "R1", kc_keys=lambda kc: ("ln2", kc))
            ev = p.add("scalar", lambda s, P=P_t, mc=mc, bup_col=bup_col: s.activation(
                R2[:, mc * 1024:(mc + 1) * 1024], P[:, :], AF.Identity,
                bias=bupP[:, bup_col + mc:bup_col + mc + 1] if bup_col is not None
                else bsuP[:, mc:mc + 1]),
                deps=p.rdep(pkey) + p.wdep("R2"))
            p.note_r(ev, pkey)
            p.note_w(ev, "R2")
            p.note_w(ev, ("u", mc))
        u_all = p.rdep(*[("u", mc) for mc in range(16)])
        for mc in range(16):
            slab, skey = ring_load(gt_h[mc, :, :])
            pkey, P_t = next_psum()
            mm_group(P_t, pkey, slab, skey, R2, "R2", kc_keys=lambda kc: ("u", kc))
            sv = p.add("scalar", lambda s, P=P_t, mc=mc, bgt_col=bgt_col: s.activation(
                R3[:, mc * 1024:(mc + 1) * 1024], P[:, :], AF.Silu,
                bias=bgtP[:, bgt_col + mc:bgt_col + mc + 1] if bgt_col is not None
                else bsgP[:, mc:mc + 1]),
                deps=p.rdep(pkey) + p.wdep("R3a"))
            p.note_r(sv, pkey)
            p.note_w(sv, "R3a")
            gv = p.add("vector", lambda v, mc=mc: v.tensor_tensor(
                R3[:, mc * 1024:(mc + 1) * 1024], R3[:, mc * 1024:(mc + 1) * 1024],
                R2[:, mc * 1024:(mc + 1) * 1024], ALU.mult),
                deps=(sv,) + p.rdep(("u", mc)))
            p.note_r(gv, "R2")
            last = gv
            if use_probs:
                last = p.add("vector", lambda v, mc=mc: v.tensor_tensor(
                    R3[:, mc * 1024:(mc + 1) * 1024], R3[:, mc * 1024:(mc + 1) * 1024],
                    bc1[:, :], ALU.mult), deps=(gv,) + p.rdep("bc1"))
                p.note_r(last, "bc1")
            p.note_w(last, ("g", mc))
            p.note_w(last, "R3a")
        g_all = p.rdep(*[("g", mc) for mc in range(16)])
        for mc in range(16):
            slab, skey = ring_load(dn_h[mc, :, :])
            pkey, P_t = next_psum()
            first_deps = p.wdep(pkey) + p.rdep(skey, "R3a")
            last = None
            for kc in range(16):
                dd0 = (first_deps if kc == 0 else ()) + p.rdep(("g", kc))
                for hh in range(2):
                    last = p.add("tensor", lambda t, P=P_t, sl=slab, kc=kc, hh=hh:
                                 t.matmul(P[:, hh * 512:(hh + 1) * 512],
                                          sl[:, kc * 128:(kc + 1) * 128],
                                          R3[:, kc * 1024 + hh * 512:kc * 1024 + (hh + 1) * 512],
                                          start=(kc == 0), stop=(kc == 15 and not add_bdown)),
                                 deps=dd0 if hh == 0 else ())
            if add_bdown:
                for hh in range(2):
                    last = p.add("tensor", lambda t, P=P_t, mc=mc, hh=hh: t.matmul(
                        P[:, hh * 512:(hh + 1) * 512], bdnT[0:7, mc * 128:(mc + 1) * 128],
                        probsTb[0:7, hh * 512:(hh + 1) * 512], start=False, stop=True),
                        deps=p.rdep("bdnT", "probsTb"))
            p.note_r(last, skey, "R3a")
            p.note_w(last, pkey)
            if bsd_col is None:
                av = p.add("vector", lambda v, P=P_t, mc=mc: v.tensor_tensor(
                    acc[:, mc * 1024:(mc + 1) * 1024], acc[:, mc * 1024:(mc + 1) * 1024],
                    P[:, :], ALU.add), deps=p.rdep(pkey, ("acc", mc)) + p.wdep(("acc", mc)))
            else:
                av = p.add("vector", lambda v, P=P_t, mc=mc: v.scalar_tensor_tensor(
                    acc[:, mc * 1024:(mc + 1) * 1024], P[:, :], bsdP[:, mc:mc + 1],
                    acc[:, mc * 1024:(mc + 1) * 1024], ALU.add, ALU.add),
                    deps=p.rdep(pkey, ("acc", mc), "bsdP") + p.wdep(("acc", mc)))
            p.note_r(av, pkey)
            p.note_w(av, ("acc", mc))

    for e in range(NE):
        expert_chain(e, wup_h[e], wgt_h[e], wdn_h[e], e * 16, e * 16, True)
    expert_chain(None, wsu_h, wsg_h, wsd_h, None, None, False, bsd_col=0, add_bdown=True)

    for tc in range(8):
        k = tc % 2
        evs = []
        for q4 in range(4):
            tids = []
            for j in range(4):
                fc = q4 * 4 + j
                ti = p.add("tensor", lambda t, fc=fc, j=j, tc=tc: t.transpose(
                    PC[:, j * 128:(j + 1) * 128],
                    acc[:, fc * 1024 + tc * 128:fc * 1024 + (tc + 1) * 128],
                    ident_f[:, :]),
                    deps=(p.rdep(*[("acc", q4 * 4 + jj) for jj in range(4)]) + p.wdep("PC"))
                         if j == 0 else ())
                tids.append(ti)
            for jj in range(4):
                p.note_r(tids[-1], ("acc", q4 * 4 + jj))
            p.note_w(tids[-1], "PC")
            ev = p.add("vector", lambda v, q4=q4, k=k: v.tensor_copy(
                xstage[k][:, q4 * 512:(q4 + 1) * 512], PC[:, 0:512]),
                deps=(tids[-1],) + p.wdep(("xs", k)))
            p.note_r(ev, "PC")
            p.note_w(ev, ("xs", k))
            evs.append(ev)
        st_ = p.add("sync", lambda s, k=k, tc=tc: s.dma_start(
            out=y_h[tc * 128:(tc + 1) * 128, :], in_=xstage[k][:, :]),
            deps=tuple(evs))
        p.note_r(st_, ("xs", k))

    sems = {}
    sem_ctx = []
    for e in Prog.ENGINES:
        if e in Prog.DMA_ENG:
            hs = []
            for s_ in range(4):
                c = nc.semaphore(f"sem_{e}{s_}")
                sem_ctx.append(c)
                hs.append(c.__enter__())
            sems[e] = hs
        else:
            c = nc.semaphore(f"sem_{e}")
            sem_ctx.append(c)
            sems[e] = c.__enter__()
    p.finalize(nc, sems)
    for c in reversed(sem_ctx):
        c.__exit__(None, None, None)
    for _, g_ in reversed(open_ctx):
        try:
            g_.__exit__(None, None, None)
        except Exception:
            pass
    for g_ in reversed(cph):
        g_.__exit__(None, None, None)
    for g_ in reversed(psum_ctx):
        g_.__exit__(None, None, None)
    _nc_cache[0] = nc
    return nc


def _prep_host(inputs):
    def tile_w(Wt):
        K, M = Wt.shape
        return np.ascontiguousarray(
            Wt.reshape(16, 128, M // 128, 128).transpose(2, 1, 0, 3).reshape(M // 128, 128, 2048),
            dtype=np.float32)

    def col_pack(v):
        return np.ascontiguousarray(np.asarray(v, np.float32).reshape(16, 128).T)

    qkv = np.asarray(inputs["L0_attn_qkv_w"], np.float32).reshape(H, 3, D, HID)
    com = {}
    com["wq"] = tile_w(np.ascontiguousarray(qkv[:, 0].reshape(HID, HID).T))
    com["wk"] = tile_w(np.ascontiguousarray(qkv[:, 1].reshape(HID, HID).T))
    com["wv"] = tile_w(np.ascontiguousarray(qkv[:, 2].reshape(HID, HID).T))
    com["wg"] = tile_w(np.ascontiguousarray(np.asarray(inputs["L0_attn_gate_w"], np.float32).T))
    com["wo"] = tile_w(np.ascontiguousarray(np.asarray(inputs["L0_attn_out_w"], np.float32).T))
    Wr = np.asarray(inputs["L0_ffn_Wr"], np.float32)
    wr_t = np.zeros((128, 112), np.float32)
    for kc in range(16):
        wr_t[:, kc * 7:(kc + 1) * 7] = Wr[kc * 128:(kc + 1) * 128, :]
    com["wr"] = wr_t
    com["wup"] = np.stack([tile_w(np.asarray(inputs["L0_ffn_Wup"][e], np.float32)) for e in range(NE)])
    com["wgt"] = np.stack([tile_w(np.asarray(inputs["L0_ffn_Wgate"][e], np.float32)) for e in range(NE)])
    com["wdn"] = np.stack([tile_w(np.asarray(inputs["L0_ffn_Wdown"][e], np.float32)) for e in range(NE)])
    com["wsu"] = tile_w(np.asarray(inputs["L0_ffn_Wsu"], np.float32))
    com["wsg"] = tile_w(np.asarray(inputs["L0_ffn_Wsg"], np.float32))
    com["wsd"] = tile_w(np.asarray(inputs["L0_ffn_Wsd"], np.float32))
    com["bupP"] = np.concatenate([col_pack(np.asarray(inputs["L0_ffn_bup"][e])) for e in range(NE)], 1)
    com["bgtP"] = np.concatenate([col_pack(np.asarray(inputs["L0_ffn_bgate"][e])) for e in range(NE)], 1)
    com["bdnT"] = np.ascontiguousarray(np.asarray(inputs["L0_ffn_bdown"], np.float32))
    com["bsuP"] = col_pack(inputs["L0_ffn_bsu"])
    com["bsgP"] = col_pack(inputs["L0_ffn_bsg"])
    com["bsdP"] = col_pack(inputs["L0_ffn_bsd"])
    com["nwP"] = col_pack(inputs["L0_attn_norm_w"])
    com["br"] = np.ascontiguousarray(np.asarray(inputs["L0_ffn_br"], np.float32).reshape(7, 1))
    com["gb"] = np.ascontiguousarray(np.asarray(inputs["L0_ffn_gate_bias"], np.float32).reshape(1, 7))
    return com


def kernel(**inputs):
    nc = build_nc()
    com = _prep_host(inputs)
    x = np.asarray(inputs["L0_attn_hidden_states"], np.float32)
    zeros = np.zeros((T, HID), np.float32)
    in_maps = []
    for core in range(8):
        b, half = core // 2, core % 2
        m = {"i_" + k: v for k, v in com.items()}
        m["i_xm"] = np.ascontiguousarray(x[b, half * T:(half + 1) * T])
        m["i_xp"] = zeros if half == 0 else np.ascontiguousarray(x[b, 0:T])
        in_maps.append(m)
    res = run_bass_kernel_spmd(nc, in_maps, core_ids=list(range(8)))
    out = np.empty((4, 2048, HID), np.float32)
    for core in range(8):
        b, half = core // 2, core % 2
        out[b, half * T:(half + 1) * T] = res.results[core]["y"]
    return out


# version 18
# speedup vs baseline: 1.1940x; 1.0809x over previous
import numpy as np
import concourse.bass as bass
import concourse.mybir as mybir
from concourse.bass_utils import run_bass_kernel_spmd

F32 = mybir.dt.float32
BF16 = mybir.dt.bfloat16
AF = mybir.ActivationFunctionType
ALU = mybir.AluOpType

H, D, BS, HID, NE = 32, 64, 256, 2048, 7
T = 1024
EPS = 1e-5

_nc_cache = [None]


def _host_consts():
    base = 1.0 / 2.0 ** (8.0 / H)
    rate = base ** (np.arange(H, dtype=np.float64) + 1.0)
    factor = 1.0 - 0.0 / (H - 1 + 1e-05) + 1e-05
    slope = rate * factor
    r1 = np.arange(1, BS + 1, dtype=np.float64)
    qd = np.exp(-slope[:, None] * r1[None, :])
    kd = np.exp(-slope[:, None] * (BS - r1[None, :]))
    bd = np.exp(-slope * BS)
    qdF = np.zeros((128, 16 * BS), np.float32)
    for hp in range(16):
        for p_ in range(128):
            qdF[p_, hp * BS:(hp + 1) * BS] = qd[2 * hp + p_ // 64]
    kdT = np.zeros((128, 2 * HID), np.float32)
    for par in range(2):
        for m in range(HID):
            kdT[:, par * HID + m] = kd[m // 64, par * 128:(par + 1) * 128]
    Darg = np.zeros((128, 512), np.float32)
    for cj in range(2):
        for p_ in range(128):
            tj = cj * 128 + p_
            row = np.arange(BS) - tj
            Darg[p_, cj * BS:(cj + 1) * BS] = np.where(row >= 0, row, 1e9)
    bdv = np.zeros((128, 16), np.float32)
    for ghp in range(16):
        bdv[0:64, ghp] = bd[2 * ghp]
        bdv[64:128, ghp] = bd[2 * ghp + 1]
    return (qdF, kdT, Darg, bdv, [float(-s) for s in slope])


class Prog:
    ENGINES = ("tensor", "vector", "scalar", "gpsimd", "sync")
    DMA_ENG = {"gpsimd", "sync"}

    def __init__(self):
        self.ops = []
        self.buf = {}
        self._pend = set()

    def add(self, eng, fn, deps=(), self_deps=()):
        dd = tuple(sorted({d for d in deps if d is not None}))
        sd = tuple(sorted({d for d in self_deps if d is not None}))
        self.ops.append([eng, fn, dd, sd])
        i = len(self.ops) - 1
        for n in self._pend:
            self._b(n)["r"].append(i)
        self._pend.clear()
        return i

    def _b(self, n):
        return self.buf.setdefault(n, {"w": [], "r": []})

    def rdep(self, *names):
        out = []
        for n in names:
            out += self._b(n)["w"]
        self._pend.update(names)
        return tuple(out)

    def wdep(self, *names):
        out = []
        for n in names:
            b = self._b(n)
            out += b["w"] + b["r"]
        return tuple(out)

    def note_r(self, i, *names):
        for n in names:
            self._b(n)["r"].append(i)

    def note_w(self, i, *names):
        for n in names:
            self._b(n)["w"].append(i)

    def finalize(self, nc, sems):
        NSUB = 4
        eng_ops = {e: [] for e in self.ENGINES}
        for i, (e, fn, dd, sd) in enumerate(self.ops):
            eng_ops[e].append(i)
        SYNC_SELF = {"vector", "scalar"}
        marked = [self.ops[i][0] in self.DMA_ENG for i in range(len(self.ops))]
        for i, (e, fn, dd, sd) in enumerate(self.ops):
            for d in dd:
                de = self.ops[d][0]
                if de != e or (e in SYNC_SELF and de not in self.DMA_ENG):
                    marked[d] = True
            for d in sd:
                marked[d] = True
        inc_info = [None] * len(self.ops)
        for e in self.ENGINES:
            if e in self.DMA_ENG:
                k = 0
                for i in eng_ops[e]:
                    sub = k % NSUB
                    rank = k // NSUB
                    inc_info[i] = (sub, 16 * (rank + 1))
                    k += 1
            else:
                c = 0
                for i in eng_ops[e]:
                    if marked[i]:
                        c += 1
                    inc_info[i] = (None, c)

        def emit(eng_obj, e):
            last_wait = {}
            dma_k = 0
            for i in eng_ops[e]:
                need = {}
                for d in self.ops[i][2]:
                    de = self.ops[d][0]
                    if de == e and e not in ("vector", "scalar"):
                        continue
                    sub, v = inc_info[d]
                    key = (de, sub)
                    if v > need.get(key, -1):
                        need[key] = v
                for d in self.ops[i][3]:
                    de = self.ops[d][0]
                    sub, v = inc_info[d]
                    key = (de, sub)
                    if v > need.get(key, -1):
                        need[key] = v
                if e in self.DMA_ENG:
                    sub = dma_k % NSUB
                    rank = dma_k // NSUB
                    if rank >= 1:
                        key = (e, sub)
                        v = 16 * rank
                        if v > need.get(key, -1):
                            need[key] = v
                    dma_k += 1
                for (de, sub), v in sorted(need.items(), key=lambda x: (str(x[0][0]), -1 if x[0][1] is None else x[0][1], x[1])):
                    sh = sems[de][sub] if sub is not None else sems[de]
                    kk = (de, sub)
                    if v > last_wait.get(kk, 0):
                        eng_obj.wait_ge(sh, v)
                        last_wait[kk] = v
                inst = self.ops[i][1](eng_obj)
                if e in self.DMA_ENG:
                    sub, v = inc_info[i]
                    inst.then_inc(sems[e][sub], 16)
                elif marked[i]:
                    inst.then_inc(sems[e], 1)
            if e == "sync":
                for sub in range(NSUB):
                    tot = 0
                    k = 0
                    for i in eng_ops[e]:
                        if k % NSUB == sub:
                            tot = inc_info[i][1]
                        k += 1
                    if tot:
                        eng_obj.wait_ge(sems[e][sub], tot)

        with nc.Block() as blk:
            @blk.tensor
            def _(t):
                emit(t, "tensor")

            @blk.vector
            def _(v):
                emit(v, "vector")

            @blk.scalar
            def _(s):
                emit(s, "scalar")

            @blk.gpsimd
            def _(g):
                emit(g, "gpsimd")

            @blk.sync
            def _(s):
                emit(s, "sync")

def build_nc():
    if _nc_cache[0] is not None:
        return _nc_cache[0]
    import ml_dtypes
    qdF_h, kdT_h, Darg_h, bdv_h, neg_slope = _host_consts()
    nc = bass.Bass("TRN2", target_bir_lowering=False)

    def din(name, shape):
        return nc.dram_tensor("i_" + name, shape, F32, kind="ExternalInput").ap()

    xm_h = din("xm", [T, HID])
    xp_h = din("xp", [T, HID])
    wq_h = din("wq", [16, 128, 2048])
    wk_h = din("wk", [16, 128, 2048])
    wv_h = din("wv", [16, 128, 2048])
    wg_h = din("wg", [16, 128, 2048])
    wo_h = din("wo", [16, 128, 2048])
    wr_h = din("wr", [128, 112])
    wup_h = din("wup", [NE, 16, 128, 2048])
    wgt_h = din("wgt", [NE, 16, 128, 2048])
    wdn_h = din("wdn", [NE, 16, 128, 2048])
    wsu_h = din("wsu", [16, 128, 2048])
    wsg_h = din("wsg", [16, 128, 2048])
    wsd_h = din("wsd", [16, 128, 2048])
    bupP_h = din("bupP", [128, 112])
    bgtP_h = din("bgtP", [128, 112])
    bdnT_h = din("bdnT", [7, 2048])
    bsuP_h = din("bsuP", [128, 16])
    bsgP_h = din("bsgP", [128, 16])
    bsdP_h = din("bsdP", [128, 16])
    nwP_h = din("nwP", [128, 16])
    br_h = din("br", [7, 1])
    gb_h = din("gb", [1, 7])
    y_h = nc.dram_tensor("y", [T, HID], F32, kind="ExternalOutput").ap()

    ident_f_h = nc.inline_tensor(np.eye(128, dtype=np.float32), "identf_inl").ap()
    ident_b_h = nc.inline_tensor(np.eye(128, dtype=ml_dtypes.bfloat16), "identb_inl").ap()
    onesb_h = nc.inline_tensor(np.ones((128, 1), dtype=ml_dtypes.bfloat16), "onesb_inl").ap()
    _onesrf = np.zeros((33, 128), dtype=np.float32)
    _onesrf[0, :] = 1.0
    _onesrf[32, :] = 1.0
    onesrf_h = nc.inline_tensor(_onesrf, "onesrf_inl").ap()
    onesrb_h = nc.inline_tensor(np.ones((1, 128), dtype=ml_dtypes.bfloat16), "onesrb_inl").ap()
    qdF_hh = nc.inline_tensor(qdF_h, "qdF_inl").ap()
    kdT_hh = nc.inline_tensor(kdT_h, "kdT_inl").ap()
    Darg_hh = nc.inline_tensor(Darg_h, "Darg_inl").ap()
    bdv_hh = nc.inline_tensor(bdv_h, "bdv_inl").ap()


    p = Prog()
    open_ctx = []

    def sb(name, shp, dt):
        g = nc.sbuf_tensor(name, shp, dt)
        h = g.__enter__()
        open_ctx.append((name, g))
        return h

    psum_ctx = []

    def psum(name, shp, dt=F32):
        g = nc.psum_tensor(name, shp, dt)
        h = g.__enter__()
        psum_ctx.append(g)
        return h

    ident_f = sb("ident_f", [128, 128], F32)
    ident_b = sb("ident_b", [128, 128], BF16)
    onesb = sb("onesb", [128, 1], BF16)
    onesrf = sb("onesrf", [33, 128], F32)
    onesrb = sb("onesrb", [1, 128], BF16)
    nwP = sb("nwP", [128, 16], F32)
    bupP = sb("bupP", [128, 112], F32)
    bgtP = sb("bgtP", [128, 112], F32)
    bsuP = sb("bsuP", [128, 16], F32)
    bsgP = sb("bsgP", [128, 16], F32)
    bsdP = sb("bsdP", [128, 16], F32)
    brS = sb("brS", [7, 1], F32)
    gbS = sb("gbS", [1, 7], F32)
    gb_bc = sb("gb_bc", [128, 7], F32)
    wrS = sb("wrS", [128, 112], BF16)
    statc = sb("statc", [128, 32], F32)
    epsc = sb("epsc", [128, 1], F32)
    routb = sb("routb", [128, 48], F32)
    R1 = sb("R1", [128, 16384], BF16)
    R2 = sb("R2", [128, 16384], BF16)
    R3 = sb("R3", [128, 16384], BF16)
    NRING = 2
    ring = [sb(f"ring{i}", [128, 2048], BF16) for i in range(NRING)]

    PA = psum("PA", [128, 1024])
    PC = psum("PC", [128, 512])
    PD = psum("PD", [128, 1024])
    PE_ = psum("PE", [128, 1024])
    PT = psum("PT", [128, 1024], BF16)

    xstage = [sb(f"xstage{i}", [128, 2048], F32) for i in range(2)]

    bph = []

    def sb_b(name, shp, dt):
        g = nc.sbuf_tensor(name, shp, dt)
        h = g.__enter__()
        bph.append(g)
        return h

    Darg = sb_b("Darg", [128, 512], F32)
    qdFg = sb_b("qdFg", [128, 1024], F32)
    kdTg = sb_b("kdTg", [128, 1024], F32)
    bdv = sb_b("bdv", [128, 16], F32)
    S_sb = sb_b("S_sb", [128, 512], F32)
    S_bf = sb_b("S_bf", [128, 512], BF16)
    qqd = sb_b("qqd", [128, 2048], BF16)
    ddT = sb_b("ddT", [128, 2048], BF16)
    awT = sb_b("awT", [128, 2048], BF16)
    q_fm = sb_b("q_fm", [128, 4096], BF16)
    k_fm = sb_b("k_fm", [128, 4096], BF16)
    tmp_fm = sb_b("tmp_fm", [128, 4096], BF16)
    k_tm = sb_b("k_tm", [128, 4096], BF16)
    v_tm = sb_b("v_tm", [128, 4096], BF16)

    for dst, src, key in [(ident_f, ident_f_h, "identf"), (ident_b, ident_b_h, "identb"),
                          (onesb, onesb_h, "onesb"), (onesrf, onesrf_h, "onesrf"),
                          (onesrb, onesrb_h, "onesrb"), (nwP, nwP_h, "nwP"),
                          (bupP, bupP_h, "bupP"), (bgtP, bgtP_h, "bgtP"),
                          (bsuP, bsuP_h, "bsuP"), (bsgP, bsgP_h, "bsgP"),
                          (bsdP, bsdP_h, "bsdP"), (brS, br_h, "brS"),
                          (gbS, gb_h, "gbS"), (Darg, Darg_hh, "Darg"),
                          (bdv, bdv_hh, "bdv")]:
        i = p.add("sync", lambda s, d=dst, so=src: s.dma_start(out=d[:, :], in_=so))
        p.note_w(i, key)
    i = p.add("vector", lambda v: v.memset(epsc[:, 0:1], EPS))
    p.note_w(i, "epsc")
    i = p.add("gpsimd", lambda g: g.dma_start(out=wrS[:, :], in_=wr_h))
    p.note_w(i, "wrS")
    i = p.add("tensor", lambda t: t.matmul(PC[0:128, 0:7], onesrf[0:1, :], gbS[0:1, :],
                                           start=True, stop=True),
              deps=p.rdep("onesrf", "gbS") + p.wdep("PC"))
    p.note_w(i, "PC")
    i2 = p.add("vector", lambda v: v.tensor_copy(gb_bc[:, :], PC[:, 0:7]),
               deps=p.rdep("PC") + p.wdep("gb_bc"))
    p.note_r(i2, "PC")
    p.note_w(i2, "gb_bc")

    ring_i = [0]

    def ring_load(src_ap):
        s = ring_i[0] % NRING
        ring_i[0] += 1
        key = ("ring", s)
        i = p.add("gpsimd", lambda g, d=ring[s], so=src_ap: g.dma_start(out=d[:, :], in_=so),
                  deps=p.wdep(key))
        p.note_w(i, key)
        return ring[s], key

    rot = [0]
    PROJ = [("PD", PD), ("PE", PE_)]

    def next_psum():
        r = PROJ[rot[0] % 2]
        rot[0] += 1
        return r

    def mm_group(P_t, pkey, slab, skey, rhs, rkey, extra=(), kc_keys=None):
        first_deps = p.wdep(pkey) + p.rdep(skey, rkey) + tuple(extra)
        last = None
        for kc in range(16):
            dd = first_deps if kc == 0 else ()
            if kc_keys is not None:
                dd = dd + p.rdep(kc_keys(kc))
            for hh in range(2):
                st = (kc == 0)
                sp = (kc == 15)
                last = p.add("tensor", lambda t, P=P_t, sl=slab, kc=kc, hh=hh, st=st, sp=sp, rb=rhs:
                             t.matmul(P[:, hh * 512:(hh + 1) * 512],
                                      sl[:, kc * 128:(kc + 1) * 128],
                                      rb[:, kc * 1024 + hh * 512:kc * 1024 + (hh + 1) * 512],
                                      start=st, stop=sp),
                             deps=dd if hh == 0 else ())
        p.note_r(last, skey, rkey)
        p.note_w(last, pkey)
        if kc_keys is not None:
            for kc in range(16):
                p.note_r(last, kc_keys(kc))
        return last

    def transpose_batch(src, src_cols, src_key, src_deps, dst_ap_fn, dst_key, ident):
        tids = []
        for j, c0 in enumerate(src_cols):
            ti = p.add("tensor", lambda t, j=j, c0=c0: t.transpose(
                PT[:, j * 128:(j + 1) * 128], src[:, c0:c0 + 128], ident[:, :]),
                deps=(p.wdep("PT") + tuple(src_deps) + p.rdep(src_key)) if j == 0 else ())
            tids.append(ti)
        p.note_r(tids[-1], src_key)
        p.note_w(tids[-1], "PT")
        i2 = p.add("vector", lambda v: v.tensor_copy(
            dst_ap_fn(), PT[:, 0:len(src_cols) * 128].rearrange("p (a f) -> p a f", f=128)),
            deps=(tids[-1],) + p.wdep(dst_key))
        p.note_r(i2, "PT")
        p.note_w(i2, dst_key)
        return i2

    def stage_A(jobs):
        for tc in range(8):
          for k, (x_hbm, dst, dst_key) in enumerate(jobs):
              so = 16 * k
              lo = 2048 * k
              i = p.add("sync", lambda s, k=k, tc=tc, xh=x_hbm: s.dma_start(
                  out=xstage[k][:, :], in_=xh[tc * 128:(tc + 1) * 128, :]),
                  deps=p.wdep(("xs", k)))
              p.note_w(i, ("xs", k))
              i = p.add("vector", lambda v, k=k, so=so: v.reduce_sum(
                  statc[:, so:so + 1], xstage[k][:, :], axis=mybir.AxisListType.X),
                  deps=p.rdep(("xs", k)) + p.wdep(("st_sum", k)))
              p.note_r(i, ("xs", k))
              p.note_w(i, ("st_sum", k))
              i = p.add("scalar", lambda s, k=k, so=so: s.activation(
                  R3[:, 4096 + 2048 * k:6144 + 2048 * k], xstage[k][:, :], AF.Square,
                  accum_out=statc[:, so + 2:so + 3]),
                  deps=p.rdep(("xs", k)) + p.wdep(("st_sq", k), ("R3dump", k)))
              p.note_r(i, ("xs", k))
              p.note_w(i, ("st_sq", k))
              p.note_w(i, ("R3dump", k))
              i = p.add("scalar", lambda s, so=so: s.activation(
                  statc[:, so + 4:so + 5], statc[:, so:so + 1], AF.Identity, scale=1.0 / 2048.0),
                  deps=p.rdep(("st_sum", k)) + p.wdep(("st_m", k)))
              p.note_w(i, ("st_m", k))
              i = p.add("scalar", lambda s, so=so: s.activation(
                  statc[:, so + 6:so + 7], statc[:, so + 2:so + 3], AF.Identity, scale=1.0 / 2048.0),
                  deps=p.rdep(("st_sq", k)) + p.wdep(("st_e2", k)))
              p.note_w(i, ("st_e2", k))
              i = p.add("vector", lambda v, so=so: v.tensor_mul(
                  statc[:, so + 8:so + 9], statc[:, so + 4:so + 5], statc[:, so + 4:so + 5]),
                  deps=p.rdep(("st_m", k)) + p.wdep(("st_msq", k)))
              p.note_w(i, ("st_msq", k))
              i = p.add("vector", lambda v, so=so: v.tensor_sub(
                  statc[:, so + 10:so + 11], statc[:, so + 6:so + 7], statc[:, so + 8:so + 9]),
                  deps=p.rdep(("st_e2", k), ("st_msq", k)) + p.wdep(("st_var", k)))
              p.note_w(i, ("st_var", k))
              i = p.add("scalar", lambda s, so=so: s.activation(
                  statc[:, so + 12:so + 13], statc[:, so + 10:so + 11], AF.Sqrt,
                  bias=epsc[:, 0:1]),
                  deps=p.rdep(("st_var", k), "epsc") + p.wdep(("st_sd", k)))
              p.note_w(i, ("st_sd", k))
              i = p.add("vector", lambda v, so=so: v.reciprocal(
                  statc[:, so + 14:so + 15], statc[:, so + 12:so + 13]),
                  deps=p.rdep(("st_sd", k)) + p.wdep(("st_a", k)))
              p.note_w(i, ("st_a", k))
              i = p.add("vector", lambda v, so=so: v.scalar_tensor_tensor(
                  statc[:, so + 8:so + 9], statc[:, so + 4:so + 5], -1.0,
                  statc[:, so + 14:so + 15], ALU.mult, ALU.mult),
                  deps=p.rdep(("st_a", k), ("st_m", k)) + p.wdep(("st_msq", k)))
              p.note_w(i, ("st_b", k))
              i = p.add("scalar", lambda s, k=k, so=so, lo=lo: s.activation(
                  R3[:, lo:lo + 2048], xstage[k][:, :], AF.Identity,
                  bias=statc[:, so + 8:so + 9], scale=statc[:, so + 14:so + 15]),
                  deps=p.rdep(("xs", k), ("st_a", k), ("st_b", k)) + p.wdep(("lntm", k)))
              p.note_r(i, ("xs", k))
              p.note_w(i, ("lntm", k))
              for q4 in range(4):
                  transpose_batch(
                      R3, [lo + (q4 * 4 + j) * 128 for j in range(4)], ("lntm", k), (),
                      lambda tc=tc, q4=q4, dst=dst: dst.ap().rearrange(
                          "p (fc t) -> p fc t", t=1024)[:, q4 * 4:(q4 + 1) * 4,
                                                        tc * 128:(tc + 1) * 128],
                      dst_key, ident_b)

    stage_A([(xm_h, R1, "R1"), (xp_h, R2, "R2")])

    def project(slab, skey, rhs, rkey, dst, dst_key, dst_col, extra=()):
        pkey, P_t = next_psum()
        mm_group(P_t, pkey, slab, skey, rhs, rkey, extra=extra)
        ev = p.add("scalar", lambda s, P=P_t, d=dst, dc=dst_col: s.activation(
            d[:, dc:dc + 1024], P[:, :], AF.Silu),
            deps=p.rdep(pkey) + p.wdep(dst_key))
        p.note_r(ev, pkey)
        p.note_w(ev, dst_key)
        return ev

    for g in range(4):
        i = p.add("sync", lambda s, g=g: s.dma_start(
            out=qdFg[:, :], in_=qdF_hh[:, g * 1024:(g + 1) * 1024]),
            deps=p.wdep("qdFg"))
        p.note_w(i, "qdFg")
        i = p.add("sync", lambda s, g=g: s.dma_start(
            out=kdTg[:, :].rearrange("p (a m) -> p a m", m=512),
            in_=kdT_hh.rearrange("p (a m) -> p a m", m=2048)[:, :, g * 512:(g + 1) * 512]),
            deps=p.wdep("kdTg"))
        p.note_w(i, "kdTg")

        pend = []
        for mcl in range(4):
            mc = g * 4 + mcl
            slab, skey = ring_load(wk_h[mc, :, :])
            project(slab, skey, R1, "R1", k_fm, ("kf", g, mcl), mcl * 1024)
            evp = project(slab, skey, R2, "R2", tmp_fm, ("tm", mcl), mcl * 1024)
            for th in pend:
                th()
            pend = [lambda q4=q4, mcl=mcl, evp=evp: transpose_batch(
                tmp_fm, [mcl * 1024 + (q4 * 4 + j) * 128 for j in range(4)],
                ("tm", mcl), (evp,),
                lambda q4=q4, mcl=mcl: k_tm.ap().rearrange(
                    "p (tc j) -> p tc j", j=512)[:, q4 * 4:(q4 + 1) * 4,
                                                 mcl * 128:(mcl + 1) * 128],
                "ktm", ident_b) for q4 in range(2)]
        for mcl in range(4):
            mc = g * 4 + mcl
            slab, skey = ring_load(wv_h[mc, :, :])
            project(slab, skey, R1, "R1", q_fm, ("vmain", g, mcl), mcl * 1024)
            evp = project(slab, skey, R2, "R2", tmp_fm, ("tm", mcl), mcl * 1024)
            for th in pend:
                th()
            pend = [lambda q4=q4, mcl=mcl, evp=evp: transpose_batch(
                tmp_fm, [mcl * 1024 + (q4 * 4 + j) * 128 for j in range(4)],
                ("tm", mcl), (evp,),
                lambda q4=q4, mcl=mcl: v_tm.ap().rearrange(
                    "p (tc j) -> p tc j", j=512)[:, q4 * 4:(q4 + 1) * 4,
                                                 mcl * 128:(mcl + 1) * 128],
                "vtm", ident_b) for q4 in range(2)]
        for th in pend:
            th()
        for tc in range(8):
            i = p.add("vector", lambda v, tc=tc: v.tensor_tensor(
                k_tm[:, tc * 512:(tc + 1) * 512], k_tm[:, tc * 512:(tc + 1) * 512],
                kdTg[:, (tc % 2) * 512:((tc % 2) + 1) * 512], ALU.mult),
                deps=p.rdep("ktm", "kdTg") + p.wdep(("kkd", tc)))
            p.note_r(i, "kdTg")
            p.note_w(i, "ktm")
            p.note_w(i, ("kkd", tc))
        i = p.add("vector", lambda v: v.memset(S_sb[:, :], 0.0),
                  deps=p.wdep("S", ("Sc", 0), ("Sc", 1), ("Sc", 2), ("Sc", 3)))
        p.note_w(i, "S", ("Sc", 0), ("Sc", 1), ("Sc", 2), ("Sc", 3))
        for c in range(4):
            ghp = g * 4 + c
            for pb in range(4):
                mm1 = p.add("tensor", lambda t, pb=pb, c=c: t.matmul(
                    PC[:, 256:384],
                    k_tm[:, (2 * pb) * 512 + c * 128:(2 * pb) * 512 + (c + 1) * 128],
                    v_tm[:, (2 * pb) * 512 + c * 128:(2 * pb) * 512 + (c + 1) * 128],
                    start=True, stop=False),
                    deps=p.rdep("ktm", "vtm") + p.wdep("PC"))
                mm2 = p.add("tensor", lambda t, pb=pb, c=c: t.matmul(
                    PC[:, 256:384],
                    k_tm[:, (2 * pb + 1) * 512 + c * 128:(2 * pb + 1) * 512 + (c + 1) * 128],
                    v_tm[:, (2 * pb + 1) * 512 + c * 128:(2 * pb + 1) * 512 + (c + 1) * 128],
                    start=False, stop=True))
                p.note_r(mm2, "ktm", "vtm")
                p.note_w(mm2, "PC")
                dec = p.add("vector", lambda v, c=c, ghp=ghp: v.tensor_scalar_mul(
                    S_sb[:, c * 128:(c + 1) * 128], S_sb[:, c * 128:(c + 1) * 128],
                    bdv[:, ghp:ghp + 1]),
                    deps=p.rdep("bdv") + p.wdep(("Sc", c)))
                p.note_w(dec, ("Sc", c))
                ad = p.add("vector", lambda v, c=c: v.tensor_tensor(
                    S_sb[:, c * 128:(c + 1) * 128], S_sb[:, c * 128:(c + 1) * 128],
                    PC[:, 256:384], ALU.add),
                    deps=(mm2,) + p.wdep(("Sc", c)))
                p.note_r(ad, "PC")
                p.note_w(ad, ("Sc", c))
            cst = p.add("scalar", lambda s, c=c: s.activation(
                S_bf[:, c * 128:(c + 1) * 128], S_sb[:, c * 128:(c + 1) * 128], AF.Identity),
                deps=p.rdep(("Sc", c)) + p.wdep(("Sbf", c)))
            p.note_r(cst, ("Sc", c))
            p.note_w(cst, ("Sbf", c))
        pref_done = tuple(p._b("PC")["r"])
        for mcl in range(4):
            for q4 in range(2):
                transpose_batch(
                    k_fm, [mcl * 1024 + (q4 * 4 + j) * 128 for j in range(4)],
                    ("kf", g, mcl), pref_done,
                    lambda q4=q4, mcl=mcl: k_tm.ap().rearrange(
                        "p (tc j) -> p tc j", j=512)[:, q4 * 4:(q4 + 1) * 4,
                                                     mcl * 128:(mcl + 1) * 128],
                    "ktm", ident_b)
                transpose_batch(
                    q_fm, [mcl * 1024 + (q4 * 4 + j) * 128 for j in range(4)],
                    ("vmain", g, mcl), pref_done,
                    lambda q4=q4, mcl=mcl: v_tm.ap().rearrange(
                        "p (tc j) -> p tc j", j=512)[:, q4 * 4:(q4 + 1) * 4,
                                                     mcl * 128:(mcl + 1) * 128],
                    "vtm", ident_b)
        for tc in range(8):
            i = p.add("vector", lambda v, tc=tc: v.tensor_tensor(
                k_tm[:, tc * 512:(tc + 1) * 512], k_tm[:, tc * 512:(tc + 1) * 512],
                kdTg[:, (tc % 2) * 512:((tc % 2) + 1) * 512], ALU.mult),
                deps=p.rdep("ktm", "kdTg") + p.wdep(("kkd", tc)))
            p.note_r(i, "kdTg")
            p.note_w(i, "ktm")
            p.note_w(i, ("kkd", tc))
        for mcl in range(4):
            mc = g * 4 + mcl
            slab, skey = ring_load(wq_h[mc, :, :])
            project(slab, skey, R1, "R1", q_fm, ("vmain", g, mcl), mcl * 1024)
        def emit_aw(c, blk):
            par = (c * 4 + blk) % 2
            AWP, awk = (PA, "PA") if par == 0 else (PD, "PD")
            awm = []
            for hs in range(2):
                for cj in range(2):
                    b0 = hs * 64
                    tp = (64, 0) if hs == 1 else None
                    i = p.add("tensor", lambda t, c=c, blk=blk, hs=hs, cj=cj, b0=b0,
                              tp=tp, AWP=AWP: t.matmul(
                        AWP[:, hs * 512 + cj * 256:hs * 512 + (cj + 1) * 256],
                        k_fm[b0:b0 + 64, c * 1024 + blk * 256 + cj * 128:
                             c * 1024 + blk * 256 + (cj + 1) * 128],
                        q_fm[b0:b0 + 64, c * 1024 + blk * 256:c * 1024 + (blk + 1) * 256],
                        start=True, stop=True, tile_position=tp),
                        deps=(p.rdep(("kf", g, c), ("vmain", g, c)) + p.wdep(awk))
                             if (hs == 0 and cj == 0) else ())
                    awm.append(i)
            p.note_r(awm[-1], ("kf", g, c), ("vmain", g, c))
            p.note_w(awm[-1], awk)
            return awm[-1]

        for c in range(4):
            ghp = g * 4 + c
            hA, hB = 2 * ghp, 2 * ghp + 1
            do = (c % 2) * 1024
            dA = p.add("scalar", lambda s, sl=neg_slope[hA], do=do: s.activation(
                ddT[:, do:do + 512], Darg[:, :], AF.Exp, scale=sl),
                deps=p.rdep("Darg") + p.wdep(("ddT", c % 2)))
            p.note_w(dA, ("ddT", c % 2))
            dB = p.add("scalar", lambda s, sl=neg_slope[hB], do=do: s.activation(
                ddT[:, do + 512:do + 1024], Darg[:, :], AF.Exp, scale=sl),
                deps=p.wdep(("ddT", c % 2)))
            p.note_w(dB, ("ddT", c % 2))
            qq = []
            for i4 in range(4):
                i = p.add("vector", lambda v, i4=i4, c=c, do=do: v.tensor_tensor(
                    qqd[:, do + i4 * 256:do + (i4 + 1) * 256],
                    q_fm[:, c * 1024 + i4 * 256:c * 1024 + (i4 + 1) * 256],
                    qdFg[:, c * 256:(c + 1) * 256], ALU.mult),
                    deps=p.rdep(("vmain", g, c), "qdFg") +
                         (p.wdep(("qqd", c % 2)) if i4 == 0 else ()))
                qq.append(i)
            p.note_r(qq[0], ("vmain", g, c))
            p.note_r(qq[-1], "qdFg")
            p.note_w(qq[-1], ("qqd", c % 2))
            aw_last = emit_aw(c, 0)
            for blk in range(4):
                par = (c * 4 + blk) % 2
                AWP, awk = (PA, "PA") if par == 0 else (PD, "PD")
                IOP, iok = (PC, "PC") if par == 0 else (PE_, "PE")
                sbase = 256 if par == 0 else 512
                ao = par * 1024
                ev = p.add("vector", lambda v, AWP=AWP, ao=ao, do=do: v.tensor_tensor(
                    awT[:, ao:ao + 1024], AWP[:, :], ddT[:, do:do + 1024], ALU.mult),
                    deps=(aw_last, dA, dB) + p.wdep(("awT", par)))
                p.note_r(ev, awk)
                p.note_r(ev, ("ddT", c % 2))
                p.note_w(ev, ("awT", par))
                if blk < 3:
                    aw_last = emit_aw(c, blk + 1)
                last_io = None
                for hs in range(2):
                    b0 = hs * 64
                    tpc = (0, 64) if hs == 1 else None
                    for cj in range(2):
                        i = p.add("tensor", lambda t, c=c, blk=blk, hs=hs, cj=cj, b0=b0,
                                  tpc=tpc, IOP=IOP, ao=ao: t.matmul(
                            IOP[b0:b0 + 64, 0:256],
                            v_tm[:, (2 * blk + cj) * 512 + c * 128 + b0:
                                 (2 * blk + cj) * 512 + c * 128 + b0 + 64],
                            awT[:, ao + hs * 512 + cj * 256:ao + hs * 512 + (cj + 1) * 256],
                            start=(cj == 0), stop=False, tile_position=tpc),
                            deps=(ev,) + ((p.rdep("vtm") + p.wdep(iok))
                                          if (hs == 0 and cj == 0) else ()))
                        last_io = i
                    tpi = (64, 64) if hs == 1 else None
                    i = p.add("tensor", lambda t, c=c, blk=blk, hs=hs, b0=b0, tpi=tpi,
                              IOP=IOP, do=do: t.matmul(
                        IOP[b0:b0 + 64, 0:256],
                        S_bf[b0:b0 + 64, c * 128 + b0:c * 128 + b0 + 64],
                        qqd[b0:b0 + 64, do + blk * 256:do + (blk + 1) * 256],
                        start=False, stop=True, tile_position=tpi),
                        deps=p.rdep(("Sbf", c), ("qqd", c % 2)))
                    last_io = i
                p.note_r(last_io, "vtm", ("awT", par), ("Sbf", c), ("qqd", c % 2))
                p.note_w(last_io, iok)
                ae = p.add("scalar", lambda s, ghp=ghp, blk=blk, IOP=IOP: s.activation(
                    R3[:, ghp * 1024 + blk * 256:ghp * 1024 + (blk + 1) * 256],
                    IOP[:, 0:256], AF.Identity),
                    deps=(last_io,) + p.wdep("R3a"))
                p.note_r(ae, iok)
                p.note_w(ae, "R3a")
                mm1 = p.add("tensor", lambda t, blk=blk, c=c, IOP=IOP, sbase=sbase: t.matmul(
                    IOP[:, sbase:sbase + 128],
                    k_tm[:, (2 * blk) * 512 + c * 128:(2 * blk) * 512 + (c + 1) * 128],
                    v_tm[:, (2 * blk) * 512 + c * 128:(2 * blk) * 512 + (c + 1) * 128],
                    start=True, stop=False),
                    deps=p.rdep("ktm", "vtm") + p.wdep(iok))
                mm2 = p.add("tensor", lambda t, blk=blk, c=c, IOP=IOP, sbase=sbase: t.matmul(
                    IOP[:, sbase:sbase + 128],
                    k_tm[:, (2 * blk + 1) * 512 + c * 128:(2 * blk + 1) * 512 + (c + 1) * 128],
                    v_tm[:, (2 * blk + 1) * 512 + c * 128:(2 * blk + 1) * 512 + (c + 1) * 128],
                    start=False, stop=True))
                p.note_r(mm2, "ktm", "vtm")
                p.note_w(mm2, iok)
                if blk < 3:
                    dec = p.add("vector", lambda v, c=c, ghp=ghp: v.tensor_scalar_mul(
                        S_sb[:, c * 128:(c + 1) * 128], S_sb[:, c * 128:(c + 1) * 128],
                        bdv[:, ghp:ghp + 1]), deps=p.rdep("bdv") + p.wdep(("Sc", c)))
                    p.note_w(dec, ("Sc", c))
                    ad = p.add("vector", lambda v, c=c, IOP=IOP, sbase=sbase: v.tensor_tensor(
                        S_sb[:, c * 128:(c + 1) * 128], S_sb[:, c * 128:(c + 1) * 128],
                        IOP[:, sbase:sbase + 128], ALU.add), deps=(mm2,) + p.wdep(("Sc", c)))
                    p.note_r(ad, iok)
                    p.note_w(ad, ("Sc", c))
                    cst = p.add("scalar", lambda s, c=c: s.activation(
                        S_bf[:, c * 128:(c + 1) * 128], S_sb[:, c * 128:(c + 1) * 128],
                        AF.Identity), deps=p.rdep(("Sc", c)) + p.wdep(("Sbf", c)))
                    p.note_r(cst, ("Sc", c))
                    p.note_w(cst, ("Sbf", c))
                else:
                    ad = p.add("vector", lambda v, IOP=IOP, sbase=sbase: v.tensor_copy(
                        statc[:, 0:1], IOP[:, sbase:sbase + 1]), deps=(mm2,))
                    p.note_r(ad, iok)

    for g_ in reversed(bph):
        g_.__exit__(None, None, None)

    cph = []

    def sb_c(name, shp, dt):
        g = nc.sbuf_tensor(name, shp, dt)
        h = g.__enter__()
        cph.append(g)
        return h

    acc = sb_c("acc", [128, 16384], F32)
    rowb = sb_c("rowb", [65, 2048], F32)
    probsTb = sb_c("probsTb", [7, 1024], BF16)
    bc1 = sb_c("bc1", [128, 1024], BF16)
    bcB = sb_c("bcB", [128, 1024], BF16)
    tmpxb = sb_c("tmpxb", [128, 1024], BF16)
    bdnT = sb_c("bdnT", [7, 2048], BF16)
    i = p.add("gpsimd", lambda g: g.dma_start(out=bdnT[:, :], in_=bdnT_h))
    p.note_w(i, "bdnT")

    for mc in range(16):
        sq = p.add("scalar", lambda s, mc=mc: s.activation(
            tmpxb[:, :], R3[:, mc * 1024:(mc + 1) * 1024], AF.Square),
            deps=p.rdep("R3a") + p.wdep("tmpxb"))
        p.note_r(sq, "R3a")
        p.note_w(sq, "tmpxb")
        p.note_w(sq, ("sqd", mc))
        for hh in range(2):
            i = p.add("tensor", lambda t, hh=hh, mc=mc: t.matmul(
                PA[0:1, hh * 512:(hh + 1) * 512], onesb[:, 0:1],
                tmpxb[:, hh * 512:(hh + 1) * 512],
                start=(mc == 0), stop=(mc == 15)),
                deps=(sq,) + ((p.rdep("onesb") + p.wdep("PA")) if mc == 0 and hh == 0 else ()))
        p.note_r(i, "tmpxb")
    p.note_w(i, "PA")
    i = p.add("scalar", lambda s: s.activation(rowb[64:65, 0:1024], PA[0:1, :], AF.Sqrt,
                                               scale=1.0 / 2048.0, bias=epsc[0:1, 0:1]),
              deps=p.rdep("PA", "epsc") + p.wdep("row1"))
    p.note_r(i, "PA")
    p.note_w(i, "row1")
    i = p.add("vector", lambda v: v.reciprocal(rowb[0:1, 0:1024], rowb[64:65, 0:1024]),
              deps=p.rdep("row1") + p.wdep("row0"))
    p.note_w(i, "row0")
    for hh in range(2):
        i = p.add("tensor", lambda t, hh=hh: t.matmul(
            PA[:, hh * 512:(hh + 1) * 512], onesrf[0:1, :], rowb[0:1, hh * 512:(hh + 1) * 512],
            start=True, stop=True),
            deps=p.rdep("row0", "onesrf") + (p.wdep("PA") if hh == 0 else ()))
    p.note_w(i, "PA")
    i2 = p.add("vector", lambda v: v.tensor_copy(bc1[:, :], PA[:, :]),
               deps=(i,) + p.wdep("bc1"))
    p.note_r(i2, "PA")
    p.note_w(i2, "bc1")

    for mc in range(16):
        slab, skey = ring_load(wg_h[mc, :, :])
        pkey, P_t = next_psum()
        mm_group(P_t, pkey, slab, skey, R1, "R1")
        sg = p.add("scalar", lambda s, P=P_t: s.activation(bcB[:, :], P[:, :], AF.Sigmoid),
                   deps=p.rdep(pkey) + p.wdep("bcB"))
        p.note_r(sg, pkey)
        p.note_w(sg, "bcB")
        v3 = p.add("vector", lambda v, mc=mc: v.tensor_tensor(
            R3[:, mc * 1024:(mc + 1) * 1024], R3[:, mc * 1024:(mc + 1) * 1024],
            bcB[:, :], ALU.mult),
            deps=(sg,) + p.rdep(("sqd", mc)) + p.wdep("R3a") + p.rdep("R3a"))
        p.note_r(v3, "bcB")
        p.note_w(v3, "R3a")
        v1 = p.add("vector", lambda v, mc=mc: v.tensor_scalar_mul(
            R3[:, mc * 1024:(mc + 1) * 1024], R3[:, mc * 1024:(mc + 1) * 1024],
            nwP[:, mc:mc + 1]),
            deps=(v3,) + p.rdep("nwP"))
        v2 = p.add("vector", lambda v, mc=mc: v.tensor_tensor(
            R3[:, mc * 1024:(mc + 1) * 1024], R3[:, mc * 1024:(mc + 1) * 1024],
            bc1[:, :], ALU.mult), deps=(v1,) + p.rdep("bc1"))
        p.note_w(v2, ("ga", mc))
        p.note_w(v2, "R3a")

    for mc in range(16):
        slab, skey = ring_load(wo_h[mc, :, :])
        pkey, P_t = next_psum()
        mm_group(P_t, pkey, slab, skey, R3, "R3a", kc_keys=lambda kc: ("ga", kc))
        ev = p.add("vector", lambda v, P=P_t, mc=mc: v.tensor_copy(
            acc[:, mc * 1024:(mc + 1) * 1024], P[:, :]),
            deps=p.rdep(pkey) + p.wdep(("acc", mc)))
        p.note_r(ev, pkey)
        p.note_w(ev, ("acc", mc))

    for tc in range(8):
        k = tc % 2
        ld = p.add("sync", lambda s, k=k, tc=tc: s.dma_start(
            out=xstage[k][:, :], in_=xm_h[tc * 128:(tc + 1) * 128, :]),
            deps=p.wdep(("xs", k)))
        p.note_w(ld, ("xs", k))
        for q4 in range(4):
            tids = []
            for j in range(4):
                fc = q4 * 4 + j
                ti = p.add("tensor", lambda t, k=k, fc=fc, j=j: t.transpose(
                    PC[:, j * 128:(j + 1) * 128], xstage[k][:, fc * 128:(fc + 1) * 128],
                    ident_f[:, :]),
                    deps=((ld,) + p.rdep("identf") + p.wdep("PC")) if j == 0 else ())
                tids.append(ti)
            p.note_r(tids[-1], ("xs", k))
            p.note_w(tids[-1], "PC")
            av = p.add("vector", lambda v, q4=q4, tc=tc: v.tensor_tensor(
                acc.ap().rearrange("p (fc t) -> p fc t", t=1024)[:, q4 * 4:(q4 + 1) * 4,
                                                                 tc * 128:(tc + 1) * 128],
                acc.ap().rearrange("p (fc t) -> p fc t", t=1024)[:, q4 * 4:(q4 + 1) * 4,
                                                                 tc * 128:(tc + 1) * 128],
                PC[:, 0:512].rearrange("p (a f) -> p a f", f=128), ALU.add),
                deps=(tids[-1],) + p.rdep(*[("acc", q4 * 4 + j) for j in range(4)]))
            p.note_r(av, "PC")
            for j in range(4):
                p.note_w(av, ("acc", q4 * 4 + j))

    acc_all = p.rdep(*[("acc", mc) for mc in range(16)])

    for pass_i, (func, dst_row) in enumerate([(AF.Square, 2), (AF.Identity, 1)]):
        for mc in range(16):
            sqb, sqk = (tmpxb, "tmpxb") if mc % 2 == 0 else (bc1, "bc1")
            cv = p.add("scalar", lambda s, mc=mc, func=func, sqb=sqb: s.activation(
                sqb[:, :], acc[:, mc * 1024:(mc + 1) * 1024], func),
                deps=acc_all + p.wdep(sqk) + p.rdep(("acc", mc)))
            p.note_r(cv, ("acc", mc))
            p.note_w(cv, sqk)
            for hh in range(2):
                i = p.add("tensor", lambda t, hh=hh, mc=mc, sqb=sqb: t.matmul(
                    PA[0:1, hh * 512:(hh + 1) * 512], onesb[:, 0:1],
                    sqb[:, hh * 512:(hh + 1) * 512],
                    start=(mc == 0), stop=(mc == 15)),
                    deps=(cv,) + (p.wdep("PA") if mc == 0 and hh == 0 else ()))
            p.note_r(i, sqk)
        p.note_w(i, "PA")
        dst_ap = (rowb[64:65, 0:1024] if dst_row == 2 else rowb[0:1, 1024:2048])
        i = p.add("scalar", lambda s, dst_ap=dst_ap: s.activation(
            dst_ap, PA[0:1, :], AF.Identity, scale=1.0 / 2048.0),
            deps=p.rdep("PA") + p.wdep(f"rowL{dst_row}"))
        p.note_r(i, "PA")
        p.note_w(i, f"rowL{dst_row}")
    i = p.add("vector", lambda v: v.tensor_mul(rowb[64:65, 1024:2048], rowb[0:1, 1024:2048],
                                               rowb[0:1, 1024:2048]),
              deps=p.rdep("rowL1") + p.wdep("rowL3"))
    p.note_w(i, "rowL3")
    i = p.add("vector", lambda v: v.tensor_sub(rowb[64:65, 0:1024], rowb[64:65, 0:1024],
                                               rowb[64:65, 1024:2048]),
              deps=p.rdep("rowL2", "rowL3") + p.wdep("rowL4"))
    p.note_w(i, "rowL4")
    i = p.add("scalar", lambda s: s.activation(rowb[32:33, 1024:2048], rowb[64:65, 0:1024],
                                               AF.Sqrt, bias=epsc[64:65, 0:1]),
              deps=p.rdep("rowL4", "epsc") + p.wdep("rowL3s"))
    p.note_w(i, "rowL3s")
    i = p.add("vector", lambda v: v.reciprocal(rowb[0:1, 0:1024], rowb[32:33, 1024:2048]),
              deps=p.rdep("rowL3s") + p.wdep("rowL0"))
    p.note_w(i, "rowL0")
    i = p.add("vector", lambda v: v.scalar_tensor_tensor(
        rowb[32:33, 0:1024], rowb[0:1, 1024:2048], -1.0, rowb[0:1, 0:1024],
        ALU.mult, ALU.mult),
        deps=p.rdep("rowL0", "rowL1") + p.wdep("rowL32"))
    p.note_w(i, "rowL32")
    for src_row, dst, dkey in [(0, bc1, "bc1"), (32, bcB, "bcB")]:
        for hh in range(2):
            i = p.add("tensor", lambda t, hh=hh, src_row=src_row: t.matmul(
                PA[:, hh * 512:(hh + 1) * 512], onesrf[src_row:src_row + 1, :],
                rowb[src_row:src_row + 1, hh * 512:hh * 512 + 512], start=True, stop=True),
                deps=p.rdep(f"rowL{src_row}", "onesrf") + (p.wdep("PA") if hh == 0 else ()))
            p.note_r(i, f"rowL{src_row}")
        p.note_w(i, "PA")
        i2 = p.add("vector", lambda v, dst=dst: v.tensor_copy(dst[:, :], PA[:, :]),
                   deps=(i,) + p.wdep(dkey))
        p.note_r(i2, "PA")
        p.note_w(i2, dkey)
    for mc in range(16):
        m1 = p.add("vector", lambda v, mc=mc: v.tensor_tensor(
            R1[:, mc * 1024:(mc + 1) * 1024], acc[:, mc * 1024:(mc + 1) * 1024],
            bc1[:, :], ALU.mult),
            deps=p.rdep(("acc", mc), "bc1") + p.wdep("R1"))
        p.note_r(m1, ("acc", mc), "bc1")
        p.note_w(m1, "R1")
        m2 = p.add("vector", lambda v, mc=mc: v.tensor_tensor(
            R1[:, mc * 1024:(mc + 1) * 1024], R1[:, mc * 1024:(mc + 1) * 1024],
            bcB[:, :], ALU.add),
            deps=(m1,) + p.rdep("bcB"))
        p.note_r(m2, "bcB")
        p.note_w(m2, ("ln2", mc))
        p.note_w(m2, "R1")
    ln2_all = p.rdep(*[("ln2", mc) for mc in range(16)])

    for kc in range(16):
        for hh in range(2):
            i = p.add("tensor", lambda t, kc=kc, hh=hh: t.matmul(
                PD[0:7, hh * 512:(hh + 1) * 512], wrS[:, kc * 7:(kc + 1) * 7],
                R1[:, kc * 1024 + hh * 512:kc * 1024 + (hh + 1) * 512],
                start=(kc == 0), stop=(kc == 15)),
                deps=(p.rdep(("ln2", kc)) + ((p.rdep("wrS") + p.wdep("PD"))
                      if kc == 0 and hh == 0 else ())) if hh == 0 else ())
    p.note_r(i, "R1")
    p.note_w(i, "PD")
    i = p.add("scalar", lambda s: s.activation(rowb[0:7, 0:1024], PD[0:7, :], AF.Identity,
                                               bias=brS[0:7, 0:1]),
              deps=p.rdep("PD", "brS") + p.wdep("rowG", "rowL0", "rowL1", "rowL2",
                                               "rowL3", "rowL3s", "rowL4", "row0", "row1",
                                               "rowL32"))
    p.note_r(i, "PD")
    p.note_w(i, "rowG")
    for tc in range(8):
        tg = p.add("tensor", lambda t, tc=tc: t.transpose(
            PC[:, 0:7], rowb[0:7, tc * 128:(tc + 1) * 128], ident_f[0:7, 0:7]),
            deps=p.rdep("rowG", "identf") + p.wdep("PC"))
        p.note_w(tg, "PC")
        ge = p.add("vector", lambda v: v.tensor_copy(routb[:, 0:7], PC[:, 0:7]),
                   deps=(tg,) + p.wdep("rt"))
        p.note_r(ge, "PC")
        p.note_w(ge, "rt")
        bg = p.add("vector", lambda v: v.tensor_tensor(routb[:, 8:15], routb[:, 0:7],
                                                       gb_bc[:, :], ALU.add),
                   deps=(ge,) + p.rdep("gb_bc"))
        m1 = p.add("vector", lambda v: v.reduce_max(routb[:, 16:17], routb[:, 8:15],
                                                    axis=mybir.AxisListType.X), deps=(bg,))
        e1 = p.add("vector", lambda v: v.tensor_single_scalar(
            routb[:, 17:24], routb[:, 8:15], routb[:, 16:17], ALU.is_equal),
            self_deps=(m1,))
        mk = p.add("vector", lambda v: v.scalar_tensor_tensor(
            routb[:, 24:31], routb[:, 17:24], -1e30, routb[:, 8:15], ALU.mult, ALU.add),
            deps=(e1,))
        m2 = p.add("vector", lambda v: v.reduce_max(routb[:, 31:32], routb[:, 24:31],
                                                    axis=mybir.AxisListType.X), deps=(mk,))
        e2 = p.add("vector", lambda v: v.tensor_single_scalar(
            routb[:, 32:39], routb[:, 24:31], routb[:, 31:32], ALU.is_equal),
            self_deps=(m2,))
        t1 = p.add("vector", lambda v: v.tensor_tensor(routb[:, 39:46], routb[:, 0:7],
                                                       routb[:, 17:24], ALU.mult), deps=(e1, ge))
        g1 = p.add("vector", lambda v: v.reduce_sum(routb[:, 46:47], routb[:, 39:46],
                                                    axis=mybir.AxisListType.X), deps=(t1,))
        t2 = p.add("vector", lambda v: v.tensor_tensor(routb[:, 39:46], routb[:, 0:7],
                                                       routb[:, 32:39], ALU.mult), deps=(e2, g1))
        g2 = p.add("vector", lambda v: v.reduce_sum(routb[:, 47:48], routb[:, 39:46],
                                                    axis=mybir.AxisListType.X), deps=(t2,))
        d12 = p.add("vector", lambda v: v.tensor_sub(routb[:, 16:17], routb[:, 46:47],
                                                     routb[:, 47:48]), deps=(g1, g2, e1))
        p1 = p.add("scalar", lambda s: s.activation(routb[:, 31:32], routb[:, 16:17],
                                                    AF.Sigmoid), deps=(d12, e2))
        p2 = p.add("scalar", lambda s: s.activation(routb[:, 24:25], routb[:, 16:17],
                                                    AF.Sigmoid, scale=-1.0), deps=(d12, mk))
        pr = p.add("vector", lambda v: v.tensor_scalar_mul(
            routb[:, 39:46], routb[:, 32:39], routb[:, 24:25]), deps=(p2, t2))
        pr2 = p.add("vector", lambda v: v.scalar_tensor_tensor(
            routb[:, 39:46], routb[:, 17:24], routb[:, 31:32], routb[:, 39:46],
            ALU.mult, ALU.add), deps=(pr, p1))
        p.note_w(pr2, "probs")
        tp_ = p.add("tensor", lambda t: t.transpose(
            PC[0:7, 0:128], routb[:, 39:46], ident_f[:, :]),
            deps=(pr2,) + p.wdep("PC"))
        p.note_w(tp_, "PC")
        pe = p.add("vector", lambda v, tc=tc: v.tensor_copy(
            probsTb[0:7, tc * 128:(tc + 1) * 128], PC[0:7, 0:128]),
            deps=(tp_,) + p.wdep("probsTb"))
        p.note_r(pe, "PC")
        p.note_w(pe, "probsTb")

    def expert_chain(e, up_h, gt_h, dn_h, bup_col, bgt_col, use_probs, bsd_col=None,
                     add_bdown=False):
        if use_probs:
            ro = 1024 * (e % 2)
            dk = ("prow", e % 2)
            dm = p.add("gpsimd", lambda g, e=e, ro=ro: g.dma_start(
                out=rowb[0:1, ro:ro + 1024], in_=probsTb.ap()[e:e + 1, 0:1024]),
                deps=p.rdep("probsTb") + p.wdep(dk, "rowL0", "rowG", "rowL1"))
            p.note_w(dm, dk)
            for hh in range(2):
                i = p.add("tensor", lambda t, hh=hh, ro=ro: t.matmul(
                    PA[:, hh * 512:(hh + 1) * 512], onesrf[0:1, :],
                    rowb[0:1, ro + hh * 512:ro + (hh + 1) * 512], start=True, stop=True),
                    deps=(p.rdep(dk, "onesrf") + p.wdep("PA")) if hh == 0 else ())
            p.note_w(i, "PA")
            be = p.add("vector", lambda v: v.tensor_copy(bc1[:, :], PA[:, :]),
                       deps=(i,) + p.wdep("bc1"))
            p.note_r(be, "PA")
            p.note_w(be, "bc1")
            p.note_r(be, dk)
        for mc in range(16):
            slab, skey = ring_load(up_h[mc, :, :])
            pkey, P_t = next_psum()
            mm_group(P_t, pkey, slab, skey, R1, "R1", kc_keys=lambda kc: ("ln2", kc))
            ev = p.add("scalar", lambda s, P=P_t, mc=mc, bup_col=bup_col: s.activation(
                R2[:, mc * 1024:(mc + 1) * 1024], P[:, :], AF.Identity,
                bias=bupP[:, bup_col + mc:bup_col + mc + 1] if bup_col is not None
                else bsuP[:, mc:mc + 1]),
                deps=p.rdep(pkey) + p.wdep("R2"))
            p.note_r(ev, pkey)
            p.note_w(ev, "R2")
            p.note_w(ev, ("u", mc))
        u_all = p.rdep(*[("u", mc) for mc in range(16)])
        for mc in range(16):
            slab, skey = ring_load(gt_h[mc, :, :])
            pkey, P_t = next_psum()
            mm_group(P_t, pkey, slab, skey, R2, "R2", kc_keys=lambda kc: ("u", kc))
            sv = p.add("scalar", lambda s, P=P_t, mc=mc, bgt_col=bgt_col: s.activation(
                R3[:, mc * 1024:(mc + 1) * 1024], P[:, :], AF.Silu,
                bias=bgtP[:, bgt_col + mc:bgt_col + mc + 1] if bgt_col is not None
                else bsgP[:, mc:mc + 1]),
                deps=p.rdep(pkey) + p.wdep("R3a"))
            p.note_r(sv, pkey)
            p.note_w(sv, "R3a")
            gv = p.add("vector", lambda v, mc=mc: v.tensor_tensor(
                R3[:, mc * 1024:(mc + 1) * 1024], R3[:, mc * 1024:(mc + 1) * 1024],
                R2[:, mc * 1024:(mc + 1) * 1024], ALU.mult),
                deps=(sv,) + p.rdep(("u", mc)))
            p.note_r(gv, "R2")
            last = gv
            if use_probs:
                last = p.add("vector", lambda v, mc=mc: v.tensor_tensor(
                    R3[:, mc * 1024:(mc + 1) * 1024], R3[:, mc * 1024:(mc + 1) * 1024],
                    bc1[:, :], ALU.mult), deps=(gv,) + p.rdep("bc1"))
                p.note_r(last, "bc1")
            p.note_w(last, ("g", mc))
            p.note_w(last, "R3a")
        g_all = p.rdep(*[("g", mc) for mc in range(16)])
        for mc in range(16):
            slab, skey = ring_load(dn_h[mc, :, :])
            pkey, P_t = next_psum()
            first_deps = p.wdep(pkey) + p.rdep(skey, "R3a")
            last = None
            for kc in range(16):
                dd0 = (first_deps if kc == 0 else ()) + p.rdep(("g", kc))
                for hh in range(2):
                    last = p.add("tensor", lambda t, P=P_t, sl=slab, kc=kc, hh=hh:
                                 t.matmul(P[:, hh * 512:(hh + 1) * 512],
                                          sl[:, kc * 128:(kc + 1) * 128],
                                          R3[:, kc * 1024 + hh * 512:kc * 1024 + (hh + 1) * 512],
                                          start=(kc == 0), stop=(kc == 15 and not add_bdown)),
                                 deps=dd0 if hh == 0 else ())
            if add_bdown:
                for hh in range(2):
                    last = p.add("tensor", lambda t, P=P_t, mc=mc, hh=hh: t.matmul(
                        P[:, hh * 512:(hh + 1) * 512], bdnT[0:7, mc * 128:(mc + 1) * 128],
                        probsTb[0:7, hh * 512:(hh + 1) * 512], start=False, stop=True),
                        deps=p.rdep("bdnT", "probsTb"))
            p.note_r(last, skey, "R3a")
            p.note_w(last, pkey)
            if bsd_col is None:
                av = p.add("vector", lambda v, P=P_t, mc=mc: v.tensor_tensor(
                    acc[:, mc * 1024:(mc + 1) * 1024], acc[:, mc * 1024:(mc + 1) * 1024],
                    P[:, :], ALU.add), deps=p.rdep(pkey, ("acc", mc)) + p.wdep(("acc", mc)))
            else:
                av = p.add("vector", lambda v, P=P_t, mc=mc: v.scalar_tensor_tensor(
                    acc[:, mc * 1024:(mc + 1) * 1024], P[:, :], bsdP[:, mc:mc + 1],
                    acc[:, mc * 1024:(mc + 1) * 1024], ALU.add, ALU.add),
                    deps=p.rdep(pkey, ("acc", mc), "bsdP") + p.wdep(("acc", mc)))
            p.note_r(av, pkey)
            p.note_w(av, ("acc", mc))

    for e in range(NE):
        expert_chain(e, wup_h[e], wgt_h[e], wdn_h[e], e * 16, e * 16, True)
    expert_chain(None, wsu_h, wsg_h, wsd_h, None, None, False, bsd_col=0, add_bdown=True)

    for tc in range(8):
        k = tc % 2
        evs = []
        for q4 in range(4):
            tids = []
            for j in range(4):
                fc = q4 * 4 + j
                ti = p.add("tensor", lambda t, fc=fc, j=j, tc=tc: t.transpose(
                    PC[:, j * 128:(j + 1) * 128],
                    acc[:, fc * 1024 + tc * 128:fc * 1024 + (tc + 1) * 128],
                    ident_f[:, :]),
                    deps=(p.rdep(*[("acc", q4 * 4 + jj) for jj in range(4)]) + p.wdep("PC"))
                         if j == 0 else ())
                tids.append(ti)
            for jj in range(4):
                p.note_r(tids[-1], ("acc", q4 * 4 + jj))
            p.note_w(tids[-1], "PC")
            ev = p.add("vector", lambda v, q4=q4, k=k: v.tensor_copy(
                xstage[k][:, q4 * 512:(q4 + 1) * 512], PC[:, 0:512]),
                deps=(tids[-1],) + p.wdep(("xs", k)))
            p.note_r(ev, "PC")
            p.note_w(ev, ("xs", k))
            evs.append(ev)
        st_ = p.add("sync", lambda s, k=k, tc=tc: s.dma_start(
            out=y_h[tc * 128:(tc + 1) * 128, :], in_=xstage[k][:, :]),
            deps=tuple(evs))
        p.note_r(st_, ("xs", k))

    sems = {}
    sem_ctx = []
    for e in Prog.ENGINES:
        if e in Prog.DMA_ENG:
            hs = []
            for s_ in range(4):
                c = nc.semaphore(f"sem_{e}{s_}")
                sem_ctx.append(c)
                hs.append(c.__enter__())
            sems[e] = hs
        else:
            c = nc.semaphore(f"sem_{e}")
            sem_ctx.append(c)
            sems[e] = c.__enter__()
    p.finalize(nc, sems)
    for c in reversed(sem_ctx):
        c.__exit__(None, None, None)
    for _, g_ in reversed(open_ctx):
        try:
            g_.__exit__(None, None, None)
        except Exception:
            pass
    for g_ in reversed(cph):
        g_.__exit__(None, None, None)
    for g_ in reversed(psum_ctx):
        g_.__exit__(None, None, None)
    _nc_cache[0] = nc
    return nc


def _prep_host(inputs):
    def tile_w(Wt):
        K, M = Wt.shape
        return np.ascontiguousarray(
            Wt.reshape(16, 128, M // 128, 128).transpose(2, 1, 0, 3).reshape(M // 128, 128, 2048),
            dtype=np.float32)

    def col_pack(v):
        return np.ascontiguousarray(np.asarray(v, np.float32).reshape(16, 128).T)

    qkv = np.asarray(inputs["L0_attn_qkv_w"], np.float32).reshape(H, 3, D, HID)
    com = {}
    com["wq"] = tile_w(np.ascontiguousarray(qkv[:, 0].reshape(HID, HID).T))
    com["wk"] = tile_w(np.ascontiguousarray(qkv[:, 1].reshape(HID, HID).T))
    com["wv"] = tile_w(np.ascontiguousarray(qkv[:, 2].reshape(HID, HID).T))
    com["wg"] = tile_w(np.ascontiguousarray(np.asarray(inputs["L0_attn_gate_w"], np.float32).T))
    com["wo"] = tile_w(np.ascontiguousarray(np.asarray(inputs["L0_attn_out_w"], np.float32).T))
    Wr = np.asarray(inputs["L0_ffn_Wr"], np.float32)
    wr_t = np.zeros((128, 112), np.float32)
    for kc in range(16):
        wr_t[:, kc * 7:(kc + 1) * 7] = Wr[kc * 128:(kc + 1) * 128, :]
    com["wr"] = wr_t
    com["wup"] = np.stack([tile_w(np.asarray(inputs["L0_ffn_Wup"][e], np.float32)) for e in range(NE)])
    com["wgt"] = np.stack([tile_w(np.asarray(inputs["L0_ffn_Wgate"][e], np.float32)) for e in range(NE)])
    com["wdn"] = np.stack([tile_w(np.asarray(inputs["L0_ffn_Wdown"][e], np.float32)) for e in range(NE)])
    com["wsu"] = tile_w(np.asarray(inputs["L0_ffn_Wsu"], np.float32))
    com["wsg"] = tile_w(np.asarray(inputs["L0_ffn_Wsg"], np.float32))
    com["wsd"] = tile_w(np.asarray(inputs["L0_ffn_Wsd"], np.float32))
    com["bupP"] = np.concatenate([col_pack(np.asarray(inputs["L0_ffn_bup"][e])) for e in range(NE)], 1)
    com["bgtP"] = np.concatenate([col_pack(np.asarray(inputs["L0_ffn_bgate"][e])) for e in range(NE)], 1)
    com["bdnT"] = np.ascontiguousarray(np.asarray(inputs["L0_ffn_bdown"], np.float32))
    com["bsuP"] = col_pack(inputs["L0_ffn_bsu"])
    com["bsgP"] = col_pack(inputs["L0_ffn_bsg"])
    com["bsdP"] = col_pack(inputs["L0_ffn_bsd"])
    com["nwP"] = col_pack(inputs["L0_attn_norm_w"])
    com["br"] = np.ascontiguousarray(np.asarray(inputs["L0_ffn_br"], np.float32).reshape(7, 1))
    com["gb"] = np.ascontiguousarray(np.asarray(inputs["L0_ffn_gate_bias"], np.float32).reshape(1, 7))
    return com


def kernel(**inputs):
    nc = build_nc()
    com = _prep_host(inputs)
    x = np.asarray(inputs["L0_attn_hidden_states"], np.float32)
    zeros = np.zeros((T, HID), np.float32)
    in_maps = []
    for core in range(8):
        b, half = core // 2, core % 2
        m = {"i_" + k: v for k, v in com.items()}
        m["i_xm"] = np.ascontiguousarray(x[b, half * T:(half + 1) * T])
        m["i_xp"] = zeros if half == 0 else np.ascontiguousarray(x[b, 0:T])
        in_maps.append(m)
    res = run_bass_kernel_spmd(nc, in_maps, core_ids=list(range(8)))
    out = np.empty((4, 2048, HID), np.float32)
    for core in range(8):
        b, half = core // 2, core % 2
        out[b, half * T:(half + 1) * T] = res.results[core]["y"]
    return out
